# version 02
# baseline (speedup 1.0000x reference)
import numpy as np
import ml_dtypes
from contextlib import ExitStack

import concourse.bass as bass
import concourse.mybir as mybir
from concourse.bass_utils import run_bass_kernel_spmd

F32 = mybir.dt.float32
BF16 = mybir.dt.bfloat16
AF = mybir.ActivationFunctionType
ALU = mybir.AluOpType
AX = mybir.AxisListType
BF16NP = ml_dtypes.bfloat16

D = 2048
OB = 4
TSEQ = 2048
NCH = 16
TOK = 1024
NH, DV = 16, 128
HID, HB = 8192, 64
NCORE = 8
SCALE_A = float(1.0 / np.sqrt(128.0))
S8192 = float(1.0 / np.sqrt(8192.0))
EPS_LN = 1e-5
NEG = -1e9

SEMS = ["ld", "wd", "pe", "pemm", "T", "pv", "ev", "exp", "ptv", "oev",
        "mx", "nrm", "cp", "sq", "dv", "ac", "ac2", "ap", "st", "rcp"]


def _rope_tables():
    theta = 10000.0 ** (-np.arange(32) / 32.0)
    srcA = np.zeros(D, dtype=np.int64)
    srcB = np.zeros(D, dtype=np.int64)
    cA = np.zeros(D, dtype=np.float32)
    cB = np.zeros(D, dtype=np.float32)
    for h in range(NH):
        cos, sin = np.cos(h * theta), np.sin(h * theta)
        for j in range(128):
            o = h * 128 + j
            if j < 32:
                m = j; a = 2 * m; ca, cb = cos[m], -sin[m]
            elif j < 64:
                m = j - 32; a = 2 * m; ca, cb = sin[m], cos[m]
            elif j < 96:
                m = j - 64; a = 64 + 2 * m; ca, cb = cos[m], -sin[m]
            else:
                m = j - 96; a = 64 + 2 * m; ca, cb = sin[m], cos[m]
            srcA[o], srcB[o] = h * 128 + a, h * 128 + a + 1
            cA[o], cB[o] = ca, cb
    pK = np.zeros(128, dtype=np.int64)
    for j in range(128):
        if j < 32:
            pK[j] = 2 * j
        elif j < 64:
            pK[j] = 2 * (j - 32) + 1
        elif j < 96:
            pK[j] = 64 + 2 * (j - 64)
        else:
            pK[j] = 64 + 2 * (j - 96) + 1
    return srcA, srcB, cA, cB, pK


def _hadamard128():
    i = np.arange(128)
    pc = np.zeros((128, 128), dtype=np.int64)
    for b in range(7):
        pc += ((i[:, None] >> b) & 1) * ((i[None, :] >> b) & 1)
    return np.where(pc % 2 == 0, 1.0, -1.0).astype(np.float32)


def build_program(debug=False):
    nc = bass.Bass("TRN2", target_bir_lowering=False)
    dt = nc.dram_tensor
    dbg = dict(kind="ExternalOutput") if debug else {}

    h_xown = dt("xown", [D, TOK], F32, kind="ExternalInput")
    h_xpre = dt("xpre", [OB * TSEQ, D], F32, kind="ExternalInput")
    h_wqA = dt("wqA", [D, D], F32, kind="ExternalInput")
    h_wqB = dt("wqB", [D, D], F32, kind="ExternalInput")
    h_cA = dt("cAb", [128, D], F32, kind="ExternalInput")
    h_cB = dt("cBb", [128, D], F32, kind="ExternalInput")
    h_wk = dt("wkp", [D, DV], F32, kind="ExternalInput")
    h_wv = dt("wvt", [D, DV], F32, kind="ExternalInput")
    h_woa = dt("woa", [D, D], F32, kind="ExternalInput")
    h_wqb = dt("wqb", [D, D], F32, kind="ExternalInput")
    h_wkb = dt("wkb", [D, D], F32, kind="ExternalInput")
    h_wvb = dt("wvb", [D, D], F32, kind="ExternalInput")
    h_wbb = dt("wbb", [D, NH], F32, kind="ExternalInput")
    h_wob = dt("wob", [D, D], F32, kind="ExternalInput")
    h_conv = dt("convr", [1, 3 * 4 * D], F32, kind="ExternalInput")
    h_g = dt("gcol", [128, 1], F32, kind="ExternalInput")
    h_w1 = dt("w1t", [D, HID], F32, kind="ExternalInput")
    h_w3 = dt("w3t", [D, HID], F32, kind="ExternalInput")
    h_w2 = dt("w2t", [HID, D], F32, kind="ExternalInput")
    h_id = dt("ident", [128, 128], BF16, kind="ExternalInput")
    h_h128 = dt("h128", [128, 128], BF16, kind="ExternalInput")
    h_onesc = dt("onesc", [128, 1], BF16, kind="ExternalInput")
    h_onesrf = dt("onesrf", [1, 128], F32, kind="ExternalInput")
    h_mskA = dt("mskA", [128, 1024], F32, kind="ExternalInput")
    h_mskB = dt("mskB", [128, 2048], F32, kind="ExternalInput")

    h_y = dt("y", [D, TOK], F32, kind="ExternalOutput")
    if debug:
        h_dxn = dt("dxn", [128, NCH * TOK], BF16, kind="ExternalOutput")
        h_dq = dt("dq", [128, NH * TOK], BF16, kind="ExternalOutput")
        h_dk = dt("dk", [128, OB * TSEQ], BF16, kind="ExternalOutput")
        h_dv = dt("dv", [128, OB * TSEQ], BF16, kind="ExternalOutput")
        h_dqo = dt("dqo", [128, NH * TOK], BF16, kind="ExternalOutput")
        h_dss = dt("dss", [128, 2 * 2048], F32, kind="ExternalOutput")
        h_dpb = dt("dpb", [128, 2 * 2048], BF16, kind="ExternalOutput")
        h_dpt = dt("dpt", [128, 2 * 2048], BF16, kind="ExternalOutput")
        h_dnm = dt("dnm", [128, 6], F32, kind="ExternalOutput")
    h_xs1 = dt("xs1", [D, TOK], F32, **dbg)
    h_xs2 = dt("xs2", [D, TOK], F32, **dbg)
    h_xs3 = dt("xs3", [D, TOK], F32, **dbg)

    cnt = {k: 0 for k in SEMS}

    with ExitStack() as ctx:
        def sba(stack, name, shape, d):
            return stack.enter_context(nc.sbuf_tensor(name, shape, d))

        sem = {k: ctx.enter_context(nc.semaphore("s_" + k)) for k in SEMS}

        xn = sba(ctx, "xn", [128, NCH * TOK], BF16)
        ident = sba(ctx, "ident_s", [128, 128], BF16)
        h128 = sba(ctx, "h128_s", [128, 128], BF16)
        onesc = sba(ctx, "onesc_s", [128, 1], BF16)
        onesrf = sba(ctx, "onesrf_s", [1, 128], F32)
        gcol = sba(ctx, "gcol_s", [128, 1], F32)
        wbb_sb = sba(ctx, "wbb_s", [128, NCH * NH], BF16)
        epsln = sba(ctx, "epsln", [128, 1], F32)
        eps12 = sba(ctx, "eps12", [128, 1], F32)
        eps12s = sba(ctx, "eps12s", [128, 1], F32)

        with nc.Block() as blk:
            @blk.sync
            def _(sy):
                loads = [(ident, h_id), (h128, h_h128), (onesc, h_onesc),
                         (onesrf, h_onesrf), (gcol, h_g)]
                for dsb, hh in loads:
                    sy.dma_start(out=dsb[:, :], in_=hh.ap()).then_inc(sem["ld"], 16)
                sy.wait_ge(sem["ld"], cnt["ld"] + 16 * len(loads))

            @blk.vector
            def _(v):
                v.memset(epsln[:, :], EPS_LN)
                v.memset(eps12[:, :], 1e-12)
                v.memset(eps12s[:, :], 128.0 * 1e-12)

            @blk.gpsimd
            def _(g):
                g.dma_start(out=wbb_sb[:, :].rearrange("p (c w) -> p c w", c=NCH),
                            in_=h_wbb.ap().rearrange("(c p) w -> p c w", p=128)
                            ).then_inc(sem["wd"], 16)
                g.wait_ge(sem["wd"], cnt["wd"] + 16)
        cnt["ld"] += 16 * 5
        cnt["wd"] += 16

        def LN(tag, src):
            b = dict(cnt)
            with ExitStack() as lctx:
                xstg = [sba(lctx, f"xstg{tag}{i}", [128, TOK], F32) for i in range(2)]
                sqs = [sba(lctx, f"sqs{tag}{i}", [128, TOK], BF16) for i in range(2)]
                rows = sba(lctx, f"rows{tag}", [1, 6 * TOK], F32)
                a_bc = sba(lctx, f"abc{tag}", [128, TOK], F32)
                b_bc = sba(lctx, f"bbc{tag}", [128, TOK], F32)
                tmpf = sba(lctx, f"tmpf{tag}", [128, TOK], F32)
                ps_s0 = lctx.enter_context(nc.psum_tensor(f"pss0{tag}", [1, TOK], F32))
                ps_s1 = lctx.enter_context(nc.psum_tensor(f"pss1{tag}", [1, TOK], F32))
                ps_b0 = lctx.enter_context(nc.psum_tensor(f"psb0{tag}", [128, TOK], F32))
                ps_b1 = lctx.enter_context(nc.psum_tensor(f"psb1{tag}", [128, TOK], F32))

                def row(i):
                    return rows[0:1, i * TOK:(i + 1) * TOK]

                with nc.Block() as blk:
                    @blk.sync
                    def _(sy):
                        for ch in range(NCH):
                            if ch >= 2:
                                sy.wait_ge(sem["cp"], b["cp"] + ch - 1)
                                sy.wait_ge(sem["sq"], b["sq"] + ch - 1)
                            sy.dma_start(out=xstg[ch % 2][:, :],
                                         in_=src.ap()[ch * 128:(ch + 1) * 128, :]
                                         ).then_inc(sem["ld"], 16)
                        sy.wait_ge(sem["cp"], b["cp"] + 16)
                        sy.wait_ge(sem["sq"], b["sq"] + 16)
                        for ch in range(NCH):
                            if ch >= 2:
                                sy.wait_ge(sem["ap"], b["ap"] + ch - 1)
                            sy.dma_start(out=xstg[ch % 2][:, :],
                                         in_=src.ap()[ch * 128:(ch + 1) * 128, :]
                                         ).then_inc(sem["ld"], 16)

                    @blk.vector
                    def _(v):
                        for ch in range(NCH):
                            v.wait_ge(sem["ld"], b["ld"] + 16 * (ch + 1))
                            v.tensor_copy(out=xn[:, ch * TOK:(ch + 1) * TOK],
                                          in_=xstg[ch % 2][:, :]).then_inc(sem["cp"], 1)
                        v.wait_ge(sem["pe"], b["pe"] + 1)
                        v.wait_ge(sem["pemm"], b["pemm"] + 16)
                        v.tensor_scalar_mul(row(0), ps_s0[:, :], 1.0 / D)
                        v.tensor_scalar_mul(row(1), ps_s1[:, :], 1.0 / D)
                        v.tensor_mul(tmpf[0:1, :], row(0), row(0))
                        v.tensor_sub(row(2), row(1), tmpf[0:1, :]).then_inc(sem["dv"], 1)
                        v.wait_ge(sem["ac"], b["ac"] + 1)
                        v.reciprocal(row(4), row(3))
                        v.tensor_mul(tmpf[0:1, :], row(0), row(4))
                        v.tensor_scalar_mul(row(5), tmpf[0:1, :], -1.0).then_inc(sem["dv"], 1)
                        for ch in range(NCH):
                            v.wait_ge(sem["ld"], b["ld"] + 16 * (16 + ch + 1))
                            if ch == 0:
                                v.wait_ge(sem["ac"], b["ac"] + 2)
                            v.tensor_mul(tmpf[:, :], xstg[ch % 2][:, :], a_bc[:, :])
                            v.tensor_add(xn[:, ch * TOK:(ch + 1) * TOK], tmpf[:, :],
                                         b_bc[:, :]).then_inc(sem["ap"], 1)

                    @blk.scalar
                    def _(s):
                        for ch in range(NCH):
                            s.wait_ge(sem["ld"], b["ld"] + 16 * (ch + 1))
                            if ch >= 2:
                                s.wait_ge(sem["pemm"], b["pemm"] + ch - 1)
                            s.activation(out=sqs[ch % 2][:, :], in_=xstg[ch % 2][:, :],
                                         func=AF.Square).then_inc(sem["sq"], 1)
                        s.wait_ge(sem["dv"], b["dv"] + 1)
                        s.activation(out=row(3), in_=row(2), func=AF.Sqrt,
                                     bias=epsln[0:1, 0:1]).then_inc(sem["ac"], 1)
                        s.wait_ge(sem["pe"], b["pe"] + 3)
                        s.activation(out=a_bc[:, :], in_=ps_b0[:, :], func=AF.Copy)
                        s.activation(out=b_bc[:, :], in_=ps_b1[:, :],
                                     func=AF.Copy).then_inc(sem["ac"], 1)

                    @blk.tensor
                    def _(t):
                        for ch in range(NCH):
                            t.wait_ge(sem["cp"], b["cp"] + ch + 1)
                            t.wait_ge(sem["sq"], b["sq"] + ch + 1)
                            ins = None
                            for hh in range(2):
                                ins = t.matmul(ps_s0[0:1, hh * 512:(hh + 1) * 512], onesc[:, :],
                                               xn[:, ch * TOK + hh * 512:ch * TOK + (hh + 1) * 512],
                                               start=(ch == 0), stop=(ch == NCH - 1))
                            if ch == NCH - 1:
                                ins.then_inc(sem["pe"], 1)
                            for hh in range(2):
                                ins = t.matmul(ps_s1[0:1, hh * 512:(hh + 1) * 512], onesc[:, :],
                                               sqs[ch % 2][:, hh * 512:(hh + 1) * 512],
                                               start=(ch == 0), stop=(ch == NCH - 1))
                            ins.then_inc(sem["pemm"], 1)
                        t.wait_ge(sem["dv"], b["dv"] + 2)
                        for hh in range(2):
                            ins = t.matmul(ps_b0[:, hh * 512:(hh + 1) * 512], onesrf[:, :],
                                           row(4)[:, hh * 512:(hh + 1) * 512],
                                           start=True, stop=True)
                        ins.then_inc(sem["pe"], 1)
                        for hh in range(2):
                            ins = t.matmul(ps_b1[:, hh * 512:(hh + 1) * 512], onesrf[:, :],
                                           row(5)[:, hh * 512:(hh + 1) * 512],
                                           start=True, stop=True)
                        ins.then_inc(sem["pe"], 1)

            cnt["ld"] += 16 * 32
            cnt["cp"] += 16
            cnt["sq"] += 16
            cnt["pemm"] += 16
            cnt["pe"] += 3
            cnt["ac"] += 2
            cnt["dv"] += 2
            cnt["ap"] += 16

        def FFN(tag, src, dst):
            with ExitStack() as fctx:
                hid = sba(fctx, f"hid{tag}", [128, HB * TOK], BF16)
                b = dict(cnt)
                with ExitStack() as actx:
                    w1b = [sba(actx, f"w1b{tag}{i}", [128, NCH * 128], BF16) for i in range(2)]
                    w3b = [sba(actx, f"w3b{tag}{i}", [128, NCH * 128], BF16) for i in range(2)]
                    gs = [sba(actx, f"gs{tag}{i}", [128, TOK], BF16) for i in range(2)]
                    ub = [sba(actx, f"ub{tag}{i}", [128, TOK], BF16) for i in range(2)]
                    psb = [actx.enter_context(
                        nc.psum_tensor(f"psA{tag}{i}", [128, 512], F32)) for i in range(8)]
                    with nc.Block() as blk:
                        @blk.gpsimd
                        def _(g):
                            for hb in range(HB):
                                if hb >= 2:
                                    g.wait_ge(sem["pemm"], b["pemm"] + 4 * (hb - 1))
                                for hsrc, wb in ((h_w1, w1b), (h_w3, w3b)):
                                    g.dma_start(
                                        out=wb[hb % 2][:, :].rearrange("p (c o) -> p c o", c=NCH),
                                        in_=hsrc.ap()[:, hb * 128:(hb + 1) * 128]
                                        .rearrange("(c p) o -> p c o", p=128)
                                    ).then_inc(sem["wd"], 16)

                        @blk.tensor
                        def _(t):
                            for hb in range(HB):
                                t.wait_ge(sem["wd"], b["wd"] + 32 * (hb + 1))
                                if hb >= 2:
                                    t.wait_ge(sem["ac"], b["ac"] + hb - 1)
                                    t.wait_ge(sem["dv"], b["dv"] + hb - 1)
                                bb = (hb % 2) * 4
                                for wsel, boff in ((w1b, 0), (w3b, 2)):
                                    for tt in range(2):
                                        ins = None
                                        for ch in range(NCH):
                                            ins = t.matmul(
                                                psb[bb + boff + tt][:, :],
                                                wsel[hb % 2][:, ch * 128:(ch + 1) * 128],
                                                xn[:, ch * TOK + tt * 512:ch * TOK + (tt + 1) * 512],
                                                start=(ch == 0), stop=(ch == NCH - 1))
                                        ins.then_inc(sem["pemm"], 1)

                        @blk.scalar
                        def _(s):
                            for hb in range(HB):
                                bb = (hb % 2) * 4
                                if hb >= 2:
                                    s.wait_ge(sem["dv"], b["dv"] + hb - 1)
                                ins = None
                                for tt in range(2):
                                    s.wait_ge(sem["pemm"], b["pemm"] + 4 * hb + tt + 1)
                                    ins = s.activation(out=gs[hb % 2][:, tt * 512:(tt + 1) * 512],
                                                       in_=psb[bb + tt][:, :], func=AF.Silu)
                                ins.then_inc(sem["ac"], 1)

                        @blk.vector
                        def _(v):
                            for hb in range(HB):
                                bb = (hb % 2) * 4
                                for tt in range(2):
                                    v.wait_ge(sem["pemm"], b["pemm"] + 4 * hb + 2 + tt + 1)
                                    v.tensor_copy(out=ub[hb % 2][:, tt * 512:(tt + 1) * 512],
                                                  in_=psb[bb + 2 + tt][:, :])
                                v.wait_ge(sem["ac"], b["ac"] + hb + 1)
                                v.tensor_mul(hid[:, hb * TOK:(hb + 1) * TOK], gs[hb % 2][:, :],
                                             ub[hb % 2][:, :]).then_inc(sem["dv"], 1)
                    cnt["wd"] += 16 * 2 * HB
                    cnt["pemm"] += 4 * HB
                    cnt["ac"] += HB
                    cnt["dv"] += HB

                b = dict(cnt)
                with ExitStack() as hctx:
                    tA = sba(hctx, f"tA{tag}", [128, TOK], BF16)
                    tB = sba(hctx, f"tB{tag}", [128, TOK], BF16)
                    ph = [hctx.enter_context(
                        nc.psum_tensor(f"psH{tag}{i}", [128, 512], F32)) for i in range(2)]
                    with nc.Block() as blk:
                        @blk.tensor
                        def _(t):
                            for idx in range(2 * HB):
                                hb, tt = divmod(idx, 2)
                                if idx >= 2:
                                    t.wait_ge(sem["ac"], b["ac"] + idx - 1)
                                t.matmul(ph[idx % 2][:, :], h128[:, :],
                                         hid[:, hb * TOK + tt * 512:hb * TOK + (tt + 1) * 512],
                                         start=True, stop=True).then_inc(sem["pemm"], 1)

                        @blk.scalar
                        def _(s):
                            for idx in range(2 * HB):
                                hb, tt = divmod(idx, 2)
                                s.wait_ge(sem["pemm"], b["pemm"] + idx + 1)
                                s.activation(out=hid[:, hb * TOK + tt * 512:hb * TOK + (tt + 1) * 512],
                                             in_=ph[idx % 2][:, :], func=AF.Copy,
                                             scale=S8192).then_inc(sem["ac"], 1)

                        @blk.vector
                        def _(v):
                            v.wait_ge(sem["ac"], b["ac"] + 2 * HB)
                            for hstep in (1, 4, 16):
                                for base in range(0, HB, 4 * hstep):
                                    for j in range(hstep):
                                        c0 = (base + j) * TOK
                                        c1 = c0 + hstep * TOK
                                        c2 = c0 + 2 * hstep * TOK
                                        c3 = c0 + 3 * hstep * TOK
                                        s0 = hid[:, c0:c0 + TOK]
                                        s1 = hid[:, c1:c1 + TOK]
                                        s2 = hid[:, c2:c2 + TOK]
                                        s3 = hid[:, c3:c3 + TOK]
                                        v.tensor_add(tA[:, :], s0, s1)
                                        v.tensor_sub(tB[:, :], s0, s1)
                                        v.tensor_add(s0, s2, s3)
                                        v.tensor_sub(s1, s2, s3)
                                        v.tensor_sub(s2, tA[:, :], s0)
                                        v.tensor_sub(s3, tB[:, :], s1)
                                        v.tensor_add(s0, tA[:, :], s0)
                                        v.tensor_add(s1, tB[:, :], s1)
                    cnt["pemm"] += 2 * HB
                    cnt["ac"] += 2 * HB

                b = dict(cnt)
                with ExitStack() as dctx:
                    w2b = [sba(dctx, f"w2b{tag}{i}", [128, 32 * 128], BF16) for i in range(3)]
                    xstg = [sba(dctx, f"xsd{tag}{i}", [128, TOK], F32) for i in range(2)]
                    ostg = [sba(dctx, f"osd{tag}{i}", [128, 512], F32) for i in range(4)]
                    pso = [dctx.enter_context(
                        nc.psum_tensor(f"psD{tag}{i}", [128, 512], F32)) for i in range(2)]
                    with nc.Block() as blk:
                        @blk.gpsimd
                        def _(g):
                            for half in range(32):
                                dc, hh = divmod(half, 2)
                                if half >= 3:
                                    g.wait_ge(sem["pemm"],
                                              b["pemm"] + 2 * ((half - 3) // 2) + 2)
                                g.dma_start(
                                    out=w2b[half % 3][:, :].rearrange("p (c o) -> p c o", c=32),
                                    in_=h_w2.ap()[hh * 4096:(hh + 1) * 4096,
                                                  dc * 128:(dc + 1) * 128]
                                    .rearrange("(c p) o -> p c o", p=128)).then_inc(sem["wd"], 16)

                        @blk.sync
                        def _(sy):
                            for dc in range(NCH):
                                if dc >= 2:
                                    sy.wait_ge(sem["dv"], b["dv"] + 2 * (dc - 1))
                                sy.dma_start(out=xstg[dc % 2][:, :],
                                             in_=src.ap()[dc * 128:(dc + 1) * 128, :]
                                             ).then_inc(sem["ld"], 16)

                        @blk.scalar
                        def _(sy):
                            for idx in range(32):
                                dc, tt = divmod(idx, 2)
                                sy.wait_ge(sem["dv"], b["dv"] + idx + 1)
                                sy.dma_start(out=dst.ap()[dc * 128:(dc + 1) * 128,
                                                          tt * 512:(tt + 1) * 512],
                                             in_=ostg[idx % 4][:, :]).then_inc(sem["st"], 16)
                            sy.wait_ge(sem["st"], b["st"] + 16 * 32)

                        @blk.tensor
                        def _(t):
                            for dc in range(NCH):
                                for tt in range(2):
                                    idx = dc * 2 + tt
                                    if idx >= 2:
                                        t.wait_ge(sem["dv"], b["dv"] + idx - 1)
                                    ins = None
                                    for hb in range(HB):
                                        half = dc * 2 + hb // 32
                                        if hb % 32 == 0 and tt == 0:
                                            t.wait_ge(sem["wd"], b["wd"] + 16 * (half + 1))
                                        ins = t.matmul(
                                            pso[idx % 2][:, :],
                                            w2b[half % 3][:, (hb % 32) * 128:(hb % 32 + 1) * 128],
                                            hid[:, hb * TOK + tt * 512:hb * TOK + (tt + 1) * 512],
                                            start=(hb == 0), stop=(hb == HB - 1))
                                    ins.then_inc(sem["pemm"], 1)

                        @blk.vector
                        def _(v):
                            for idx in range(32):
                                dc, tt = divmod(idx, 2)
                                v.wait_ge(sem["pemm"], b["pemm"] + idx + 1)
                                v.wait_ge(sem["ld"], b["ld"] + 16 * (dc + 1))
                                if idx >= 4:
                                    v.wait_ge(sem["st"], b["st"] + 16 * (idx - 3))
                                v.tensor_add(ostg[idx % 4][:, :], pso[idx % 2][:, :],
                                             xstg[dc % 2][:, tt * 512:(tt + 1) * 512]
                                             ).then_inc(sem["dv"], 1)
                    cnt["wd"] += 16 * 32
                    cnt["ld"] += 16 * NCH
                    cnt["pemm"] += 32
                    cnt["dv"] += 32
                    cnt["st"] += 16 * 32

        LN("L1", h_xown)

        with ExitStack() as s1ctx:
            qo = sba(s1ctx, "qo", [128, NH * TOK], BF16)

            b = dict(cnt)
            with ExitStack() as rctx:
                wqp = sba(rctx, "wqp", [128, NCH * D], BF16)
                cab = sba(rctx, "cab", [128, D], F32)
                cbb = sba(rctx, "cbb", [128, D], F32)
                wsA = [sba(rctx, f"wsA{i}", [128, D], F32) for i in range(2)]
                wsB = [sba(rctx, f"wsB{i}", [128, D], F32) for i in range(2)]
                tmpa = sba(rctx, "tmpa", [128, 1024], F32)
                tmpb = sba(rctx, "tmpb", [128, 1024], F32)
                with nc.Block() as blk:
                    @blk.sync
                    def _(sy):
                        sy.dma_start(out=cab[:, :], in_=h_cA.ap()).then_inc(sem["ld"], 16)
                        sy.dma_start(out=cbb[:, :], in_=h_cB.ap()).then_inc(sem["ld"], 16)
                        for ch in range(NCH):
                            if ch >= 2:
                                sy.wait_ge(sem["dv"], b["dv"] + ch - 1)
                            sy.dma_start(out=wsA[ch % 2][:, :],
                                         in_=h_wqA.ap()[ch * 128:(ch + 1) * 128, :]
                                         ).then_inc(sem["ld"], 16)
                            sy.dma_start(out=wsB[ch % 2][:, :],
                                         in_=h_wqB.ap()[ch * 128:(ch + 1) * 128, :]
                                         ).then_inc(sem["ld"], 16)

                    @blk.vector
                    def _(v):
                        for ch in range(NCH):
                            v.wait_ge(sem["ld"], b["ld"] + 32 + 32 * (ch + 1))
                            for hh in range(2):
                                sl = slice(hh * 1024, (hh + 1) * 1024)
                                v.tensor_mul(tmpa[:, :], wsA[ch % 2][:, sl], cab[:, sl])
                                v.tensor_mul(tmpb[:, :], wsB[ch % 2][:, sl], cbb[:, sl])
                                ins = v.tensor_add(wqp[:, ch * D + hh * 1024:
                                                       ch * D + (hh + 1) * 1024],
                                                   tmpa[:, :], tmpb[:, :])
                            ins.then_inc(sem["dv"], 1)
                cnt["ld"] += 16 * 34
                cnt["dv"] += 16

                b = dict(cnt)
                psq = [rctx.enter_context(
                    nc.psum_tensor(f"psQ{i}", [128, 512], F32)) for i in range(2)]
                with nc.Block() as blk:
                    @blk.tensor
                    def _(t):
                        t.wait_ge(sem["dv"], b["dv"])
                        for h in range(NH):
                            for tt in range(2):
                                gi = h * 2 + tt
                                if gi >= 2:
                                    t.wait_ge(sem["ac"], b["ac"] + gi - 1)
                                ins = None
                                for ch in range(NCH):
                                    ins = t.matmul(
                                        psq[gi % 2][:, :],
                                        wqp[:, ch * D + h * 128:ch * D + (h + 1) * 128],
                                        xn[:, ch * TOK + tt * 512:ch * TOK + (tt + 1) * 512],
                                        start=(ch == 0), stop=(ch == NCH - 1))
                                ins.then_inc(sem["pemm"], 1)

                    @blk.scalar
                    def _(s):
                        for gi in range(32):
                            h, tt = divmod(gi, 2)
                            s.wait_ge(sem["pemm"], b["pemm"] + gi + 1)
                            s.activation(out=qo[:, h * TOK + tt * 512:h * TOK + (tt + 1) * 512],
                                         in_=psq[gi % 2][:, :],
                                         func=AF.Copy).then_inc(sem["ac"], 1)
                cnt["pemm"] += 32
                cnt["ac"] += 32

            if debug:
                with nc.Block() as blk:
                    @blk.sync
                    def _(sy):
                        sy.dma_start(out=h_dxn.ap(), in_=xn[:, :]).then_inc(sem["st"], 16)
                        sy.dma_start(out=h_dq.ap(), in_=qo[:, :]).then_inc(sem["st"], 16)
                        sy.wait_ge(sem["st"], cnt["st"] + 32)
                cnt["st"] += 32

            with ExitStack() as actx:
                kfm = sba(actx, "kfm", [128, OB * TSEQ], BF16)
                vtm = sba(actx, "vtm", [128, OB * TSEQ], BF16)
                mskA = sba(actx, "mskA_s", [128, 1024], F32)
                mskB = sba(actx, "mskB_s", [128, 2048], F32)
                wkp_sb = sba(actx, "wkp_s", [128, NCH * DV], BF16)
                wvt_sb = sba(actx, "wvt_s", [128, NCH * DV], BF16)

                b = dict(cnt)
                NT = OB * 16
                with ExitStack() as pctx:
                    pstg = [sba(pctx, f"pstg{i}", [128, D], F32) for i in range(2)]
                    xnt = [sba(pctx, f"xnt{i}", [128, D], BF16) for i in range(2)]
                    xfmt = [sba(pctx, f"xfmt{i}", [128, NCH * 128], BF16) for i in range(2)]
                    sums = sba(pctx, "sums", [128, 8], F32)
                    junkb = sba(pctx, "junkb", [128, D], BF16)
                    tp = [pctx.enter_context(
                        nc.psum_tensor(f"psTp{i}", [128, 128], BF16)) for i in range(2)]
                    psk = pctx.enter_context(nc.psum_tensor("psK", [128, 128], F32))
                    psv = pctx.enter_context(nc.psum_tensor("psV", [128, 128], F32))
                    with nc.Block() as blk:
                        @blk.sync
                        def _(sy):
                            sy.dma_start(out=mskA[:, :], in_=h_mskA.ap()).then_inc(sem["ld"], 16)
                            sy.dma_start(out=mskB[:, :], in_=h_mskB.ap()).then_inc(sem["ld"], 16)
                            for i in range(NT):
                                if i >= 2:
                                    sy.wait_ge(sem["ap"], b["ap"] + i - 1)
                                sy.dma_start(out=pstg[i % 2][:, :],
                                             in_=h_xpre.ap()[i * 128:(i + 1) * 128, :]
                                             ).then_inc(sem["ld"], 16)

                        @blk.gpsimd
                        def _(g):
                            for dsb, hh in ((wkp_sb, h_wk), (wvt_sb, h_wv)):
                                g.dma_start(out=dsb[:, :].rearrange("p (c w) -> p c w", c=NCH),
                                            in_=hh.ap().rearrange("(c p) w -> p c w", p=128)
                                            ).then_inc(sem["wd"], 16)
                            g.wait_ge(sem["wd"], b["wd"] + 32)

                        @blk.vector
                        def _(v):
                            for i in range(NT):
                                p = i % 2
                                v.wait_ge(sem["ld"], b["ld"] + 32 + 16 * (i + 1))
                                v.tensor_scalar(out=junkb[:, :], in0=pstg[p][:, :],
                                                scalar1=1.0, scalar2=0.0, op0=ALU.mult,
                                                op1=ALU.add, accum_out=sums[:, 0:1])
                                v.scalar_tensor_tensor(out=junkb[:, :], in0=pstg[p][:, :],
                                                       scalar=1.0, in1=pstg[p][:, :],
                                                       op0=ALU.mult, op1=ALU.mult,
                                                       accum_out=sums[:, 1:2])
                                v.tensor_scalar_mul(sums[:, 2:3], sums[:, 0:1], 1.0 / D)
                                v.tensor_scalar_mul(sums[:, 4:5], sums[:, 1:2], 1.0 / D)
                                v.tensor_mul(sums[:, 3:4], sums[:, 2:3], sums[:, 2:3])
                                v.tensor_sub(sums[:, 5:6], sums[:, 4:5],
                                             sums[:, 3:4]).then_inc(sem["dv"], 1)
                                v.wait_ge(sem["ac"], b["ac"] + 17 * i + 1)
                                v.reciprocal(sums[:, 6:7],
                                             sums[:, 7:8]).then_inc(sem["rcp"], 1)
                                v.wait_ge(sem["rcp"], b["rcp"] + i + 1)
                                v.tensor_scalar(out=xnt[p][:, :], in0=pstg[p][:, :],
                                                scalar1=sums[:, 2:3], scalar2=sums[:, 6:7],
                                                op0=ALU.subtract, op1=ALU.mult
                                                ).then_inc(sem["ap"], 1)
                                ob, tb = divmod(i, 16)
                                v.wait_ge(sem["pemm"], b["pemm"] + 2 * i + 1)
                                v.tensor_copy(
                                    out=kfm[:, ob * TSEQ + tb * 128:ob * TSEQ + (tb + 1) * 128],
                                    in_=psk[:, :]).then_inc(sem["ac2"], 1)
                                v.wait_ge(sem["pemm"], b["pemm"] + 2 * i + 2)
                                v.tensor_copy(
                                    out=vtm[:, (ob * 16 + tb) * 128:(ob * 16 + tb + 1) * 128],
                                    in_=psv[:, :]).then_inc(sem["ac2"], 1)

                        @blk.scalar
                        def _(s):
                            for i in range(NT):
                                s.wait_ge(sem["dv"], b["dv"] + i + 1)
                                s.activation(out=sums[:, 7:8], in_=sums[:, 5:6],
                                             func=AF.Sqrt,
                                             bias=epsln[:, 0:1]).then_inc(sem["ac"], 1)
                                if i >= 2:
                                    s.wait_ge(sem["pemm"], b["pemm"] + 2 * (i - 1))
                                for ch in range(NCH):
                                    s.wait_ge(sem["T"], b["T"] + i * NCH + ch + 1)
                                    s.activation(out=xfmt[i % 2][:, ch * 128:(ch + 1) * 128],
                                                 in_=tp[(i * NCH + ch) % 2][:, :],
                                                 func=AF.Copy).then_inc(sem["ac"], 1)

                        @blk.tensor
                        def _(t):
                            for i in range(NT):
                                t.wait_ge(sem["ap"], b["ap"] + i + 1)
                                for ch in range(NCH):
                                    gT = i * NCH + ch
                                    if gT >= 2:
                                        gp = gT - 2
                                        ip, chp = divmod(gp, NCH)
                                        t.wait_ge(sem["ac"], b["ac"] + 17 * ip + chp + 2)
                                    t.transpose(tp[gT % 2][:, :],
                                                xnt[i % 2][:, ch * 128:(ch + 1) * 128],
                                                ident[:, :]).then_inc(sem["T"], 1)
                                t.wait_ge(sem["ac"], b["ac"] + 17 * i + 17)
                                if i >= 1:
                                    t.wait_ge(sem["ac2"], b["ac2"] + 2 * (i - 1) + 1)
                                ins = None
                                for ch in range(NCH):
                                    ins = t.matmul(psk[:, :], wkp_sb[:, ch * DV:(ch + 1) * DV],
                                                   xfmt[i % 2][:, ch * 128:(ch + 1) * 128],
                                                   start=(ch == 0), stop=(ch == NCH - 1))
                                ins.then_inc(sem["pemm"], 1)
                                if i >= 1:
                                    t.wait_ge(sem["ac2"], b["ac2"] + 2 * (i - 1) + 2)
                                ins = None
                                for ch in range(NCH):
                                    ins = t.matmul(psv[:, :],
                                                   xfmt[i % 2][:, ch * 128:(ch + 1) * 128],
                                                   wvt_sb[:, ch * DV:(ch + 1) * DV],
                                                   start=(ch == 0), stop=(ch == NCH - 1))
                                ins.then_inc(sem["pemm"], 1)
                    cnt["ld"] += 16 * (NT + 2)
                    cnt["rcp"] += NT
                    cnt["wd"] += 16 * 2
                    cnt["dv"] += NT
                    cnt["ap"] += NT
                    cnt["ac"] += NT * (NCH + 1)
                    cnt["T"] += NT * NCH
                    cnt["pemm"] += 2 * NT
                    cnt["ac2"] += 2 * NT

                if debug:
                    with nc.Block() as blk:
                        @blk.sync
                        def _(sy):
                            sy.dma_start(out=h_dk.ap(), in_=kfm[:, :]).then_inc(sem["st"], 16)
                            sy.dma_start(out=h_dv.ap(), in_=vtm[:, :]).then_inc(sem["st"], 16)
                            sy.wait_ge(sem["st"], cnt["st"] + 32)
                    cnt["st"] += 32

                combos = [(ob, h, bk) for ob in range(4) for h in range(NH)
                          for bk in (0, 1)]
                NKT = [8, 16]

                def nkt(i):
                    return NKT[combos[i][2]]

                def nseg(i):
                    return nkt(i) // 4

                cumseg = [int(x) for x in np.cumsum([0] + [nseg(i) for i in range(128)])]
                cumT = [int(x) for x in np.cumsum([0] + [nkt(i) for i in range(128)])]
                b = dict(cnt)
                with ExitStack() as atx:
                    ssb = [sba(atx, f"ssb{i}", [128, 2048], F32) for i in range(2)]
                    pbuf = [sba(atx, f"pbuf{i}", [128, 2048], BF16) for i in range(2)]
                    ptsb = [sba(atx, f"ptsb{i}", [128, 2048], BF16) for i in range(2)]
                    nm = sba(atx, "nm", [128, 2], F32)
                    rsum = sba(atx, "rsum", [128, 2], F32)
                    rinv = sba(atx, "rinv", [128, 2], F32)
                    ps_sc = atx.enter_context(nc.psum_tensor("psSC", [128, 2048], F32))
                    pspt = [atx.enter_context(
                        nc.psum_tensor(f"psPT{i}", [128, 128], BF16)) for i in range(2)]
                    ps_o = atx.enter_context(nc.psum_tensor("psO", [128, 128], F32))
                    with nc.Block() as blk:
                        @blk.tensor
                        def _(t):
                            for i in range(129):
                                if i < 128:
                                    ob, h, bk = combos[i]
                                    qcol = h * TOK + bk * 512 + ob * 128
                                    for g in range(nseg(i)):
                                        gseg = cumseg[i] + g
                                        if gseg >= 4:
                                            t.wait_ge(sem["ev"], b["ev"] + gseg - 3)
                                        t.matmul(ps_sc[:, (gseg % 4) * 512:(gseg % 4 + 1) * 512],
                                                 qo[:, qcol:qcol + 128],
                                                 kfm[:, ob * TSEQ + g * 512:
                                                     ob * TSEQ + (g + 1) * 512],
                                                 start=True, stop=True).then_inc(sem["pe"], 1)
                                if i >= 1:
                                    j = i - 1
                                    ob, h, bk = combos[j]
                                    t.wait_ge(sem["nrm"], b["nrm"] + j + 1)
                                    for kt in range(nkt(j)):
                                        gT = cumT[j] + kt
                                        if gT >= 2:
                                            t.wait_ge(sem["ptv"], b["ptv"] + gT - 1)
                                        t.transpose(pspt[gT % 2][:, :],
                                                    pbuf[j % 2][:, kt * 128:(kt + 1) * 128],
                                                    ident[:, :]).then_inc(sem["T"], 1)
                                    if j >= 1:
                                        t.wait_ge(sem["oev"], b["oev"] + j)
                                    ins = None
                                    for kt in range(nkt(j)):
                                        t.wait_ge(sem["ptv"], b["ptv"] + cumT[j] + kt + 1)
                                        ins = t.matmul(
                                            ps_o[:, :],
                                            vtm[:, (ob * 16 + kt) * 128:(ob * 16 + kt + 1) * 128],
                                            ptsb[j % 2][:, kt * 128:(kt + 1) * 128],
                                            start=(kt == 0), stop=(kt == nkt(j) - 1))
                                    ins.then_inc(sem["pv"], 1)

                        @blk.scalar
                        def _(s):
                            for i in range(129):
                                if i < 128:
                                    if i >= 2:
                                        s.wait_ge(sem["mx"], b["mx"] + i - 1)
                                        s.wait_ge(sem["T"], b["T"] + cumT[i - 1])
                                        s.wait_ge(sem["nrm"], b["nrm"] + i - 1)
                                    for g in range(nseg(i)):
                                        gseg = cumseg[i] + g
                                        s.wait_ge(sem["pe"], b["pe"] + gseg + 1)
                                        s.activation(
                                            out=ssb[i % 2][:, g * 512:(g + 1) * 512],
                                            in_=ps_sc[:, (gseg % 4) * 512:(gseg % 4 + 1) * 512],
                                            func=AF.Copy, scale=SCALE_A).then_inc(sem["ev"], 1)
                                    w = nkt(i) * 128
                                    s.wait_ge(sem["mx"], b["mx"] + i + 1)
                                    s.activation(out=pbuf[i % 2][:, 0:w],
                                                 in_=ssb[i % 2][:, 0:w], func=AF.Exp,
                                                 bias=nm[:, i % 2:i % 2 + 1],
                                                 accum_out=rsum[:, i % 2:i % 2 + 1]
                                                 ).then_inc(sem["exp"], 1)
                                if i >= 1:
                                    j = i - 1
                                    ob, h, bk = combos[j]
                                    s.wait_ge(sem["pv"], b["pv"] + j)
                                    for kt in range(nkt(j)):
                                        s.wait_ge(sem["T"], b["T"] + cumT[j] + kt + 1)
                                        s.activation(out=ptsb[j % 2][:, kt * 128:(kt + 1) * 128],
                                                     in_=pspt[(cumT[j] + kt) % 2][:, :],
                                                     func=AF.Copy).then_inc(sem["ptv"], 1)
                                    qcol = h * TOK + bk * 512 + ob * 128
                                    s.wait_ge(sem["pv"], b["pv"] + j + 1)
                                    s.activation(out=qo[:, qcol:qcol + 128], in_=ps_o[:, :],
                                                 func=AF.Copy).then_inc(sem["oev"], 1)

                        @blk.vector
                        def _(v):
                            for i in range(129):
                                if i < 128:
                                    w = nkt(i) * 128
                                    msk = mskA if combos[i][2] == 0 else mskB
                                    v.wait_ge(sem["ev"], b["ev"] + cumseg[i] + nseg(i))
                                    if i >= 2:
                                        v.wait_ge(sem["exp"], b["exp"] + i - 1)
                                    v.tensor_add(ssb[i % 2][:, 0:w], ssb[i % 2][:, 0:w],
                                                 msk[:, 0:w])
                                    v.reduce_max(nm[:, i % 2:i % 2 + 1], ssb[i % 2][:, 0:w],
                                                 axis=AX.X, negate=True).then_inc(sem["mx"], 1)
                                if i >= 1:
                                    j = i - 1
                                    w = nkt(j) * 128
                                    v.wait_ge(sem["exp"], b["exp"] + j + 1)
                                    v.reciprocal(rinv[:, j % 2:j % 2 + 1],
                                                 rsum[:, j % 2:j % 2 + 1]
                                                 ).then_inc(sem["rcp"], 1)
                                    v.wait_ge(sem["rcp"], b["rcp"] + j + 1)
                                    v.tensor_scalar_mul(pbuf[j % 2][:, 0:w],
                                                        pbuf[j % 2][:, 0:w],
                                                        rinv[:, j % 2:j % 2 + 1]
                                                        ).then_inc(sem["nrm"], 1)
                    cnt["pe"] += int(cumseg[128])
                    cnt["ev"] += int(cumseg[128])
                    cnt["exp"] += 128
                    cnt["mx"] += 128
                    cnt["nrm"] += 128
                    cnt["T"] += int(cumT[128])
                    cnt["ptv"] += int(cumT[128])
                    cnt["pv"] += 128
                    cnt["oev"] += 128
                    cnt["rcp"] += 128

                    if debug:
                        with nc.Block() as blk:
                            @blk.sync
                            def _(sy):
                                sy.dma_start(out=h_dqo.ap(), in_=qo[:, :]).then_inc(sem["st"], 16)
                                sy.dma_start(out=h_dss.ap()[:, 0:2048], in_=ssb[0][:, :]).then_inc(sem["st"], 16)
                                sy.dma_start(out=h_dss.ap()[:, 2048:4096], in_=ssb[1][:, :]).then_inc(sem["st"], 16)
                                sy.dma_start(out=h_dpb.ap()[:, 0:2048], in_=pbuf[0][:, :]).then_inc(sem["st"], 16)
                                sy.dma_start(out=h_dpb.ap()[:, 2048:4096], in_=pbuf[1][:, :]).then_inc(sem["st"], 16)
                                sy.dma_start(out=h_dpt.ap()[:, 0:2048], in_=ptsb[0][:, :]).then_inc(sem["st"], 16)
                                sy.dma_start(out=h_dpt.ap()[:, 2048:4096], in_=ptsb[1][:, :]).then_inc(sem["st"], 16)
                                sy.dma_start(out=h_dnm.ap()[:, 0:2], in_=nm[:, :]).then_inc(sem["st"], 16)
                                sy.dma_start(out=h_dnm.ap()[:, 2:4], in_=rsum[:, :]).then_inc(sem["st"], 16)
                                sy.dma_start(out=h_dnm.ap()[:, 4:6], in_=rinv[:, :]).then_inc(sem["st"], 16)
                                sy.wait_ge(sem["st"], cnt["st"] + 160)
                        cnt["st"] += 160

            b = dict(cnt)
            with ExitStack() as wctx:
                woab = [sba(wctx, f"woab{i}", [128, NCH * 128], BF16) for i in range(2)]
                xstg = [sba(wctx, f"xsw{i}", [128, TOK], F32) for i in range(2)]
                ostg = [sba(wctx, f"osw{i}", [128, 512], F32) for i in range(4)]
                psw = [wctx.enter_context(
                    nc.psum_tensor(f"psW{i}", [128, 512], F32)) for i in range(2)]
                with nc.Block() as blk:
                    @blk.gpsimd
                    def _(g):
                        for oc in range(NCH):
                            if oc >= 2:
                                g.wait_ge(sem["pemm"], b["pemm"] + 2 * (oc - 1))
                            g.dma_start(
                                out=woab[oc % 2][:, :].rearrange("p (c o) -> p c o", c=NCH),
                                in_=h_woa.ap()[:, oc * 128:(oc + 1) * 128]
                                .rearrange("(c p) o -> p c o", p=128)).then_inc(sem["wd"], 16)

                    @blk.sync
                    def _(sy):
                        for oc in range(NCH):
                            if oc >= 2:
                                sy.wait_ge(sem["dv"], b["dv"] + 2 * (oc - 1))
                            sy.dma_start(out=xstg[oc % 2][:, :],
                                         in_=h_xown.ap()[oc * 128:(oc + 1) * 128, :]
                                         ).then_inc(sem["ld"], 16)

                    @blk.scalar
                    def _(sy):
                        for idx in range(32):
                            oc, tt = divmod(idx, 2)
                            sy.wait_ge(sem["dv"], b["dv"] + idx + 1)
                            sy.dma_start(out=h_xs1.ap()[oc * 128:(oc + 1) * 128,
                                                        tt * 512:(tt + 1) * 512],
                                         in_=ostg[idx % 4][:, :]).then_inc(sem["st"], 16)
                        sy.wait_ge(sem["st"], b["st"] + 16 * 32)

                    @blk.tensor
                    def _(t):
                        for oc in range(NCH):
                            t.wait_ge(sem["wd"], b["wd"] + 16 * (oc + 1))
                            for tt in range(2):
                                idx = oc * 2 + tt
                                if idx >= 2:
                                    t.wait_ge(sem["dv"], b["dv"] + idx - 1)
                                ins = None
                                for ch in range(NCH):
                                    ins = t.matmul(
                                        psw[idx % 2][:, :],
                                        woab[oc % 2][:, ch * 128:(ch + 1) * 128],
                                        qo[:, ch * TOK + tt * 512:ch * TOK + (tt + 1) * 512],
                                        start=(ch == 0), stop=(ch == NCH - 1))
                                ins.then_inc(sem["pemm"], 1)

                    @blk.vector
                    def _(v):
                        for idx in range(32):
                            oc, tt = divmod(idx, 2)
                            v.wait_ge(sem["pemm"], b["pemm"] + idx + 1)
                            v.wait_ge(sem["ld"], b["ld"] + 16 * (oc + 1))
                            if idx >= 4:
                                v.wait_ge(sem["st"], b["st"] + 16 * (idx - 3))
                            v.tensor_add(ostg[idx % 4][:, :], psw[idx % 2][:, :],
                                         xstg[oc % 2][:, tt * 512:(tt + 1) * 512]
                                         ).then_inc(sem["dv"], 1)
                cnt["wd"] += 16 * NCH
                cnt["ld"] += 16 * NCH
                cnt["pemm"] += 32
                cnt["dv"] += 32
                cnt["st"] += 16 * 32

        LN("L2", h_xs1)
        FFN("f1", h_xs1, h_xs2)

        LN("L3", h_xs2)
        _attn_b(nc, sem, cnt, sba, xn, h_conv, wbb_sb, gcol, onesrf, ident,
                h_wqb, h_wkb, h_wvb, h_wob, h_xs2, h_xs3, epsln, eps12, eps12s)

        LN("L4", h_xs3)
        FFN("f2", h_xs3, h_y)

    return nc


def _attn_b(nc, sem, cnt, sba, xn, h_conv, wbb_sb, gcol, onesrf, ident,
            h_wqb, h_wkb, h_wvb, h_wob, h_src, h_dst, epsln, eps12, eps12s):
    from contextlib import ExitStack
    TERMS = {0: [(0, 3)], 1: [(0, 2), (1, 3)], 2: [(0, 1), (1, 2), (2, 3)],
             3: [(0, 0), (1, 1), (2, 2), (3, 3)]}
    CP = [(1, 0), (2, 0), (2, 1), (3, 0), (3, 1), (3, 2)]
    APRS = [(t, i) for t in range(4) for i in range(t + 1)]
    with ExitStack() as bctx:
        ofm = [sba(bctx, f"ofm{b_}", [128, NH * 512], BF16) for b_ in range(2)]
        for bk in range(2):
            with ExitStack() as dctx:
                yq = sba(dctx, f"yq{bk}", [128, 4 * D], BF16)
                yk = sba(dctx, f"yk{bk}", [128, 4 * D], BF16)
                yv = sba(dctx, f"yv{bk}", [128, 4 * D], BF16)

                with ExitStack() as pctx:
                    raw = sba(pctx, f"raw{bk}", [128, 4 * D], BF16)
                    wsl = [sba(pctx, f"wsl{bk}{i}", [128, NCH * 512], BF16)
                           for i in range(2)]
                    wtap = sba(pctx, f"wtap{bk}", [128, 4 * D], BF16)
                    ctmp = sba(pctx, f"ctmp{bk}", [128, D], BF16)
                    crow = [sba(pctx, f"crow{bk}{i}", [1, D], F32) for i in range(2)]
                    psp = [pctx.enter_context(
                        nc.psum_tensor(f"psP{bk}{i}", [128, 512], F32)) for i in range(4)]
                    psc = pctx.enter_context(nc.psum_tensor(f"psC{bk}", [128, 512], F32))
                    for pi, (hw, ybuf) in enumerate(
                            ((h_wqb, yq), (h_wkb, yk), (h_wvb, yv))):
                        b = dict(cnt)
                        with nc.Block() as blk:
                            @blk.sync
                            def _(sy):
                                for tap in range(4):
                                    if tap >= 2:
                                        sy.wait_ge(sem["pe"], b["pe"] + 4 * (tap - 1))
                                    sy.dma_start(
                                        out=crow[tap % 2][:, :],
                                        in_=h_conv.ap()[0:1, (pi * 4 + tap) * D:
                                                        (pi * 4 + tap + 1) * D]
                                        ).then_inc(sem["ld"], 16)

                            @blk.gpsimd
                            def _(g):
                                for ct in range(4):
                                    if ct >= 2:
                                        g.wait_ge(sem["pemm"], b["pemm"] + 4 * (ct - 1))
                                    g.dma_start(
                                        out=wsl[ct % 2][:, :]
                                        .rearrange("p (c o) -> p c o", c=NCH),
                                        in_=hw.ap()[:, ct * 512:(ct + 1) * 512]
                                        .rearrange("(c p) o -> p c o", p=128)
                                        ).then_inc(sem["wd"], 16)

                            @blk.tensor
                            def _(t):
                                for ct in range(4):
                                    t.wait_ge(sem["wd"], b["wd"] + 16 * (ct + 1))
                                    for s in range(4):
                                        gi = ct * 4 + s
                                        if gi >= 4:
                                            t.wait_ge(sem["ac"], b["ac"] + gi - 3)
                                        ins = None
                                        for ch in range(NCH):
                                            ins = t.matmul(
                                                psp[gi % 4][:, :],
                                                xn[:, ch * TOK + bk * 512 + s * 128:
                                                   ch * TOK + bk * 512 + (s + 1) * 128],
                                                wsl[ct % 2][:, ch * 512:(ch + 1) * 512],
                                                start=(ch == 0), stop=(ch == NCH - 1))
                                        ins.then_inc(sem["pemm"], 1)
                                for tap in range(4):
                                    t.wait_ge(sem["ld"], b["ld"] + 16 * (tap + 1))
                                    for sg in range(4):
                                        gi = 16 + tap * 4 + sg
                                        if gi >= 17:
                                            t.wait_ge(sem["ac"], b["ac"] + gi)
                                        t.matmul(psc[:, :], onesrf[:, :],
                                                 crow[tap % 2][0:1, sg * 512:(sg + 1) * 512],
                                                 start=True, stop=True).then_inc(sem["pe"], 1)

                            @blk.scalar
                            def _(s_):
                                for gi in range(16):
                                    ct, s = divmod(gi, 4)
                                    s_.wait_ge(sem["pemm"], b["pemm"] + gi + 1)
                                    s_.activation(
                                        out=raw[:, s * D + ct * 512:s * D + (ct + 1) * 512],
                                        in_=psp[gi % 4][:, :],
                                        func=AF.Copy).then_inc(sem["ac"], 1)
                                for gi in range(16):
                                    tap, sg = divmod(gi, 4)
                                    s_.wait_ge(sem["pe"], b["pe"] + gi + 1)
                                    s_.activation(
                                        out=wtap[:, tap * D + sg * 512:tap * D + (sg + 1) * 512],
                                        in_=psc[:, :], func=AF.Copy).then_inc(sem["ac"], 1)

                            @blk.vector
                            def _(v):
                                v.wait_ge(sem["ac"], b["ac"] + 32)
                                ins = None
                                for s in range(4):
                                    terms = TERMS[s]
                                    (s0, tp0) = terms[0]
                                    ysl = ybuf[:, s * D:(s + 1) * D]
                                    ins = v.tensor_mul(ysl, raw[:, s0 * D:(s0 + 1) * D],
                                                       wtap[:, tp0 * D:(tp0 + 1) * D])
                                    for (sj, tpj) in terms[1:]:
                                        v.tensor_mul(ctmp[:, :], raw[:, sj * D:(sj + 1) * D],
                                                     wtap[:, tpj * D:(tpj + 1) * D])
                                        ins = v.tensor_add(ysl, ysl, ctmp[:, :])
                                ins.then_inc(sem["dv"], 1)

                            @blk.scalar
                            def _(s_):
                                s_.wait_ge(sem["dv"], b["dv"] + 1)
                                for s in range(4):
                                    s_.activation(out=ybuf[:, s * D:(s + 1) * D],
                                                  in_=ybuf[:, s * D:(s + 1) * D],
                                                  func=AF.Silu).then_inc(sem["ac"], 1)
                        cnt["wd"] += 16 * 4
                        cnt["ld"] += 16 * 4
                        cnt["pemm"] += 16
                        cnt["pe"] += 16
                        cnt["ac"] += 36
                        cnt["dv"] += 1

                b = dict(cnt)
                with ExitStack() as rctx:
                    ubuf = sba(rctx, f"ubuf{bk}", [128, 4 * D], F32)
                    obuf = sba(rctx, f"obuf{bk}", [128, 4 * D], F32)
                    beta = sba(rctx, f"beta{bk}", [128, 4 * NH], F32)
                    dots = sba(rctx, f"dots{bk}", [128, 16 * NH], F32)
                    lsum = sba(rctx, f"lsum{bk}", [128, 8 * NH], F32)
                    scr = sba(rctx, f"scr{bk}", [128, D], BF16)
                    onb = sba(rctx, f"onb{bk}", [128, 2 * 128], BF16)
                    psb_ = rctx.enter_context(nc.psum_tensor(f"psBt{bk}", [128, NH], F32))
                    tpp = [rctx.enter_context(
                        nc.psum_tensor(f"psTo{bk}{i}", [128, 128], BF16)) for i in range(2)]
                    with nc.Block() as blk:
                        @blk.tensor
                        def _(t):
                            for s in range(4):
                                if s >= 1:
                                    t.wait_ge(sem["ac"], b["ac"] + s)
                                ins = None
                                for ch in range(NCH):
                                    ins = t.matmul(
                                        psb_[:, :],
                                        xn[:, ch * TOK + bk * 512 + s * 128:
                                           ch * TOK + bk * 512 + (s + 1) * 128],
                                        wbb_sb[:, ch * NH:(ch + 1) * NH],
                                        start=(ch == 0), stop=(ch == NCH - 1))
                                ins.then_inc(sem["pemm"], 1)
                            for idx in range(64):
                                t.wait_ge(sem["dv"], b["dv"] + 2 + idx)
                                if idx >= 2:
                                    t.wait_ge(sem["ac"], b["ac"] + 6 + idx)
                                t.transpose(tpp[idx % 2][:, :],
                                            onb[:, (idx % 2) * 128:(idx % 2 + 1) * 128],
                                            ident[:, :]).then_inc(sem["T"], 1)

                        @blk.scalar
                        def _(s_):
                            for s in range(4):
                                s_.wait_ge(sem["pemm"], b["pemm"] + s + 1)
                                s_.activation(out=beta[:, s * NH:(s + 1) * NH],
                                              in_=psb_[:, :],
                                              func=AF.Sigmoid).then_inc(sem["ac"], 1)
                            s_.wait_ge(sem["dv"], b["dv"] + 1)
                            s_.activation(out=lsum[:, 0:64], in_=lsum[:, 0:64],
                                          func=AF.Sqrt, scale=128.0,
                                          bias=eps12s[:, 0:1]).then_inc(sem["ac"], 1)
                            s_.activation(out=lsum[:, 64:128], in_=lsum[:, 64:128],
                                          func=AF.Sqrt,
                                          bias=eps12[:, 0:1]).then_inc(sem["ac"], 1)
                            s_.wait_ge(sem["mx"], b["mx"] + 1)
                            s_.activation(out=lsum[:, 0:64], in_=lsum[:, 0:64],
                                          func=AF.Sqrt, scale=1.0 / 128.0,
                                          bias=epsln[:, 0:1]).then_inc(sem["ac"], 1)
                            for idx in range(64):
                                s, h = divmod(idx, 16)
                                s_.wait_ge(sem["T"], b["T"] + idx + 1)
                                s_.activation(
                                    out=ofm[bk][:, h * 512 + s * 128:h * 512 + (s + 1) * 128],
                                    in_=tpp[idx % 2][:, :],
                                    func=AF.Copy).then_inc(sem["ac"], 1)

                        @blk.vector
                        def _(v):
                            ins = None
                            for (buf, col0) in ((yq, 0), (yk, 64)):
                                for s in range(4):
                                    for h in range(NH):
                                        sl = buf[:, s * D + h * 128:s * D + (h + 1) * 128]
                                        ins = v.scalar_tensor_tensor(
                                            out=scr[:, h * 128:(h + 1) * 128], in0=sl,
                                            scalar=1.0, in1=sl, op0=ALU.mult, op1=ALU.mult,
                                            accum_out=lsum[:, col0 + s * 16 + h:
                                                           col0 + s * 16 + h + 1])
                            ins.then_inc(sem["dv"], 1)
                            v.wait_ge(sem["ac"], b["ac"] + 6)
                            v.reciprocal(lsum[:, 0:64], lsum[:, 0:64])
                            v.reciprocal(lsum[:, 64:128],
                                         lsum[:, 64:128]).then_inc(sem["rcp"], 1)
                            v.wait_ge(sem["rcp"], b["rcp"] + 1)
                            for (buf, col0) in ((yq, 0), (yk, 64)):
                                for s in range(4):
                                    for h in range(NH):
                                        sl = buf[:, s * D + h * 128:s * D + (h + 1) * 128]
                                        v.tensor_scalar_mul(
                                            sl, sl,
                                            lsum[:, col0 + s * 16 + h:col0 + s * 16 + h + 1])
                            for p_, (tt, ii) in enumerate(CP):
                                for h in range(NH):
                                    v.scalar_tensor_tensor(
                                        out=scr[:, h * 128:(h + 1) * 128],
                                        in0=yk[:, tt * D + h * 128:tt * D + (h + 1) * 128],
                                        scalar=1.0,
                                        in1=yk[:, ii * D + h * 128:ii * D + (h + 1) * 128],
                                        op0=ALU.mult, op1=ALU.mult,
                                        accum_out=dots[:, p_ * 16 + h:p_ * 16 + h + 1]
                                        ) if not (p_ == len(CP) - 1 and h == NH - 1) else \
                                    v.scalar_tensor_tensor(
                                        out=scr[:, h * 128:(h + 1) * 128],
                                        in0=yk[:, tt * D + h * 128:tt * D + (h + 1) * 128],
                                        scalar=1.0,
                                        in1=yk[:, ii * D + h * 128:ii * D + (h + 1) * 128],
                                        op0=ALU.mult, op1=ALU.mult,
                                        accum_out=dots[:, p_ * 16 + h:p_ * 16 + h + 1]
                                        ).then_inc(sem["rcp"], 1)
                            v.wait_ge(sem["rcp"], b["rcp"] + 2)
                            v.tensor_scalar_mul(dots[:, 0:96], dots[:, 0:96], -1.0)
                            for p_, (tt, ii) in enumerate(APRS):
                                for h in range(NH):
                                    v.scalar_tensor_tensor(
                                        out=scr[:, h * 128:(h + 1) * 128],
                                        in0=yq[:, tt * D + h * 128:tt * D + (h + 1) * 128],
                                        scalar=1.0,
                                        in1=yk[:, ii * D + h * 128:ii * D + (h + 1) * 128],
                                        op0=ALU.mult, op1=ALU.mult,
                                        accum_out=dots[:, 96 + p_ * 16 + h:
                                                       96 + p_ * 16 + h + 1]
                                        ) if not (p_ == len(APRS) - 1 and h == NH - 1) else \
                                    v.scalar_tensor_tensor(
                                        out=scr[:, h * 128:(h + 1) * 128],
                                        in0=yq[:, tt * D + h * 128:tt * D + (h + 1) * 128],
                                        scalar=1.0,
                                        in1=yk[:, ii * D + h * 128:ii * D + (h + 1) * 128],
                                        op0=ALU.mult, op1=ALU.mult,
                                        accum_out=dots[:, 96 + p_ * 16 + h:
                                                       96 + p_ * 16 + h + 1]
                                        ).then_inc(sem["rcp"], 1)
                            v.wait_ge(sem["rcp"], b["rcp"] + 3)
                            v.wait_ge(sem["ac"], b["ac"] + 4)

                            def ccol(t_, i_):
                                return CP.index((t_, i_)) * 16

                            def acol(t_, i_):
                                return 96 + APRS.index((t_, i_)) * 16

                            for s in range(4):
                                if s == 0:
                                    for h in range(NH):
                                        v.tensor_scalar_mul(
                                            ubuf[:, h * 128:(h + 1) * 128],
                                            yv[:, h * 128:(h + 1) * 128],
                                            beta[:, h:h + 1])
                                else:
                                    v.tensor_copy(out=ubuf[:, s * D:(s + 1) * D],
                                                  in_=yv[:, s * D:(s + 1) * D])
                                    for i_ in range(s):
                                        pc = ccol(s, i_)
                                        for h in range(NH):
                                            v.scalar_tensor_tensor(
                                                out=ubuf[:, s * D + h * 128:
                                                         s * D + (h + 1) * 128],
                                                in0=ubuf[:, i_ * D + h * 128:
                                                         i_ * D + (h + 1) * 128],
                                                scalar=dots[:, pc + h:pc + h + 1],
                                                in1=ubuf[:, s * D + h * 128:
                                                         s * D + (h + 1) * 128],
                                                op0=ALU.mult, op1=ALU.add)
                                    for h in range(NH):
                                        v.tensor_scalar_mul(
                                            ubuf[:, s * D + h * 128:s * D + (h + 1) * 128],
                                            ubuf[:, s * D + h * 128:s * D + (h + 1) * 128],
                                            beta[:, s * 16 + h:s * 16 + h + 1])
                                for i_ in range(s + 1):
                                    pa = acol(s, i_)
                                    for h in range(NH):
                                        if i_ == 0:
                                            v.tensor_scalar_mul(
                                                obuf[:, s * D + h * 128:s * D + (h + 1) * 128],
                                                ubuf[:, h * 128:(h + 1) * 128],
                                                dots[:, pa + h:pa + h + 1])
                                        else:
                                            v.scalar_tensor_tensor(
                                                out=obuf[:, s * D + h * 128:
                                                         s * D + (h + 1) * 128],
                                                in0=ubuf[:, i_ * D + h * 128:
                                                         i_ * D + (h + 1) * 128],
                                                scalar=dots[:, pa + h:pa + h + 1],
                                                in1=obuf[:, s * D + h * 128:
                                                         s * D + (h + 1) * 128],
                                                op0=ALU.mult, op1=ALU.add)
                            ins = None
                            for s in range(4):
                                for h in range(NH):
                                    sl = obuf[:, s * D + h * 128:s * D + (h + 1) * 128]
                                    ins = v.scalar_tensor_tensor(
                                        out=scr[:, h * 128:(h + 1) * 128], in0=sl,
                                        scalar=1.0, in1=sl, op0=ALU.mult, op1=ALU.mult,
                                        accum_out=lsum[:, s * 16 + h:s * 16 + h + 1])
                            ins.then_inc(sem["mx"], 1)
                            v.wait_ge(sem["ac"], b["ac"] + 7)
                            v.reciprocal(lsum[:, 0:64],
                                         lsum[:, 0:64]).then_inc(sem["rcp"], 1)
                            v.wait_ge(sem["rcp"], b["rcp"] + 4)
                            for idx in range(64):
                                s, h = divmod(idx, 16)
                                if idx >= 2:
                                    v.wait_ge(sem["T"], b["T"] + idx - 1)
                                v.tensor_scalar_mul(
                                    onb[:, (idx % 2) * 128:(idx % 2 + 1) * 128],
                                    obuf[:, s * D + h * 128:s * D + (h + 1) * 128],
                                    lsum[:, s * 16 + h:s * 16 + h + 1]).then_inc(sem["dv"], 1)
                    cnt["pemm"] += 4
                    cnt["ac"] += 7 + 64
                    cnt["dv"] += 1 + 64
                    cnt["mx"] += 1
                    cnt["rcp"] += 4
                    cnt["T"] += 64

        b = dict(cnt)
        with ExitStack() as wctx:
            wobb = [sba(wctx, f"wobb{i}", [128, NCH * 128], BF16) for i in range(2)]
            xstg = [sba(wctx, f"xsb{i}", [128, TOK], F32) for i in range(2)]
            ostg = [sba(wctx, f"osb{i}", [128, 512], F32) for i in range(4)]
            psw = [wctx.enter_context(
                nc.psum_tensor(f"psWb{i}", [128, 512], F32)) for i in range(2)]
            with nc.Block() as blk:
                @blk.gpsimd
                def _(g):
                    for oc in range(NCH):
                        if oc >= 2:
                            g.wait_ge(sem["pemm"], b["pemm"] + 2 * (oc - 1))
                        g.dma_start(
                            out=wobb[oc % 2][:, :].rearrange("p (c o) -> p c o", c=NCH),
                            in_=h_wob.ap()[:, oc * 128:(oc + 1) * 128]
                            .rearrange("(c p) o -> p c o", p=128)).then_inc(sem["wd"], 16)

                @blk.vector
                def _(v):
                    def adds(idx):
                        oc, bk2 = divmod(idx, 2)
                        v.wait_ge(sem["pemm"], b["pemm"] + idx + 1)
                        v.wait_ge(sem["ld"], b["ld"] + 16 * (oc + 1))
                        if idx >= 4:
                            v.wait_ge(sem["st"], b["st"] + 16 * (idx - 3))
                        v.tensor_add(ostg[idx % 4][:, :], psw[idx % 2][:, :],
                                     xstg[oc % 2][:, bk2 * 512:(bk2 + 1) * 512]
                                     ).then_inc(sem["dv"], 1)

                    for oc in range(NCH):
                        v.wait_ge(sem["wd"], b["wd"] + 16 * (oc + 1))
                        v.tensor_scalar_mul(wobb[oc % 2][:, :], wobb[oc % 2][:, :],
                                            gcol[:, 0:1]).then_inc(sem["nrm"], 1)
                        if oc >= 1:
                            adds(2 * (oc - 1))
                            adds(2 * (oc - 1) + 1)
                    adds(30)
                    adds(31)

                @blk.sync
                def _(sy):
                    for oc in range(NCH):
                        if oc >= 2:
                            sy.wait_ge(sem["dv"], b["dv"] + 2 * (oc - 1))
                        sy.dma_start(out=xstg[oc % 2][:, :],
                                     in_=h_src.ap()[oc * 128:(oc + 1) * 128, :]
                                     ).then_inc(sem["ld"], 16)

                @blk.scalar
                def _(sy):
                    for idx in range(32):
                        oc, bk2 = divmod(idx, 2)
                        sy.wait_ge(sem["dv"], b["dv"] + idx + 1)
                        sy.dma_start(out=h_dst.ap()[oc * 128:(oc + 1) * 128,
                                                    bk2 * 512:(bk2 + 1) * 512],
                                     in_=ostg[idx % 4][:, :]).then_inc(sem["st"], 16)
                    sy.wait_ge(sem["st"], b["st"] + 16 * 32)

                @blk.tensor
                def _(t):
                    for oc in range(NCH):
                        t.wait_ge(sem["nrm"], b["nrm"] + oc + 1)
                        for bk2 in range(2):
                            idx = oc * 2 + bk2
                            if idx >= 2:
                                t.wait_ge(sem["dv"], b["dv"] + idx - 1)
                            ins = None
                            for ch in range(NCH):
                                ins = t.matmul(
                                    psw[idx % 2][:, :],
                                    wobb[oc % 2][:, ch * 128:(ch + 1) * 128],
                                    ofm[bk2][:, ch * 512:(ch + 1) * 512],
                                    start=(ch == 0), stop=(ch == NCH - 1))
                            ins.then_inc(sem["pemm"], 1)
            cnt["wd"] += 16 * NCH
            cnt["ld"] += 16 * NCH
            cnt["nrm"] += NCH
            cnt["pemm"] += 32
            cnt["dv"] += 32
            cnt["st"] += 16 * 32


def make_in_maps(inputs):
    x = np.asarray(inputs["L0_attn_a_hidden_states"], dtype=np.float32)
    srcA, srcB, cA, cB, pK = _rope_tables()
    WqT = np.ascontiguousarray(np.asarray(inputs["L0_attn_a_Wq"], np.float32).T)
    wqA = np.ascontiguousarray(WqT[:, srcA])
    wqB = np.ascontiguousarray(WqT[:, srcB])
    cAb = np.ascontiguousarray(np.broadcast_to(cA[None, :], (128, D)))
    cBb = np.ascontiguousarray(np.broadcast_to(cB[None, :], (128, D)))
    wkp = np.ascontiguousarray(np.asarray(inputs["L0_attn_a_Wk"], np.float32).T[:, pK])
    wvt = np.ascontiguousarray(np.asarray(inputs["L0_attn_a_Wv"], np.float32).T)
    woa = np.ascontiguousarray(np.asarray(inputs["L0_attn_a_Wo"], np.float32).T)
    wqb = np.ascontiguousarray(np.asarray(inputs["L0_attn_b_Wq"], np.float32))
    wkb = np.ascontiguousarray(np.asarray(inputs["L0_attn_b_Wk"], np.float32))
    wvb = np.ascontiguousarray(np.asarray(inputs["L0_attn_b_Wv"], np.float32))
    wbb = np.ascontiguousarray(np.asarray(inputs["L0_attn_b_Wb"], np.float32))
    wob = np.ascontiguousarray(np.asarray(inputs["L0_attn_b_Wo"], np.float32))
    convr = np.ascontiguousarray(np.concatenate(
        [np.asarray(inputs["L0_attn_b_conv_q"], np.float32).T.reshape(-1),
         np.asarray(inputs["L0_attn_b_conv_k"], np.float32).T.reshape(-1),
         np.asarray(inputs["L0_attn_b_conv_v"], np.float32).T.reshape(-1)]
    ).reshape(1, 3 * 4 * D))
    gcol_a = np.ascontiguousarray(
        np.asarray(inputs["L0_attn_b_g_norm"], np.float32).reshape(128, 1))
    w1t = np.ascontiguousarray(np.asarray(inputs["L0_ffn_W1"], np.float32).T)
    w3t = np.ascontiguousarray(np.asarray(inputs["L0_ffn_W3"], np.float32).T)
    w2t = np.ascontiguousarray(np.asarray(inputs["L0_ffn_W2"], np.float32).T)

    identity = np.eye(128, dtype=BF16NP)
    h128 = _hadamard128().astype(BF16NP)
    onesc = np.ones((128, 1), dtype=BF16NP)
    onesrf = np.ones((1, 128), dtype=np.float32)

    x_fm = np.ascontiguousarray(x.transpose(2, 0, 1))
    x_tm = np.ascontiguousarray(x.reshape(OB * TSEQ, D))

    ql = np.arange(128)
    in_maps = []
    for c in range(NCORE):
        bA, bB = c, 15 - c
        xown = np.ascontiguousarray(np.concatenate(
            [x_fm[:, :, bA * 128:(bA + 1) * 128].reshape(D, 512),
             x_fm[:, :, bB * 128:(bB + 1) * 128].reshape(D, 512)], axis=1))
        kA = np.arange(1024)
        mskA = np.where(kA[None, :] <= bA * 128 + ql[:, None], 0.0, NEG
                        ).astype(np.float32)
        kB = np.arange(2048)
        mskB = np.where(kB[None, :] <= bB * 128 + ql[:, None], 0.0, NEG
                        ).astype(np.float32)
        in_maps.append({
            "xown": xown, "xpre": x_tm, "wqA": wqA, "wqB": wqB, "cAb": cAb,
            "cBb": cBb, "wkp": wkp, "wvt": wvt, "woa": woa, "wqb": wqb,
            "wkb": wkb, "wvb": wvb, "wbb": wbb, "wob": wob, "convr": convr,
            "gcol": gcol_a, "w1t": w1t, "w3t": w3t, "w2t": w2t,
            "ident": identity, "h128": h128, "onesc": onesc, "onesrf": onesrf,
            "mskA": mskA, "mskB": mskB,
        })

    return in_maps


def assemble(res_results, key="y"):
    out = np.empty((OB, TSEQ, D), dtype=np.float32)
    for c in range(NCORE):
        y = res_results[c][key]
        y4 = y.reshape(D, 2, OB, 128)
        for bi, bblk in enumerate((c, 15 - c)):
            out[:, bblk * 128:(bblk + 1) * 128, :] = y4[:, bi].transpose(1, 2, 0)
    return out


def kernel(**inputs):
    in_maps = make_in_maps(inputs)
    nc = build_program()
    res = run_bass_kernel_spmd(nc, in_maps, core_ids=list(range(NCORE)))
    return assemble(res.results, "y")


if __name__ == "__main__":
    build_program()
    print("build ok")


# version 03
# speedup vs baseline: 1.2038x; 1.2038x over previous
import numpy as np
import ml_dtypes
from contextlib import ExitStack

import concourse.bass as bass
import concourse.mybir as mybir
from concourse.bass_utils import run_bass_kernel_spmd

F32 = mybir.dt.float32
BF16 = mybir.dt.bfloat16
AF = mybir.ActivationFunctionType
ALU = mybir.AluOpType
AX = mybir.AxisListType
BF16NP = ml_dtypes.bfloat16

D = 2048
OB = 4
TSEQ = 2048
NCH = 16
TOK = 1024
NH, DV = 16, 128
HID, HB = 8192, 64
NCORE = 8
SCALE_A = float(1.0 / np.sqrt(128.0))
S8192 = float(1.0 / np.sqrt(8192.0))
EPS_LN = 1e-5
NEG = -1e9

SEMS = ["ld", "wd", "pe", "pemm", "T", "pv", "ev", "exp", "ptv", "oev",
        "mx", "nrm", "cp", "sq", "dv", "ac", "ac2", "ap", "st", "rcp"]


def _rope_tables():
    theta = 10000.0 ** (-np.arange(32) / 32.0)
    srcA = np.zeros(D, dtype=np.int64)
    srcB = np.zeros(D, dtype=np.int64)
    cA = np.zeros(D, dtype=np.float32)
    cB = np.zeros(D, dtype=np.float32)
    for h in range(NH):
        cos, sin = np.cos(h * theta), np.sin(h * theta)
        for j in range(128):
            o = h * 128 + j
            if j < 32:
                m = j; a = 2 * m; ca, cb = cos[m], -sin[m]
            elif j < 64:
                m = j - 32; a = 2 * m; ca, cb = sin[m], cos[m]
            elif j < 96:
                m = j - 64; a = 64 + 2 * m; ca, cb = cos[m], -sin[m]
            else:
                m = j - 96; a = 64 + 2 * m; ca, cb = sin[m], cos[m]
            srcA[o], srcB[o] = h * 128 + a, h * 128 + a + 1
            cA[o], cB[o] = ca, cb
    pK = np.zeros(128, dtype=np.int64)
    for j in range(128):
        if j < 32:
            pK[j] = 2 * j
        elif j < 64:
            pK[j] = 2 * (j - 32) + 1
        elif j < 96:
            pK[j] = 64 + 2 * (j - 64)
        else:
            pK[j] = 64 + 2 * (j - 96) + 1
    return srcA, srcB, cA, cB, pK


def _hadamard128():
    i = np.arange(128)
    pc = np.zeros((128, 128), dtype=np.int64)
    for b in range(7):
        pc += ((i[:, None] >> b) & 1) * ((i[None, :] >> b) & 1)
    return np.where(pc % 2 == 0, 1.0, -1.0).astype(np.float32)


def build_program(debug=False):
    nc = bass.Bass("TRN2", target_bir_lowering=False)
    dt = nc.dram_tensor
    dbg = dict(kind="ExternalOutput") if debug else {}

    h_xown = dt("xown", [D, TOK], F32, kind="ExternalInput")
    h_xpre = dt("xpre", [OB * TSEQ, D], F32, kind="ExternalInput")
    h_wqA = dt("wqA", [D, D], F32, kind="ExternalInput")
    h_wqB = dt("wqB", [D, D], F32, kind="ExternalInput")
    h_cA = dt("cAb", [128, D], F32, kind="ExternalInput")
    h_cB = dt("cBb", [128, D], F32, kind="ExternalInput")
    h_wk = dt("wkp", [D, DV], F32, kind="ExternalInput")
    h_wv = dt("wvt", [D, DV], F32, kind="ExternalInput")
    h_woa = dt("woa", [D, D], F32, kind="ExternalInput")
    h_wqb = dt("wqb", [D, D], F32, kind="ExternalInput")
    h_wkb = dt("wkb", [D, D], F32, kind="ExternalInput")
    h_wvb = dt("wvb", [D, D], F32, kind="ExternalInput")
    h_wbb = dt("wbb", [D, NH], F32, kind="ExternalInput")
    h_wob = dt("wob", [D, D], F32, kind="ExternalInput")
    h_conv = dt("convr", [1, 3 * 4 * D], F32, kind="ExternalInput")
    h_g = dt("gcol", [128, 1], F32, kind="ExternalInput")
    h_w1 = dt("w1t", [D, HID], F32, kind="ExternalInput")
    h_w3 = dt("w3t", [D, HID], F32, kind="ExternalInput")
    h_w2 = dt("w2t", [HID, D], F32, kind="ExternalInput")
    h_id = dt("ident", [128, 128], BF16, kind="ExternalInput")
    h_h128 = dt("h128", [128, 128], BF16, kind="ExternalInput")
    h_onesc = dt("onesc", [128, 1], BF16, kind="ExternalInput")
    h_onesrf = dt("onesrf", [1, 128], F32, kind="ExternalInput")
    h_onesrb = dt("onesrb", [1, 128], BF16, kind="ExternalInput")
    h_mskA = dt("mskA", [128, 1024], F32, kind="ExternalInput")
    h_mskB = dt("mskB", [128, 2048], F32, kind="ExternalInput")

    h_y = dt("y", [D, TOK], F32, kind="ExternalOutput")
    if debug:
        h_dxn = dt("dxn", [128, NCH * TOK], BF16, kind="ExternalOutput")
        h_dq = dt("dq", [128, NH * TOK], BF16, kind="ExternalOutput")
        h_dk = dt("dk", [128, OB * TSEQ], BF16, kind="ExternalOutput")
        h_dv = dt("dv", [128, OB * TSEQ], BF16, kind="ExternalOutput")
        h_dqo = dt("dqo", [128, NH * TOK], BF16, kind="ExternalOutput")
        h_dss = dt("dss", [128, 2 * 2048], F32, kind="ExternalOutput")
        h_dpb = dt("dpb", [128, 2 * 2048], BF16, kind="ExternalOutput")
        h_dpt = dt("dpt", [128, 2 * 2048], BF16, kind="ExternalOutput")
        h_dnm = dt("dnm", [128, 6], F32, kind="ExternalOutput")
    h_xs1 = dt("xs1", [D, TOK], F32, **dbg)
    h_xs2 = dt("xs2", [D, TOK], F32, **dbg)
    h_xs3 = dt("xs3", [D, TOK], F32, **dbg)

    cnt = {k: 0 for k in SEMS}

    with ExitStack() as ctx:
        def sba(stack, name, shape, d):
            return stack.enter_context(nc.sbuf_tensor(name, shape, d))

        sem = {k: ctx.enter_context(nc.semaphore("s_" + k)) for k in SEMS}

        xn = sba(ctx, "xn", [128, NCH * TOK], BF16)
        ident = sba(ctx, "ident_s", [128, 128], BF16)
        h128 = sba(ctx, "h128_s", [128, 128], BF16)
        onesc = sba(ctx, "onesc_s", [128, 1], BF16)
        onesrf = sba(ctx, "onesrf_s", [1, 128], F32)
        onesrb = sba(ctx, "onesrb_s", [1, 128], BF16)
        gcol = sba(ctx, "gcol_s", [128, 1], F32)
        wbb_sb = sba(ctx, "wbb_s", [128, NCH * NH], BF16)
        epsln = sba(ctx, "epsln", [128, 1], F32)
        eps12 = sba(ctx, "eps12", [128, 1], F32)
        eps12s = sba(ctx, "eps12s", [128, 1], F32)

        with nc.Block() as blk:
            @blk.sync
            def _(sy):
                loads = [(ident, h_id), (h128, h_h128), (onesc, h_onesc),
                         (onesrf, h_onesrf), (onesrb, h_onesrb), (gcol, h_g)]
                for dsb, hh in loads:
                    sy.dma_start(out=dsb[:, :], in_=hh.ap()).then_inc(sem["ld"], 16)
                sy.wait_ge(sem["ld"], cnt["ld"] + 16 * len(loads))

            @blk.vector
            def _(v):
                v.memset(epsln[:, :], EPS_LN)
                v.memset(eps12[:, :], 1e-12)
                v.memset(eps12s[:, :], 128.0 * 1e-12)

            @blk.gpsimd
            def _(g):
                g.dma_start(out=wbb_sb[:, :].rearrange("p (c w) -> p c w", c=NCH),
                            in_=h_wbb.ap().rearrange("(c p) w -> p c w", p=128)
                            ).then_inc(sem["wd"], 16)
                g.wait_ge(sem["wd"], cnt["wd"] + 16)
        cnt["ld"] += 16 * 6
        cnt["wd"] += 16

        def LN(tag, src):
            b = dict(cnt)
            with ExitStack() as lctx:
                xstg = [sba(lctx, f"xstg{tag}{i}", [128, TOK], F32) for i in range(2)]
                sqs = [sba(lctx, f"sqs{tag}{i}", [128, TOK], BF16) for i in range(2)]
                rows = sba(lctx, f"rows{tag}", [1, 6 * TOK], F32)
                ab_b = sba(lctx, f"abb{tag}", [1, 2 * TOK], BF16)
                a_bc = sba(lctx, f"abc{tag}", [128, TOK], F32)
                b_bc = sba(lctx, f"bbc{tag}", [128, TOK], F32)
                tmpf = sba(lctx, f"tmpf{tag}", [128, TOK], F32)
                ps_s0 = lctx.enter_context(nc.psum_tensor(f"pss0{tag}", [1, TOK], F32))
                ps_s1 = lctx.enter_context(nc.psum_tensor(f"pss1{tag}", [1, TOK], F32))
                ps_b0 = lctx.enter_context(nc.psum_tensor(f"psb0{tag}", [128, TOK], F32))
                ps_b1 = lctx.enter_context(nc.psum_tensor(f"psb1{tag}", [128, TOK], F32))

                def row(i):
                    return rows[0:1, i * TOK:(i + 1) * TOK]

                with nc.Block() as blk:
                    @blk.sync
                    def _(sy):
                        for ch in range(NCH):
                            if ch >= 2:
                                sy.wait_ge(sem["cp"], b["cp"] + ch - 1)
                                sy.wait_ge(sem["sq"], b["sq"] + ch - 1)
                            sy.dma_start(out=xstg[ch % 2][:, :],
                                         in_=src.ap()[ch * 128:(ch + 1) * 128, :]
                                         ).then_inc(sem["ld"], 16)
                        sy.wait_ge(sem["cp"], b["cp"] + 16)
                        sy.wait_ge(sem["sq"], b["sq"] + 16)
                        for ch in range(NCH):
                            if ch >= 2:
                                sy.wait_ge(sem["ap"], b["ap"] + ch - 1)
                            sy.dma_start(out=xstg[ch % 2][:, :],
                                         in_=src.ap()[ch * 128:(ch + 1) * 128, :]
                                         ).then_inc(sem["ld"], 16)

                    @blk.vector
                    def _(v):
                        for ch in range(NCH):
                            v.wait_ge(sem["ld"], b["ld"] + 16 * (ch + 1))
                            v.tensor_copy(out=xn[:, ch * TOK:(ch + 1) * TOK],
                                          in_=xstg[ch % 2][:, :]).then_inc(sem["cp"], 1)
                        v.wait_ge(sem["pe"], b["pe"] + 1)
                        v.wait_ge(sem["pemm"], b["pemm"] + 16)
                        v.tensor_scalar_mul(row(0), ps_s0[:, :], 1.0 / D)
                        v.tensor_scalar_mul(row(1), ps_s1[:, :], 1.0 / D)
                        v.tensor_mul(tmpf[0:1, :], row(0), row(0))
                        v.tensor_sub(row(2), row(1), tmpf[0:1, :]).then_inc(sem["dv"], 1)
                        v.wait_ge(sem["ac"], b["ac"] + 1)
                        v.reciprocal(row(4), row(3))
                        v.tensor_copy(out=ab_b[0:1, 0:TOK], in_=row(4))
                        v.tensor_mul(tmpf[0:1, :], row(0), row(4))
                        v.tensor_scalar_mul(ab_b[0:1, TOK:2 * TOK], tmpf[0:1, :],
                                            -1.0).then_inc(sem["dv"], 1)
                        for ch in range(NCH):
                            v.wait_ge(sem["ld"], b["ld"] + 16 * (16 + ch + 1))
                            if ch == 0:
                                v.wait_ge(sem["ac"], b["ac"] + 2)
                            v.tensor_mul(tmpf[:, :], xstg[ch % 2][:, :], a_bc[:, :])
                            v.tensor_add(xn[:, ch * TOK:(ch + 1) * TOK], tmpf[:, :],
                                         b_bc[:, :]).then_inc(sem["ap"], 1)

                    @blk.scalar
                    def _(s):
                        for ch in range(NCH):
                            s.wait_ge(sem["ld"], b["ld"] + 16 * (ch + 1))
                            if ch >= 2:
                                s.wait_ge(sem["pemm"], b["pemm"] + ch - 1)
                            s.activation(out=sqs[ch % 2][:, :], in_=xstg[ch % 2][:, :],
                                         func=AF.Square).then_inc(sem["sq"], 1)
                        s.wait_ge(sem["dv"], b["dv"] + 1)
                        s.activation(out=row(3), in_=row(2), func=AF.Sqrt,
                                     bias=epsln[0:1, 0:1]).then_inc(sem["ac"], 1)
                        s.wait_ge(sem["pe"], b["pe"] + 3)
                        s.activation(out=a_bc[:, :], in_=ps_b0[:, :], func=AF.Copy)
                        s.activation(out=b_bc[:, :], in_=ps_b1[:, :],
                                     func=AF.Copy).then_inc(sem["ac"], 1)

                    @blk.tensor
                    def _(t):
                        for ch in range(NCH):
                            t.wait_ge(sem["cp"], b["cp"] + ch + 1)
                            t.wait_ge(sem["sq"], b["sq"] + ch + 1)
                            ins = None
                            for hh in range(2):
                                ins = t.matmul(ps_s0[0:1, hh * 512:(hh + 1) * 512], onesc[:, :],
                                               xn[:, ch * TOK + hh * 512:ch * TOK + (hh + 1) * 512],
                                               start=(ch == 0), stop=(ch == NCH - 1))
                            if ch == NCH - 1:
                                ins.then_inc(sem["pe"], 1)
                            for hh in range(2):
                                ins = t.matmul(ps_s1[0:1, hh * 512:(hh + 1) * 512], onesc[:, :],
                                               sqs[ch % 2][:, hh * 512:(hh + 1) * 512],
                                               start=(ch == 0), stop=(ch == NCH - 1))
                            ins.then_inc(sem["pemm"], 1)
                        t.wait_ge(sem["dv"], b["dv"] + 2)
                        for hh in range(2):
                            ins = t.matmul(ps_b0[:, hh * 512:(hh + 1) * 512], onesrb[:, :],
                                           ab_b[0:1, hh * 512:(hh + 1) * 512],
                                           start=True, stop=True)
                        ins.then_inc(sem["pe"], 1)
                        for hh in range(2):
                            ins = t.matmul(ps_b1[:, hh * 512:(hh + 1) * 512], onesrb[:, :],
                                           ab_b[0:1, TOK + hh * 512:TOK + (hh + 1) * 512],
                                           start=True, stop=True)
                        ins.then_inc(sem["pe"], 1)

            cnt["ld"] += 16 * 32
            cnt["cp"] += 16
            cnt["sq"] += 16
            cnt["pemm"] += 16
            cnt["pe"] += 3
            cnt["ac"] += 2
            cnt["dv"] += 2
            cnt["ap"] += 16

        def FFN(tag, src, dst):
            with ExitStack() as fctx:
                hid = sba(fctx, f"hid{tag}", [128, HB * TOK], BF16)
                b = dict(cnt)
                with ExitStack() as actx:
                    w1b = [sba(actx, f"w1b{tag}{i}", [128, NCH * 128], BF16) for i in range(2)]
                    w3b = [sba(actx, f"w3b{tag}{i}", [128, NCH * 128], BF16) for i in range(2)]
                    gs = [sba(actx, f"gs{tag}{i}", [128, TOK], BF16) for i in range(2)]
                    tA = sba(actx, f"tA{tag}", [128, TOK], BF16)
                    tB = sba(actx, f"tB{tag}", [128, TOK], BF16)
                    ub = [sba(actx, f"ub{tag}{i}", [128, TOK], BF16) for i in range(2)]
                    psb = [actx.enter_context(
                        nc.psum_tensor(f"psA{tag}{i}", [128, 512], F32)) for i in range(8)]
                    with nc.Block() as blk:
                        @blk.gpsimd
                        def _(g):
                            for hb in range(HB):
                                if hb >= 2:
                                    g.wait_ge(sem["pemm"], b["pemm"] + 4 * (hb - 1))
                                for hsrc, wb in ((h_w1, w1b), (h_w3, w3b)):
                                    g.dma_start(
                                        out=wb[hb % 2][:, :].rearrange("p (c o) -> p c o", c=NCH),
                                        in_=hsrc.ap()[:, hb * 128:(hb + 1) * 128]
                                        .rearrange("(c p) o -> p c o", p=128)
                                    ).then_inc(sem["wd"], 16)

                        @blk.tensor
                        def _(t):
                            def h128mm(c):
                                t.wait_ge(sem["dv"], b["dv"] + c + 1)
                                t.wait_ge(sem["ac"], b["ac"] + c + 1)
                                bb2 = (c % 2) * 4
                                for tt in range(2):
                                    t.matmul(psb[bb2 + tt][:, :], h128[:, :],
                                             hid[:, c * TOK + tt * 512:c * TOK + (tt + 1) * 512],
                                             start=True, stop=True).then_inc(sem["pe"], 1)

                            for hb in range(HB):
                                t.wait_ge(sem["wd"], b["wd"] + 32 * (hb + 1))
                                if hb >= 2:
                                    t.wait_ge(sem["ac"], b["ac"] + hb - 1)
                                    t.wait_ge(sem["dv"], b["dv"] + hb - 1)
                                    t.wait_ge(sem["ac2"], b["ac2"] + 2 * (hb - 1))
                                bb = (hb % 2) * 4
                                for wsel, boff in ((w1b, 0), (w3b, 2)):
                                    for tt in range(2):
                                        ins = None
                                        for ch in range(NCH):
                                            ins = t.matmul(
                                                psb[bb + boff + tt][:, :],
                                                wsel[hb % 2][:, ch * 128:(ch + 1) * 128],
                                                xn[:, ch * TOK + tt * 512:ch * TOK + (tt + 1) * 512],
                                                start=(ch == 0), stop=(ch == NCH - 1))
                                        ins.then_inc(sem["pemm"], 1)
                                if hb >= 1:
                                    h128mm(hb - 1)
                            h128mm(HB - 1)

                        @blk.scalar
                        def _(s):
                            def h128ev(c):
                                bb2 = (c % 2) * 4
                                for tt in range(2):
                                    s.wait_ge(sem["pe"], b["pe"] + 2 * c + tt + 1)
                                    s.activation(out=hid[:, c * TOK + tt * 512:
                                                         c * TOK + (tt + 1) * 512],
                                                 in_=psb[bb2 + tt][:, :], func=AF.Copy,
                                                 scale=S8192).then_inc(sem["ac2"], 1)

                            for hb in range(HB):
                                bb = (hb % 2) * 4
                                if hb >= 2:
                                    s.wait_ge(sem["dv"], b["dv"] + hb - 1)
                                ins = None
                                for tt in range(2):
                                    s.wait_ge(sem["pemm"], b["pemm"] + 4 * hb + tt + 1)
                                    ins = s.activation(out=gs[hb % 2][:, tt * 512:(tt + 1) * 512],
                                                       in_=psb[bb + tt][:, :], func=AF.Silu)
                                ins.then_inc(sem["ac"], 1)
                                if hb >= 1:
                                    h128ev(hb - 1)
                            h128ev(HB - 1)

                        @blk.vector
                        def _(v):
                            def radix4(c0, c1, c2, c3):
                                s0 = hid[:, c0 * TOK:(c0 + 1) * TOK]
                                s1 = hid[:, c1 * TOK:(c1 + 1) * TOK]
                                s2 = hid[:, c2 * TOK:(c2 + 1) * TOK]
                                s3 = hid[:, c3 * TOK:(c3 + 1) * TOK]
                                v.tensor_add(tA[:, :], s0, s1)
                                v.tensor_sub(tB[:, :], s0, s1)
                                v.tensor_add(s0, s2, s3)
                                v.tensor_sub(s1, s2, s3)
                                v.tensor_sub(s2, tA[:, :], s0)
                                v.tensor_sub(s3, tB[:, :], s1)
                                v.tensor_add(s0, tA[:, :], s0)
                                v.tensor_add(s1, tB[:, :], s1)

                            for hb in range(HB):
                                bb = (hb % 2) * 4
                                for tt in range(2):
                                    v.wait_ge(sem["pemm"], b["pemm"] + 4 * hb + 2 + tt + 1)
                                    v.tensor_copy(out=ub[hb % 2][:, tt * 512:(tt + 1) * 512],
                                                  in_=psb[bb + 2 + tt][:, :])
                                v.wait_ge(sem["ac"], b["ac"] + hb + 1)
                                v.tensor_mul(hid[:, hb * TOK:(hb + 1) * TOK], gs[hb % 2][:, :],
                                             ub[hb % 2][:, :]).then_inc(sem["dv"], 1)
                                if hb % 4 == 3:
                                    qd = hb // 4
                                    v.wait_ge(sem["ac2"], b["ac2"] + 2 * (4 * qd + 4))
                                    radix4(4 * qd, 4 * qd + 1, 4 * qd + 2, 4 * qd + 3)
                                if hb % 16 == 15:
                                    g2 = hb // 16
                                    for j in range(4):
                                        radix4(16 * g2 + j, 16 * g2 + j + 4,
                                               16 * g2 + j + 8, 16 * g2 + j + 12)
                            for j in range(16):
                                radix4(j, j + 16, j + 32, j + 48)
                    cnt["wd"] += 16 * 2 * HB
                    cnt["pemm"] += 4 * HB
                    cnt["pe"] += 2 * HB
                    cnt["ac"] += HB
                    cnt["ac2"] += 2 * HB
                    cnt["dv"] += HB

                b = dict(cnt)
                with ExitStack() as dctx:
                    w2b = [sba(dctx, f"w2b{tag}{i}", [128, 32 * 128], BF16) for i in range(3)]
                    xstg = [sba(dctx, f"xsd{tag}{i}", [128, TOK], F32) for i in range(2)]
                    ostg = [sba(dctx, f"osd{tag}{i}", [128, 512], F32) for i in range(4)]
                    pso = [dctx.enter_context(
                        nc.psum_tensor(f"psD{tag}{i}", [128, 512], F32)) for i in range(2)]
                    with nc.Block() as blk:
                        @blk.gpsimd
                        def _(g):
                            for half in range(32):
                                dc, hh = divmod(half, 2)
                                if half >= 3:
                                    g.wait_ge(sem["pemm"],
                                              b["pemm"] + 2 * ((half - 3) // 2) + 2)
                                g.dma_start(
                                    out=w2b[half % 3][:, :].rearrange("p (c o) -> p c o", c=32),
                                    in_=h_w2.ap()[hh * 4096:(hh + 1) * 4096,
                                                  dc * 128:(dc + 1) * 128]
                                    .rearrange("(c p) o -> p c o", p=128)).then_inc(sem["wd"], 16)

                        @blk.sync
                        def _(sy):
                            for dc in range(NCH):
                                if dc >= 2:
                                    sy.wait_ge(sem["dv"], b["dv"] + 2 * (dc - 1))
                                sy.dma_start(out=xstg[dc % 2][:, :],
                                             in_=src.ap()[dc * 128:(dc + 1) * 128, :]
                                             ).then_inc(sem["ld"], 16)

                        @blk.scalar
                        def _(sy):
                            for idx in range(32):
                                dc, tt = divmod(idx, 2)
                                sy.wait_ge(sem["dv"], b["dv"] + idx + 1)
                                sy.dma_start(out=dst.ap()[dc * 128:(dc + 1) * 128,
                                                          tt * 512:(tt + 1) * 512],
                                             in_=ostg[idx % 4][:, :]).then_inc(sem["st"], 16)
                            sy.wait_ge(sem["st"], b["st"] + 16 * 32)

                        @blk.tensor
                        def _(t):
                            for dc in range(NCH):
                                for tt in range(2):
                                    idx = dc * 2 + tt
                                    if idx >= 2:
                                        t.wait_ge(sem["dv"], b["dv"] + idx - 1)
                                    ins = None
                                    for hb in range(HB):
                                        half = dc * 2 + hb // 32
                                        if hb % 32 == 0 and tt == 0:
                                            t.wait_ge(sem["wd"], b["wd"] + 16 * (half + 1))
                                        ins = t.matmul(
                                            pso[idx % 2][:, :],
                                            w2b[half % 3][:, (hb % 32) * 128:(hb % 32 + 1) * 128],
                                            hid[:, hb * TOK + tt * 512:hb * TOK + (tt + 1) * 512],
                                            start=(hb == 0), stop=(hb == HB - 1))
                                    ins.then_inc(sem["pemm"], 1)

                        @blk.vector
                        def _(v):
                            for idx in range(32):
                                dc, tt = divmod(idx, 2)
                                v.wait_ge(sem["pemm"], b["pemm"] + idx + 1)
                                v.wait_ge(sem["ld"], b["ld"] + 16 * (dc + 1))
                                if idx >= 4:
                                    v.wait_ge(sem["st"], b["st"] + 16 * (idx - 3))
                                v.tensor_add(ostg[idx % 4][:, :], pso[idx % 2][:, :],
                                             xstg[dc % 2][:, tt * 512:(tt + 1) * 512]
                                             ).then_inc(sem["dv"], 1)
                    cnt["wd"] += 16 * 32
                    cnt["ld"] += 16 * NCH
                    cnt["pemm"] += 32
                    cnt["dv"] += 32
                    cnt["st"] += 16 * 32

        LN("L1", h_xown)

        with ExitStack() as s1ctx:
            qo = sba(s1ctx, "qo", [128, NH * TOK], BF16)

            b = dict(cnt)
            with ExitStack() as rctx:
                wqp = sba(rctx, "wqp", [128, NCH * D], BF16)
                cab = sba(rctx, "cab", [128, D], F32)
                cbb = sba(rctx, "cbb", [128, D], F32)
                wsA = [sba(rctx, f"wsA{i}", [128, D], F32) for i in range(2)]
                wsB = [sba(rctx, f"wsB{i}", [128, D], F32) for i in range(2)]
                tmpa = sba(rctx, "tmpa", [128, 1024], F32)
                tmpb = sba(rctx, "tmpb", [128, 1024], F32)
                with nc.Block() as blk:
                    @blk.sync
                    def _(sy):
                        sy.dma_start(out=cab[:, :], in_=h_cA.ap()).then_inc(sem["ld"], 16)
                        sy.dma_start(out=cbb[:, :], in_=h_cB.ap()).then_inc(sem["ld"], 16)
                        for ch in range(NCH):
                            if ch >= 2:
                                sy.wait_ge(sem["dv"], b["dv"] + ch - 1)
                            sy.dma_start(out=wsA[ch % 2][:, :],
                                         in_=h_wqA.ap()[ch * 128:(ch + 1) * 128, :]
                                         ).then_inc(sem["ld"], 16)
                            sy.dma_start(out=wsB[ch % 2][:, :],
                                         in_=h_wqB.ap()[ch * 128:(ch + 1) * 128, :]
                                         ).then_inc(sem["ld"], 16)

                    @blk.vector
                    def _(v):
                        for ch in range(NCH):
                            v.wait_ge(sem["ld"], b["ld"] + 32 + 32 * (ch + 1))
                            for hh in range(2):
                                sl = slice(hh * 1024, (hh + 1) * 1024)
                                v.tensor_mul(tmpa[:, :], wsA[ch % 2][:, sl], cab[:, sl])
                                v.tensor_mul(tmpb[:, :], wsB[ch % 2][:, sl], cbb[:, sl])
                                ins = v.tensor_add(wqp[:, ch * D + hh * 1024:
                                                       ch * D + (hh + 1) * 1024],
                                                   tmpa[:, :], tmpb[:, :])
                            ins.then_inc(sem["dv"], 1)
                cnt["ld"] += 16 * 34
                cnt["dv"] += 16

                b = dict(cnt)
                psq = [rctx.enter_context(
                    nc.psum_tensor(f"psQ{i}", [128, 512], F32)) for i in range(2)]
                with nc.Block() as blk:
                    @blk.tensor
                    def _(t):
                        t.wait_ge(sem["dv"], b["dv"])
                        for h in range(NH):
                            for tt in range(2):
                                gi = h * 2 + tt
                                if gi >= 2:
                                    t.wait_ge(sem["ac"], b["ac"] + gi - 1)
                                ins = None
                                for ch in range(NCH):
                                    ins = t.matmul(
                                        psq[gi % 2][:, :],
                                        wqp[:, ch * D + h * 128:ch * D + (h + 1) * 128],
                                        xn[:, ch * TOK + tt * 512:ch * TOK + (tt + 1) * 512],
                                        start=(ch == 0), stop=(ch == NCH - 1))
                                ins.then_inc(sem["pemm"], 1)

                    @blk.scalar
                    def _(s):
                        for gi in range(32):
                            h, tt = divmod(gi, 2)
                            s.wait_ge(sem["pemm"], b["pemm"] + gi + 1)
                            s.activation(out=qo[:, h * TOK + tt * 512:h * TOK + (tt + 1) * 512],
                                         in_=psq[gi % 2][:, :],
                                         func=AF.Copy).then_inc(sem["ac"], 1)
                cnt["pemm"] += 32
                cnt["ac"] += 32

            if debug:
                with nc.Block() as blk:
                    @blk.sync
                    def _(sy):
                        sy.dma_start(out=h_dxn.ap(), in_=xn[:, :]).then_inc(sem["st"], 16)
                        sy.dma_start(out=h_dq.ap(), in_=qo[:, :]).then_inc(sem["st"], 16)
                        sy.wait_ge(sem["st"], cnt["st"] + 32)
                cnt["st"] += 32

            with ExitStack() as actx:
                kfm = sba(actx, "kfm", [128, OB * TSEQ], BF16)
                vtm = sba(actx, "vtm", [128, OB * TSEQ], BF16)
                mskA = sba(actx, "mskA_s", [128, 1024], F32)
                mskB = sba(actx, "mskB_s", [128, 2048], F32)
                wkp_sb = sba(actx, "wkp_s", [128, NCH * DV], BF16)
                wvt_sb = sba(actx, "wvt_s", [128, NCH * DV], BF16)

                b = dict(cnt)
                NT = OB * 16
                with ExitStack() as pctx:
                    pstg = [sba(pctx, f"pstg{i}", [128, D], F32) for i in range(2)]
                    xnt = [sba(pctx, f"xnt{i}", [128, D], BF16) for i in range(2)]
                    xfmt = [sba(pctx, f"xfmt{i}", [128, NCH * 128], BF16) for i in range(2)]
                    sums = sba(pctx, "sums", [128, 8], F32)
                    junkb = sba(pctx, "junkb", [128, D], BF16)
                    tp = [pctx.enter_context(
                        nc.psum_tensor(f"psTp{i}", [128, 128], BF16)) for i in range(2)]
                    psk = pctx.enter_context(nc.psum_tensor("psK", [128, 128], F32))
                    psv = pctx.enter_context(nc.psum_tensor("psV", [128, 128], F32))
                    with nc.Block() as blk:
                        @blk.sync
                        def _(sy):
                            sy.dma_start(out=mskA[:, :], in_=h_mskA.ap()).then_inc(sem["ld"], 16)
                            sy.dma_start(out=mskB[:, :], in_=h_mskB.ap()).then_inc(sem["ld"], 16)
                            for i in range(NT):
                                if i >= 2:
                                    sy.wait_ge(sem["ap"], b["ap"] + i - 1)
                                sy.dma_start(out=pstg[i % 2][:, :],
                                             in_=h_xpre.ap()[i * 128:(i + 1) * 128, :]
                                             ).then_inc(sem["ld"], 16)

                        @blk.gpsimd
                        def _(g):
                            for dsb, hh in ((wkp_sb, h_wk), (wvt_sb, h_wv)):
                                g.dma_start(out=dsb[:, :].rearrange("p (c w) -> p c w", c=NCH),
                                            in_=hh.ap().rearrange("(c p) w -> p c w", p=128)
                                            ).then_inc(sem["wd"], 16)
                            g.wait_ge(sem["wd"], b["wd"] + 32)

                        @blk.vector
                        def _(v):
                            for i in range(NT):
                                p = i % 2
                                v.wait_ge(sem["ld"], b["ld"] + 32 + 16 * (i + 1))
                                v.tensor_scalar(out=junkb[:, :], in0=pstg[p][:, :],
                                                scalar1=1.0, scalar2=0.0, op0=ALU.mult,
                                                op1=ALU.add, accum_out=sums[:, 0:1])
                                v.scalar_tensor_tensor(out=junkb[:, :], in0=pstg[p][:, :],
                                                       scalar=1.0, in1=pstg[p][:, :],
                                                       op0=ALU.mult, op1=ALU.mult,
                                                       accum_out=sums[:, 1:2])
                                v.tensor_scalar_mul(sums[:, 2:3], sums[:, 0:1], 1.0 / D)
                                v.tensor_scalar_mul(sums[:, 4:5], sums[:, 1:2], 1.0 / D)
                                v.tensor_mul(sums[:, 3:4], sums[:, 2:3], sums[:, 2:3])
                                v.tensor_sub(sums[:, 5:6], sums[:, 4:5],
                                             sums[:, 3:4]).then_inc(sem["dv"], 1)
                                v.wait_ge(sem["ac"], b["ac"] + 17 * i + 1)
                                v.reciprocal(sums[:, 6:7],
                                             sums[:, 7:8]).then_inc(sem["rcp"], 1)
                                v.wait_ge(sem["rcp"], b["rcp"] + i + 1)
                                v.tensor_scalar(out=xnt[p][:, :], in0=pstg[p][:, :],
                                                scalar1=sums[:, 2:3], scalar2=sums[:, 6:7],
                                                op0=ALU.subtract, op1=ALU.mult
                                                ).then_inc(sem["ap"], 1)
                                ob, tb = divmod(i, 16)
                                v.wait_ge(sem["pemm"], b["pemm"] + 2 * i + 1)
                                v.tensor_copy(
                                    out=kfm[:, ob * TSEQ + tb * 128:ob * TSEQ + (tb + 1) * 128],
                                    in_=psk[:, :]).then_inc(sem["ac2"], 1)
                                v.wait_ge(sem["pemm"], b["pemm"] + 2 * i + 2)
                                v.tensor_copy(
                                    out=vtm[:, (ob * 16 + tb) * 128:(ob * 16 + tb + 1) * 128],
                                    in_=psv[:, :]).then_inc(sem["ac2"], 1)

                        @blk.scalar
                        def _(s):
                            for i in range(NT):
                                s.wait_ge(sem["dv"], b["dv"] + i + 1)
                                s.activation(out=sums[:, 7:8], in_=sums[:, 5:6],
                                             func=AF.Sqrt,
                                             bias=epsln[:, 0:1]).then_inc(sem["ac"], 1)
                                if i >= 2:
                                    s.wait_ge(sem["pemm"], b["pemm"] + 2 * (i - 1))
                                for ch in range(NCH):
                                    s.wait_ge(sem["T"], b["T"] + i * NCH + ch + 1)
                                    s.activation(out=xfmt[i % 2][:, ch * 128:(ch + 1) * 128],
                                                 in_=tp[(i * NCH + ch) % 2][:, :],
                                                 func=AF.Copy).then_inc(sem["ac"], 1)

                        @blk.tensor
                        def _(t):
                            for i in range(NT):
                                t.wait_ge(sem["ap"], b["ap"] + i + 1)
                                for ch in range(NCH):
                                    gT = i * NCH + ch
                                    if gT >= 2:
                                        gp = gT - 2
                                        ip, chp = divmod(gp, NCH)
                                        t.wait_ge(sem["ac"], b["ac"] + 17 * ip + chp + 2)
                                    t.transpose(tp[gT % 2][:, :],
                                                xnt[i % 2][:, ch * 128:(ch + 1) * 128],
                                                ident[:, :]).then_inc(sem["T"], 1)
                                t.wait_ge(sem["ac"], b["ac"] + 17 * i + 17)
                                if i >= 1:
                                    t.wait_ge(sem["ac2"], b["ac2"] + 2 * (i - 1) + 1)
                                ins = None
                                for ch in range(NCH):
                                    ins = t.matmul(psk[:, :], wkp_sb[:, ch * DV:(ch + 1) * DV],
                                                   xfmt[i % 2][:, ch * 128:(ch + 1) * 128],
                                                   start=(ch == 0), stop=(ch == NCH - 1))
                                ins.then_inc(sem["pemm"], 1)
                                if i >= 1:
                                    t.wait_ge(sem["ac2"], b["ac2"] + 2 * (i - 1) + 2)
                                ins = None
                                for ch in range(NCH):
                                    ins = t.matmul(psv[:, :],
                                                   xfmt[i % 2][:, ch * 128:(ch + 1) * 128],
                                                   wvt_sb[:, ch * DV:(ch + 1) * DV],
                                                   start=(ch == 0), stop=(ch == NCH - 1))
                                ins.then_inc(sem["pemm"], 1)
                    cnt["ld"] += 16 * (NT + 2)
                    cnt["rcp"] += NT
                    cnt["wd"] += 16 * 2
                    cnt["dv"] += NT
                    cnt["ap"] += NT
                    cnt["ac"] += NT * (NCH + 1)
                    cnt["T"] += NT * NCH
                    cnt["pemm"] += 2 * NT
                    cnt["ac2"] += 2 * NT

                if debug:
                    with nc.Block() as blk:
                        @blk.sync
                        def _(sy):
                            sy.dma_start(out=h_dk.ap(), in_=kfm[:, :]).then_inc(sem["st"], 16)
                            sy.dma_start(out=h_dv.ap(), in_=vtm[:, :]).then_inc(sem["st"], 16)
                            sy.wait_ge(sem["st"], cnt["st"] + 32)
                    cnt["st"] += 32

                combos = [(ob, h, bk) for ob in range(4) for h in range(NH)
                          for bk in (0, 1)]
                NKT = [8, 16]

                def nkt(i):
                    return NKT[combos[i][2]]

                def nseg(i):
                    return nkt(i) // 4

                cumseg = [int(x) for x in np.cumsum([0] + [nseg(i) for i in range(128)])]
                cumT = [int(x) for x in np.cumsum([0] + [nkt(i) for i in range(128)])]
                b = dict(cnt)
                with ExitStack() as atx:
                    ssb = [sba(atx, f"ssb{i}", [128, 2048], F32) for i in range(2)]
                    pbuf = [sba(atx, f"pbuf{i}", [128, 2048], BF16) for i in range(2)]
                    ptsb = [sba(atx, f"ptsb{i}", [128, 2048], BF16) for i in range(2)]
                    nm = sba(atx, "nm", [128, 2], F32)
                    rsum = sba(atx, "rsum", [128, 2], F32)
                    rinv = sba(atx, "rinv", [128, 2], F32)
                    ps_sc = atx.enter_context(nc.psum_tensor("psSC", [128, 2048], F32))
                    pspt = [atx.enter_context(
                        nc.psum_tensor(f"psPT{i}", [128, 128], BF16)) for i in range(2)]
                    ps_o = atx.enter_context(nc.psum_tensor("psO", [128, 128], F32))
                    with nc.Block() as blk:
                        @blk.tensor
                        def _(t):
                            for i in range(129):
                                if i < 128:
                                    ob, h, bk = combos[i]
                                    qcol = h * TOK + bk * 512 + ob * 128
                                    for g in range(nseg(i)):
                                        gseg = cumseg[i] + g
                                        if gseg >= 4:
                                            t.wait_ge(sem["ev"], b["ev"] + gseg - 3)
                                        t.matmul(ps_sc[:, (gseg % 4) * 512:(gseg % 4 + 1) * 512],
                                                 qo[:, qcol:qcol + 128],
                                                 kfm[:, ob * TSEQ + g * 512:
                                                     ob * TSEQ + (g + 1) * 512],
                                                 start=True, stop=True).then_inc(sem["pe"], 1)
                                if i >= 1:
                                    j = i - 1
                                    ob, h, bk = combos[j]
                                    t.wait_ge(sem["nrm"], b["nrm"] + j + 1)
                                    for kt in range(nkt(j)):
                                        gT = cumT[j] + kt
                                        if gT >= 2:
                                            t.wait_ge(sem["ptv"], b["ptv"] + gT - 1)
                                        t.transpose(pspt[gT % 2][:, :],
                                                    pbuf[j % 2][:, kt * 128:(kt + 1) * 128],
                                                    ident[:, :]).then_inc(sem["T"], 1)
                                    if j >= 1:
                                        t.wait_ge(sem["oev"], b["oev"] + j)
                                    ins = None
                                    for kt in range(nkt(j)):
                                        t.wait_ge(sem["ptv"], b["ptv"] + cumT[j] + kt + 1)
                                        ins = t.matmul(
                                            ps_o[:, :],
                                            vtm[:, (ob * 16 + kt) * 128:(ob * 16 + kt + 1) * 128],
                                            ptsb[j % 2][:, kt * 128:(kt + 1) * 128],
                                            start=(kt == 0), stop=(kt == nkt(j) - 1))
                                    ins.then_inc(sem["pv"], 1)

                        @blk.scalar
                        def _(s):
                            for i in range(129):
                                if i < 128:
                                    if i >= 2:
                                        s.wait_ge(sem["mx"], b["mx"] + i - 1)
                                        s.wait_ge(sem["T"], b["T"] + cumT[i - 1])
                                        s.wait_ge(sem["nrm"], b["nrm"] + i - 1)
                                    for g in range(nseg(i)):
                                        gseg = cumseg[i] + g
                                        s.wait_ge(sem["pe"], b["pe"] + gseg + 1)
                                        s.activation(
                                            out=ssb[i % 2][:, g * 512:(g + 1) * 512],
                                            in_=ps_sc[:, (gseg % 4) * 512:(gseg % 4 + 1) * 512],
                                            func=AF.Copy, scale=SCALE_A).then_inc(sem["ev"], 1)
                                    w = nkt(i) * 128
                                    s.wait_ge(sem["mx"], b["mx"] + i + 1)
                                    s.activation(out=pbuf[i % 2][:, 0:w],
                                                 in_=ssb[i % 2][:, 0:w], func=AF.Exp,
                                                 bias=nm[:, i % 2:i % 2 + 1],
                                                 accum_out=rsum[:, i % 2:i % 2 + 1]
                                                 ).then_inc(sem["exp"], 1)
                                if i >= 1:
                                    j = i - 1
                                    ob, h, bk = combos[j]
                                    s.wait_ge(sem["pv"], b["pv"] + j)
                                    for kt in range(nkt(j)):
                                        s.wait_ge(sem["T"], b["T"] + cumT[j] + kt + 1)
                                        s.activation(out=ptsb[j % 2][:, kt * 128:(kt + 1) * 128],
                                                     in_=pspt[(cumT[j] + kt) % 2][:, :],
                                                     func=AF.Copy).then_inc(sem["ptv"], 1)
                                    qcol = h * TOK + bk * 512 + ob * 128
                                    s.wait_ge(sem["pv"], b["pv"] + j + 1)
                                    s.activation(out=qo[:, qcol:qcol + 128], in_=ps_o[:, :],
                                                 func=AF.Copy).then_inc(sem["oev"], 1)

                        @blk.vector
                        def _(v):
                            for i in range(129):
                                if i < 128:
                                    w = nkt(i) * 128
                                    msk = mskA if combos[i][2] == 0 else mskB
                                    v.wait_ge(sem["ev"], b["ev"] + cumseg[i] + nseg(i))
                                    if i >= 2:
                                        v.wait_ge(sem["exp"], b["exp"] + i - 1)
                                    v.tensor_add(ssb[i % 2][:, 0:w], ssb[i % 2][:, 0:w],
                                                 msk[:, 0:w])
                                    v.reduce_max(nm[:, i % 2:i % 2 + 1], ssb[i % 2][:, 0:w],
                                                 axis=AX.X, negate=True).then_inc(sem["mx"], 1)
                                if i >= 1:
                                    j = i - 1
                                    w = nkt(j) * 128
                                    v.wait_ge(sem["exp"], b["exp"] + j + 1)
                                    v.reciprocal(rinv[:, j % 2:j % 2 + 1],
                                                 rsum[:, j % 2:j % 2 + 1]
                                                 ).then_inc(sem["rcp"], 1)
                                    v.wait_ge(sem["rcp"], b["rcp"] + j + 1)
                                    v.tensor_scalar_mul(pbuf[j % 2][:, 0:w],
                                                        pbuf[j % 2][:, 0:w],
                                                        rinv[:, j % 2:j % 2 + 1]
                                                        ).then_inc(sem["nrm"], 1)
                    cnt["pe"] += int(cumseg[128])
                    cnt["ev"] += int(cumseg[128])
                    cnt["exp"] += 128
                    cnt["mx"] += 128
                    cnt["nrm"] += 128
                    cnt["T"] += int(cumT[128])
                    cnt["ptv"] += int(cumT[128])
                    cnt["pv"] += 128
                    cnt["oev"] += 128
                    cnt["rcp"] += 128

                    if debug:
                        with nc.Block() as blk:
                            @blk.sync
                            def _(sy):
                                sy.dma_start(out=h_dqo.ap(), in_=qo[:, :]).then_inc(sem["st"], 16)
                                sy.dma_start(out=h_dss.ap()[:, 0:2048], in_=ssb[0][:, :]).then_inc(sem["st"], 16)
                                sy.dma_start(out=h_dss.ap()[:, 2048:4096], in_=ssb[1][:, :]).then_inc(sem["st"], 16)
                                sy.dma_start(out=h_dpb.ap()[:, 0:2048], in_=pbuf[0][:, :]).then_inc(sem["st"], 16)
                                sy.dma_start(out=h_dpb.ap()[:, 2048:4096], in_=pbuf[1][:, :]).then_inc(sem["st"], 16)
                                sy.dma_start(out=h_dpt.ap()[:, 0:2048], in_=ptsb[0][:, :]).then_inc(sem["st"], 16)
                                sy.dma_start(out=h_dpt.ap()[:, 2048:4096], in_=ptsb[1][:, :]).then_inc(sem["st"], 16)
                                sy.dma_start(out=h_dnm.ap()[:, 0:2], in_=nm[:, :]).then_inc(sem["st"], 16)
                                sy.dma_start(out=h_dnm.ap()[:, 2:4], in_=rsum[:, :]).then_inc(sem["st"], 16)
                                sy.dma_start(out=h_dnm.ap()[:, 4:6], in_=rinv[:, :]).then_inc(sem["st"], 16)
                                sy.wait_ge(sem["st"], cnt["st"] + 160)
                        cnt["st"] += 160

            b = dict(cnt)
            with ExitStack() as wctx:
                woab = [sba(wctx, f"woab{i}", [128, NCH * 128], BF16) for i in range(2)]
                xstg = [sba(wctx, f"xsw{i}", [128, TOK], F32) for i in range(2)]
                ostg = [sba(wctx, f"osw{i}", [128, 512], F32) for i in range(4)]
                psw = [wctx.enter_context(
                    nc.psum_tensor(f"psW{i}", [128, 512], F32)) for i in range(2)]
                with nc.Block() as blk:
                    @blk.gpsimd
                    def _(g):
                        for oc in range(NCH):
                            if oc >= 2:
                                g.wait_ge(sem["pemm"], b["pemm"] + 2 * (oc - 1))
                            g.dma_start(
                                out=woab[oc % 2][:, :].rearrange("p (c o) -> p c o", c=NCH),
                                in_=h_woa.ap()[:, oc * 128:(oc + 1) * 128]
                                .rearrange("(c p) o -> p c o", p=128)).then_inc(sem["wd"], 16)

                    @blk.sync
                    def _(sy):
                        for oc in range(NCH):
                            if oc >= 2:
                                sy.wait_ge(sem["dv"], b["dv"] + 2 * (oc - 1))
                            sy.dma_start(out=xstg[oc % 2][:, :],
                                         in_=h_xown.ap()[oc * 128:(oc + 1) * 128, :]
                                         ).then_inc(sem["ld"], 16)

                    @blk.scalar
                    def _(sy):
                        for idx in range(32):
                            oc, tt = divmod(idx, 2)
                            sy.wait_ge(sem["dv"], b["dv"] + idx + 1)
                            sy.dma_start(out=h_xs1.ap()[oc * 128:(oc + 1) * 128,
                                                        tt * 512:(tt + 1) * 512],
                                         in_=ostg[idx % 4][:, :]).then_inc(sem["st"], 16)
                        sy.wait_ge(sem["st"], b["st"] + 16 * 32)

                    @blk.tensor
                    def _(t):
                        for oc in range(NCH):
                            t.wait_ge(sem["wd"], b["wd"] + 16 * (oc + 1))
                            for tt in range(2):
                                idx = oc * 2 + tt
                                if idx >= 2:
                                    t.wait_ge(sem["dv"], b["dv"] + idx - 1)
                                ins = None
                                for ch in range(NCH):
                                    ins = t.matmul(
                                        psw[idx % 2][:, :],
                                        woab[oc % 2][:, ch * 128:(ch + 1) * 128],
                                        qo[:, ch * TOK + tt * 512:ch * TOK + (tt + 1) * 512],
                                        start=(ch == 0), stop=(ch == NCH - 1))
                                ins.then_inc(sem["pemm"], 1)

                    @blk.vector
                    def _(v):
                        for idx in range(32):
                            oc, tt = divmod(idx, 2)
                            v.wait_ge(sem["pemm"], b["pemm"] + idx + 1)
                            v.wait_ge(sem["ld"], b["ld"] + 16 * (oc + 1))
                            if idx >= 4:
                                v.wait_ge(sem["st"], b["st"] + 16 * (idx - 3))
                            v.tensor_add(ostg[idx % 4][:, :], psw[idx % 2][:, :],
                                         xstg[oc % 2][:, tt * 512:(tt + 1) * 512]
                                         ).then_inc(sem["dv"], 1)
                cnt["wd"] += 16 * NCH
                cnt["ld"] += 16 * NCH
                cnt["pemm"] += 32
                cnt["dv"] += 32
                cnt["st"] += 16 * 32

        LN("L2", h_xs1)
        FFN("f1", h_xs1, h_xs2)

        LN("L3", h_xs2)
        _attn_b(nc, sem, cnt, sba, xn, h_conv, wbb_sb, gcol, onesrb, ident,
                h_wqb, h_wkb, h_wvb, h_wob, h_xs2, h_xs3, epsln, eps12, eps12s)

        LN("L4", h_xs3)
        FFN("f2", h_xs3, h_y)

    return nc


def _attn_b(nc, sem, cnt, sba, xn, h_conv, wbb_sb, gcol, onesrb, ident,
            h_wqb, h_wkb, h_wvb, h_wob, h_src, h_dst, epsln, eps12, eps12s):
    from contextlib import ExitStack
    TERMS = {0: [(0, 3)], 1: [(0, 2), (1, 3)], 2: [(0, 1), (1, 2), (2, 3)],
             3: [(0, 0), (1, 1), (2, 2), (3, 3)]}
    CP = [(1, 0), (2, 0), (2, 1), (3, 0), (3, 1), (3, 2)]
    APRS = [(t, i) for t in range(4) for i in range(t + 1)]
    with ExitStack() as bctx:
        ofm = [sba(bctx, f"ofm{b_}", [128, NH * 512], BF16) for b_ in range(2)]
        for bk in range(2):
            with ExitStack() as dctx:
                yq = sba(dctx, f"yq{bk}", [128, 4 * D], BF16)
                yk = sba(dctx, f"yk{bk}", [128, 4 * D], BF16)
                yv = sba(dctx, f"yv{bk}", [128, 4 * D], BF16)

                with ExitStack() as pctx:
                    raw = sba(pctx, f"raw{bk}", [128, 4 * D], BF16)
                    wsl = [sba(pctx, f"wsl{bk}{i}", [128, NCH * 512], BF16)
                           for i in range(2)]
                    wtap = sba(pctx, f"wtap{bk}", [128, 4 * D], BF16)
                    ctmp = sba(pctx, f"ctmp{bk}", [128, D], BF16)
                    crow = [sba(pctx, f"crow{bk}{i}", [1, D], BF16) for i in range(2)]
                    psp = [pctx.enter_context(
                        nc.psum_tensor(f"psP{bk}{i}", [128, 512], F32)) for i in range(4)]
                    psc = pctx.enter_context(nc.psum_tensor(f"psC{bk}", [128, 512], F32))
                    for pi, (hw, ybuf) in enumerate(
                            ((h_wqb, yq), (h_wkb, yk), (h_wvb, yv))):
                        b = dict(cnt)
                        with nc.Block() as blk:
                            @blk.gpsimd
                            def _(g):
                                for ct in range(4):
                                    if ct >= 2:
                                        g.wait_ge(sem["pemm"], b["pemm"] + 4 * (ct - 1))
                                    g.dma_start(
                                        out=wsl[ct % 2][:, :]
                                        .rearrange("p (c o) -> p c o", c=NCH),
                                        in_=hw.ap()[:, ct * 512:(ct + 1) * 512]
                                        .rearrange("(c p) o -> p c o", p=128)
                                        ).then_inc(sem["wd"], 16)
                                for tap in range(4):
                                    if tap >= 2:
                                        g.wait_ge(sem["pe"], b["pe"] + 4 * (tap - 1))
                                    g.dma_start(
                                        out=crow[tap % 2][:, :],
                                        in_=h_conv.ap()[0:1, (pi * 4 + tap) * D:
                                                        (pi * 4 + tap + 1) * D]
                                        ).then_inc(sem["wd"], 16)

                            @blk.tensor
                            def _(t):
                                for ct in range(4):
                                    t.wait_ge(sem["wd"], b["wd"] + 16 * (ct + 1))
                                    for s in range(4):
                                        gi = ct * 4 + s
                                        if gi >= 4:
                                            t.wait_ge(sem["ac"], b["ac"] + gi - 3)
                                        ins = None
                                        for ch in range(NCH):
                                            ins = t.matmul(
                                                psp[gi % 4][:, :],
                                                xn[:, ch * TOK + bk * 512 + s * 128:
                                                   ch * TOK + bk * 512 + (s + 1) * 128],
                                                wsl[ct % 2][:, ch * 512:(ch + 1) * 512],
                                                start=(ch == 0), stop=(ch == NCH - 1))
                                        ins.then_inc(sem["pemm"], 1)
                                for tap in range(4):
                                    t.wait_ge(sem["wd"], b["wd"] + 16 * (4 + tap + 1))
                                    for sg in range(4):
                                        gi = 16 + tap * 4 + sg
                                        if gi >= 17:
                                            t.wait_ge(sem["ac"], b["ac"] + gi)
                                        t.matmul(psc[:, :], onesrb[:, :],
                                                 crow[tap % 2][0:1, sg * 512:(sg + 1) * 512],
                                                 start=True, stop=True).then_inc(sem["pe"], 1)

                            @blk.scalar
                            def _(s_):
                                for gi in range(16):
                                    ct, s = divmod(gi, 4)
                                    s_.wait_ge(sem["pemm"], b["pemm"] + gi + 1)
                                    s_.activation(
                                        out=raw[:, s * D + ct * 512:s * D + (ct + 1) * 512],
                                        in_=psp[gi % 4][:, :],
                                        func=AF.Copy).then_inc(sem["ac"], 1)
                                for gi in range(16):
                                    tap, sg = divmod(gi, 4)
                                    s_.wait_ge(sem["pe"], b["pe"] + gi + 1)
                                    s_.activation(
                                        out=wtap[:, tap * D + sg * 512:tap * D + (sg + 1) * 512],
                                        in_=psc[:, :], func=AF.Copy).then_inc(sem["ac"], 1)

                            @blk.vector
                            def _(v):
                                v.wait_ge(sem["ac"], b["ac"] + 32)
                                ins = None
                                for s in range(4):
                                    terms = TERMS[s]
                                    (s0, tp0) = terms[0]
                                    ysl = ybuf[:, s * D:(s + 1) * D]
                                    ins = v.tensor_mul(ysl, raw[:, s0 * D:(s0 + 1) * D],
                                                       wtap[:, tp0 * D:(tp0 + 1) * D])
                                    for (sj, tpj) in terms[1:]:
                                        v.tensor_mul(ctmp[:, :], raw[:, sj * D:(sj + 1) * D],
                                                     wtap[:, tpj * D:(tpj + 1) * D])
                                        ins = v.tensor_add(ysl, ysl, ctmp[:, :])
                                ins.then_inc(sem["dv"], 1)

                            @blk.scalar
                            def _(s_):
                                s_.wait_ge(sem["dv"], b["dv"] + 1)
                                for s in range(4):
                                    s_.activation(out=ybuf[:, s * D:(s + 1) * D],
                                                  in_=ybuf[:, s * D:(s + 1) * D],
                                                  func=AF.Silu).then_inc(sem["ac"], 1)
                        cnt["wd"] += 16 * 8
                        cnt["pemm"] += 16
                        cnt["pe"] += 16
                        cnt["ac"] += 36
                        cnt["dv"] += 1

                b = dict(cnt)
                with ExitStack() as rctx:
                    ubuf = sba(rctx, f"ubuf{bk}", [128, 4 * D], F32)
                    obuf = sba(rctx, f"obuf{bk}", [128, 4 * D], F32)
                    beta = sba(rctx, f"beta{bk}", [128, 4 * NH], F32)
                    dots = sba(rctx, f"dots{bk}", [128, 16 * NH], F32)
                    lsum = sba(rctx, f"lsum{bk}", [128, 8 * NH], F32)
                    scr = sba(rctx, f"scr{bk}", [128, D], BF16)
                    onb = sba(rctx, f"onb{bk}", [128, 2 * 128], BF16)
                    psb_ = rctx.enter_context(nc.psum_tensor(f"psBt{bk}", [128, NH], F32))
                    tpp = [rctx.enter_context(
                        nc.psum_tensor(f"psTo{bk}{i}", [128, 128], BF16)) for i in range(2)]
                    with nc.Block() as blk:
                        @blk.tensor
                        def _(t):
                            for s in range(4):
                                if s >= 1:
                                    t.wait_ge(sem["ac"], b["ac"] + s)
                                ins = None
                                for ch in range(NCH):
                                    ins = t.matmul(
                                        psb_[:, :],
                                        xn[:, ch * TOK + bk * 512 + s * 128:
                                           ch * TOK + bk * 512 + (s + 1) * 128],
                                        wbb_sb[:, ch * NH:(ch + 1) * NH],
                                        start=(ch == 0), stop=(ch == NCH - 1))
                                ins.then_inc(sem["pemm"], 1)
                            for idx in range(64):
                                t.wait_ge(sem["dv"], b["dv"] + 2 + idx)
                                if idx >= 2:
                                    t.wait_ge(sem["ac"], b["ac"] + 6 + idx)
                                t.transpose(tpp[idx % 2][:, :],
                                            onb[:, (idx % 2) * 128:(idx % 2 + 1) * 128],
                                            ident[:, :]).then_inc(sem["T"], 1)

                        @blk.scalar
                        def _(s_):
                            for s in range(4):
                                s_.wait_ge(sem["pemm"], b["pemm"] + s + 1)
                                s_.activation(out=beta[:, s * NH:(s + 1) * NH],
                                              in_=psb_[:, :],
                                              func=AF.Sigmoid).then_inc(sem["ac"], 1)
                            s_.wait_ge(sem["dv"], b["dv"] + 1)
                            s_.activation(out=lsum[:, 0:64], in_=lsum[:, 0:64],
                                          func=AF.Sqrt, scale=128.0,
                                          bias=eps12s[:, 0:1]).then_inc(sem["ac"], 1)
                            s_.activation(out=lsum[:, 64:128], in_=lsum[:, 64:128],
                                          func=AF.Sqrt,
                                          bias=eps12[:, 0:1]).then_inc(sem["ac"], 1)
                            s_.wait_ge(sem["mx"], b["mx"] + 1)
                            s_.activation(out=lsum[:, 0:64], in_=lsum[:, 0:64],
                                          func=AF.Sqrt, scale=1.0 / 128.0,
                                          bias=epsln[:, 0:1]).then_inc(sem["ac"], 1)
                            for idx in range(64):
                                s, h = divmod(idx, 16)
                                s_.wait_ge(sem["T"], b["T"] + idx + 1)
                                s_.activation(
                                    out=ofm[bk][:, h * 512 + s * 128:h * 512 + (s + 1) * 128],
                                    in_=tpp[idx % 2][:, :],
                                    func=AF.Copy).then_inc(sem["ac"], 1)

                        @blk.vector
                        def _(v):
                            ins = None
                            for (buf, col0) in ((yq, 0), (yk, 64)):
                                for s in range(4):
                                    for h in range(NH):
                                        sl = buf[:, s * D + h * 128:s * D + (h + 1) * 128]
                                        ins = v.scalar_tensor_tensor(
                                            out=scr[:, h * 128:(h + 1) * 128], in0=sl,
                                            scalar=1.0, in1=sl, op0=ALU.mult, op1=ALU.mult,
                                            accum_out=lsum[:, col0 + s * 16 + h:
                                                           col0 + s * 16 + h + 1])
                            ins.then_inc(sem["dv"], 1)
                            v.wait_ge(sem["ac"], b["ac"] + 6)
                            v.reciprocal(lsum[:, 0:64], lsum[:, 0:64])
                            v.reciprocal(lsum[:, 64:128],
                                         lsum[:, 64:128]).then_inc(sem["rcp"], 1)
                            v.wait_ge(sem["rcp"], b["rcp"] + 1)
                            for (buf, col0) in ((yq, 0), (yk, 64)):
                                for s in range(4):
                                    for h in range(NH):
                                        sl = buf[:, s * D + h * 128:s * D + (h + 1) * 128]
                                        v.tensor_scalar_mul(
                                            sl, sl,
                                            lsum[:, col0 + s * 16 + h:col0 + s * 16 + h + 1])
                            for p_, (tt, ii) in enumerate(CP):
                                for h in range(NH):
                                    v.scalar_tensor_tensor(
                                        out=scr[:, h * 128:(h + 1) * 128],
                                        in0=yk[:, tt * D + h * 128:tt * D + (h + 1) * 128],
                                        scalar=1.0,
                                        in1=yk[:, ii * D + h * 128:ii * D + (h + 1) * 128],
                                        op0=ALU.mult, op1=ALU.mult,
                                        accum_out=dots[:, p_ * 16 + h:p_ * 16 + h + 1]
                                        ) if not (p_ == len(CP) - 1 and h == NH - 1) else \
                                    v.scalar_tensor_tensor(
                                        out=scr[:, h * 128:(h + 1) * 128],
                                        in0=yk[:, tt * D + h * 128:tt * D + (h + 1) * 128],
                                        scalar=1.0,
                                        in1=yk[:, ii * D + h * 128:ii * D + (h + 1) * 128],
                                        op0=ALU.mult, op1=ALU.mult,
                                        accum_out=dots[:, p_ * 16 + h:p_ * 16 + h + 1]
                                        ).then_inc(sem["rcp"], 1)
                            v.wait_ge(sem["rcp"], b["rcp"] + 2)
                            v.tensor_scalar_mul(dots[:, 0:96], dots[:, 0:96], -1.0)
                            for p_, (tt, ii) in enumerate(APRS):
                                for h in range(NH):
                                    v.scalar_tensor_tensor(
                                        out=scr[:, h * 128:(h + 1) * 128],
                                        in0=yq[:, tt * D + h * 128:tt * D + (h + 1) * 128],
                                        scalar=1.0,
                                        in1=yk[:, ii * D + h * 128:ii * D + (h + 1) * 128],
                                        op0=ALU.mult, op1=ALU.mult,
                                        accum_out=dots[:, 96 + p_ * 16 + h:
                                                       96 + p_ * 16 + h + 1]
                                        ) if not (p_ == len(APRS) - 1 and h == NH - 1) else \
                                    v.scalar_tensor_tensor(
                                        out=scr[:, h * 128:(h + 1) * 128],
                                        in0=yq[:, tt * D + h * 128:tt * D + (h + 1) * 128],
                                        scalar=1.0,
                                        in1=yk[:, ii * D + h * 128:ii * D + (h + 1) * 128],
                                        op0=ALU.mult, op1=ALU.mult,
                                        accum_out=dots[:, 96 + p_ * 16 + h:
                                                       96 + p_ * 16 + h + 1]
                                        ).then_inc(sem["rcp"], 1)
                            v.wait_ge(sem["rcp"], b["rcp"] + 3)
                            v.wait_ge(sem["ac"], b["ac"] + 4)

                            def ccol(t_, i_):
                                return CP.index((t_, i_)) * 16

                            def acol(t_, i_):
                                return 96 + APRS.index((t_, i_)) * 16

                            for s in range(4):
                                if s == 0:
                                    for h in range(NH):
                                        v.tensor_scalar_mul(
                                            ubuf[:, h * 128:(h + 1) * 128],
                                            yv[:, h * 128:(h + 1) * 128],
                                            beta[:, h:h + 1])
                                else:
                                    v.tensor_copy(out=ubuf[:, s * D:(s + 1) * D],
                                                  in_=yv[:, s * D:(s + 1) * D])
                                    for i_ in range(s):
                                        pc = ccol(s, i_)
                                        for h in range(NH):
                                            v.scalar_tensor_tensor(
                                                out=ubuf[:, s * D + h * 128:
                                                         s * D + (h + 1) * 128],
                                                in0=ubuf[:, i_ * D + h * 128:
                                                         i_ * D + (h + 1) * 128],
                                                scalar=dots[:, pc + h:pc + h + 1],
                                                in1=ubuf[:, s * D + h * 128:
                                                         s * D + (h + 1) * 128],
                                                op0=ALU.mult, op1=ALU.add)
                                    for h in range(NH):
                                        v.tensor_scalar_mul(
                                            ubuf[:, s * D + h * 128:s * D + (h + 1) * 128],
                                            ubuf[:, s * D + h * 128:s * D + (h + 1) * 128],
                                            beta[:, s * 16 + h:s * 16 + h + 1])
                                for i_ in range(s + 1):
                                    pa = acol(s, i_)
                                    for h in range(NH):
                                        if i_ == 0:
                                            v.tensor_scalar_mul(
                                                obuf[:, s * D + h * 128:s * D + (h + 1) * 128],
                                                ubuf[:, h * 128:(h + 1) * 128],
                                                dots[:, pa + h:pa + h + 1])
                                        else:
                                            v.scalar_tensor_tensor(
                                                out=obuf[:, s * D + h * 128:
                                                         s * D + (h + 1) * 128],
                                                in0=ubuf[:, i_ * D + h * 128:
                                                         i_ * D + (h + 1) * 128],
                                                scalar=dots[:, pa + h:pa + h + 1],
                                                in1=obuf[:, s * D + h * 128:
                                                         s * D + (h + 1) * 128],
                                                op0=ALU.mult, op1=ALU.add)
                            ins = None
                            for s in range(4):
                                for h in range(NH):
                                    sl = obuf[:, s * D + h * 128:s * D + (h + 1) * 128]
                                    ins = v.scalar_tensor_tensor(
                                        out=scr[:, h * 128:(h + 1) * 128], in0=sl,
                                        scalar=1.0, in1=sl, op0=ALU.mult, op1=ALU.mult,
                                        accum_out=lsum[:, s * 16 + h:s * 16 + h + 1])
                            ins.then_inc(sem["mx"], 1)
                            v.wait_ge(sem["ac"], b["ac"] + 7)
                            v.reciprocal(lsum[:, 0:64],
                                         lsum[:, 0:64]).then_inc(sem["rcp"], 1)
                            v.wait_ge(sem["rcp"], b["rcp"] + 4)
                            for idx in range(64):
                                s, h = divmod(idx, 16)
                                if idx >= 2:
                                    v.wait_ge(sem["T"], b["T"] + idx - 1)
                                v.tensor_scalar_mul(
                                    onb[:, (idx % 2) * 128:(idx % 2 + 1) * 128],
                                    obuf[:, s * D + h * 128:s * D + (h + 1) * 128],
                                    lsum[:, s * 16 + h:s * 16 + h + 1]).then_inc(sem["dv"], 1)
                    cnt["pemm"] += 4
                    cnt["ac"] += 7 + 64
                    cnt["dv"] += 1 + 64
                    cnt["mx"] += 1
                    cnt["rcp"] += 4
                    cnt["T"] += 64

        b = dict(cnt)
        with ExitStack() as wctx:
            wobb = [sba(wctx, f"wobb{i}", [128, NCH * 128], BF16) for i in range(2)]
            xstg = [sba(wctx, f"xsb{i}", [128, TOK], F32) for i in range(2)]
            ostg = [sba(wctx, f"osb{i}", [128, 512], F32) for i in range(4)]
            psw = [wctx.enter_context(
                nc.psum_tensor(f"psWb{i}", [128, 512], F32)) for i in range(2)]
            with nc.Block() as blk:
                @blk.gpsimd
                def _(g):
                    for oc in range(NCH):
                        if oc >= 2:
                            g.wait_ge(sem["pemm"], b["pemm"] + 2 * (oc - 1))
                        g.dma_start(
                            out=wobb[oc % 2][:, :].rearrange("p (c o) -> p c o", c=NCH),
                            in_=h_wob.ap()[:, oc * 128:(oc + 1) * 128]
                            .rearrange("(c p) o -> p c o", p=128)).then_inc(sem["wd"], 16)

                @blk.vector
                def _(v):
                    def adds(idx):
                        oc, bk2 = divmod(idx, 2)
                        v.wait_ge(sem["pemm"], b["pemm"] + idx + 1)
                        v.wait_ge(sem["ld"], b["ld"] + 16 * (oc + 1))
                        if idx >= 4:
                            v.wait_ge(sem["st"], b["st"] + 16 * (idx - 3))
                        v.tensor_add(ostg[idx % 4][:, :], psw[idx % 2][:, :],
                                     xstg[oc % 2][:, bk2 * 512:(bk2 + 1) * 512]
                                     ).then_inc(sem["dv"], 1)

                    for oc in range(NCH):
                        v.wait_ge(sem["wd"], b["wd"] + 16 * (oc + 1))
                        v.tensor_scalar_mul(wobb[oc % 2][:, :], wobb[oc % 2][:, :],
                                            gcol[:, 0:1]).then_inc(sem["nrm"], 1)
                        if oc >= 1:
                            adds(2 * (oc - 1))
                            adds(2 * (oc - 1) + 1)
                    adds(30)
                    adds(31)

                @blk.sync
                def _(sy):
                    for oc in range(NCH):
                        if oc >= 2:
                            sy.wait_ge(sem["dv"], b["dv"] + 2 * (oc - 1))
                        sy.dma_start(out=xstg[oc % 2][:, :],
                                     in_=h_src.ap()[oc * 128:(oc + 1) * 128, :]
                                     ).then_inc(sem["ld"], 16)

                @blk.scalar
                def _(sy):
                    for idx in range(32):
                        oc, bk2 = divmod(idx, 2)
                        sy.wait_ge(sem["dv"], b["dv"] + idx + 1)
                        sy.dma_start(out=h_dst.ap()[oc * 128:(oc + 1) * 128,
                                                    bk2 * 512:(bk2 + 1) * 512],
                                     in_=ostg[idx % 4][:, :]).then_inc(sem["st"], 16)
                    sy.wait_ge(sem["st"], b["st"] + 16 * 32)

                @blk.tensor
                def _(t):
                    for oc in range(NCH):
                        t.wait_ge(sem["nrm"], b["nrm"] + oc + 1)
                        for bk2 in range(2):
                            idx = oc * 2 + bk2
                            if idx >= 2:
                                t.wait_ge(sem["dv"], b["dv"] + idx - 1)
                            ins = None
                            for ch in range(NCH):
                                ins = t.matmul(
                                    psw[idx % 2][:, :],
                                    wobb[oc % 2][:, ch * 128:(ch + 1) * 128],
                                    ofm[bk2][:, ch * 512:(ch + 1) * 512],
                                    start=(ch == 0), stop=(ch == NCH - 1))
                            ins.then_inc(sem["pemm"], 1)
            cnt["wd"] += 16 * NCH
            cnt["ld"] += 16 * NCH
            cnt["nrm"] += NCH
            cnt["pemm"] += 32
            cnt["dv"] += 32
            cnt["st"] += 16 * 32


def make_in_maps(inputs):
    x = np.asarray(inputs["L0_attn_a_hidden_states"], dtype=np.float32)
    srcA, srcB, cA, cB, pK = _rope_tables()
    WqT = np.ascontiguousarray(np.asarray(inputs["L0_attn_a_Wq"], np.float32).T)
    wqA = np.ascontiguousarray(WqT[:, srcA])
    wqB = np.ascontiguousarray(WqT[:, srcB])
    cAb = np.ascontiguousarray(np.broadcast_to(cA[None, :], (128, D)))
    cBb = np.ascontiguousarray(np.broadcast_to(cB[None, :], (128, D)))
    wkp = np.ascontiguousarray(np.asarray(inputs["L0_attn_a_Wk"], np.float32).T[:, pK])
    wvt = np.ascontiguousarray(np.asarray(inputs["L0_attn_a_Wv"], np.float32).T)
    woa = np.ascontiguousarray(np.asarray(inputs["L0_attn_a_Wo"], np.float32).T)
    wqb = np.ascontiguousarray(np.asarray(inputs["L0_attn_b_Wq"], np.float32))
    wkb = np.ascontiguousarray(np.asarray(inputs["L0_attn_b_Wk"], np.float32))
    wvb = np.ascontiguousarray(np.asarray(inputs["L0_attn_b_Wv"], np.float32))
    wbb = np.ascontiguousarray(np.asarray(inputs["L0_attn_b_Wb"], np.float32))
    wob = np.ascontiguousarray(np.asarray(inputs["L0_attn_b_Wo"], np.float32))
    convr = np.ascontiguousarray(np.concatenate(
        [np.asarray(inputs["L0_attn_b_conv_q"], np.float32).T.reshape(-1),
         np.asarray(inputs["L0_attn_b_conv_k"], np.float32).T.reshape(-1),
         np.asarray(inputs["L0_attn_b_conv_v"], np.float32).T.reshape(-1)]
    ).reshape(1, 3 * 4 * D))
    gcol_a = np.ascontiguousarray(
        np.asarray(inputs["L0_attn_b_g_norm"], np.float32).reshape(128, 1))
    w1t = np.ascontiguousarray(np.asarray(inputs["L0_ffn_W1"], np.float32).T)
    w3t = np.ascontiguousarray(np.asarray(inputs["L0_ffn_W3"], np.float32).T)
    w2t = np.ascontiguousarray(np.asarray(inputs["L0_ffn_W2"], np.float32).T)

    identity = np.eye(128, dtype=BF16NP)
    h128 = _hadamard128().astype(BF16NP)
    onesc = np.ones((128, 1), dtype=BF16NP)
    onesrf = np.ones((1, 128), dtype=np.float32)

    x_fm = np.ascontiguousarray(x.transpose(2, 0, 1))
    x_tm = np.ascontiguousarray(x.reshape(OB * TSEQ, D))

    ql = np.arange(128)
    in_maps = []
    for c in range(NCORE):
        bA, bB = c, 15 - c
        xown = np.ascontiguousarray(np.concatenate(
            [x_fm[:, :, bA * 128:(bA + 1) * 128].reshape(D, 512),
             x_fm[:, :, bB * 128:(bB + 1) * 128].reshape(D, 512)], axis=1))
        kA = np.arange(1024)
        mskA = np.where(kA[None, :] <= bA * 128 + ql[:, None], 0.0, NEG
                        ).astype(np.float32)
        kB = np.arange(2048)
        mskB = np.where(kB[None, :] <= bB * 128 + ql[:, None], 0.0, NEG
                        ).astype(np.float32)
        in_maps.append({
            "xown": xown, "xpre": x_tm, "wqA": wqA, "wqB": wqB, "cAb": cAb,
            "cBb": cBb, "wkp": wkp, "wvt": wvt, "woa": woa, "wqb": wqb,
            "wkb": wkb, "wvb": wvb, "wbb": wbb, "wob": wob, "convr": convr,
            "gcol": gcol_a, "w1t": w1t, "w3t": w3t, "w2t": w2t,
            "ident": identity, "h128": h128, "onesc": onesc, "onesrf": onesrf,
        "onesrb": np.ones((1, 128), dtype=BF16NP),
            "mskA": mskA, "mskB": mskB,
        })

    return in_maps


def assemble(res_results, key="y"):
    out = np.empty((OB, TSEQ, D), dtype=np.float32)
    for c in range(NCORE):
        y = res_results[c][key]
        y4 = y.reshape(D, 2, OB, 128)
        for bi, bblk in enumerate((c, 15 - c)):
            out[:, bblk * 128:(bblk + 1) * 128, :] = y4[:, bi].transpose(1, 2, 0)
    return out


def kernel(**inputs):
    in_maps = make_in_maps(inputs)
    nc = build_program()
    res = run_bass_kernel_spmd(nc, in_maps, core_ids=list(range(NCORE)))
    return assemble(res.results, "y")


if __name__ == "__main__":
    build_program()
    print("build ok")


# version 08
# speedup vs baseline: 1.4436x; 1.1992x over previous
import numpy as np
import ml_dtypes
from contextlib import ExitStack

import concourse.bass as bass
import concourse.mybir as mybir
from concourse.bass_utils import run_bass_kernel_spmd

F32 = mybir.dt.float32
BF16 = mybir.dt.bfloat16
AF = mybir.ActivationFunctionType
ALU = mybir.AluOpType
AX = mybir.AxisListType
BF16NP = ml_dtypes.bfloat16

D = 2048
OB = 4
TSEQ = 2048
NCH = 16
TOK = 1024
NH, DV = 16, 128
HID, HB = 8192, 64
NCORE = 8
SCALE_A = float(1.0 / np.sqrt(128.0))
S8192 = float(1.0 / np.sqrt(8192.0))
EPS_LN = 1e-5
NEG = -1e9

SEMS = ["ld", "wd", "pe", "pemm", "T", "pv", "ev", "exp", "ptv", "oev",
        "mx", "nrm", "cp", "sq", "dv", "ac", "ac2", "ap", "st", "rcp", "fz"]


def _rope_tables():
    theta = 10000.0 ** (-np.arange(32) / 32.0)
    srcA = np.zeros(D, dtype=np.int64)
    srcB = np.zeros(D, dtype=np.int64)
    cA = np.zeros(D, dtype=np.float32)
    cB = np.zeros(D, dtype=np.float32)
    for h in range(NH):
        cos, sin = np.cos(h * theta), np.sin(h * theta)
        for j in range(128):
            o = h * 128 + j
            if j < 32:
                m = j; a = 2 * m; ca, cb = cos[m], -sin[m]
            elif j < 64:
                m = j - 32; a = 2 * m; ca, cb = sin[m], cos[m]
            elif j < 96:
                m = j - 64; a = 64 + 2 * m; ca, cb = cos[m], -sin[m]
            else:
                m = j - 96; a = 64 + 2 * m; ca, cb = sin[m], cos[m]
            srcA[o], srcB[o] = h * 128 + a, h * 128 + a + 1
            cA[o], cB[o] = ca, cb
    pK = np.zeros(128, dtype=np.int64)
    for j in range(128):
        if j < 32:
            pK[j] = 2 * j
        elif j < 64:
            pK[j] = 2 * (j - 32) + 1
        elif j < 96:
            pK[j] = 64 + 2 * (j - 64)
        else:
            pK[j] = 64 + 2 * (j - 96) + 1
    return srcA, srcB, cA, cB, pK


def _hadamard128():
    i = np.arange(128)
    pc = np.zeros((128, 128), dtype=np.int64)
    for b in range(7):
        pc += ((i[:, None] >> b) & 1) * ((i[None, :] >> b) & 1)
    return np.where(pc % 2 == 0, 1.0, -1.0).astype(np.float32)


def build_program(debug=False):
    nc = bass.Bass("TRN2", target_bir_lowering=False)
    dt = nc.dram_tensor
    dbg = dict(kind="ExternalOutput") if debug else {}

    h_xown = dt("xown", [D, TOK], F32, kind="ExternalInput")
    h_xpre = dt("xpre", [OB * TSEQ, D], F32, kind="ExternalInput")
    h_wqA = dt("wqA", [D, D], F32, kind="ExternalInput")
    h_wqB = dt("wqB", [D, D], F32, kind="ExternalInput")
    h_cA = dt("cAb", [128, D], F32, kind="ExternalInput")
    h_cB = dt("cBb", [128, D], F32, kind="ExternalInput")
    h_wk = dt("wkp", [D, DV], F32, kind="ExternalInput")
    h_wv = dt("wvt", [D, DV], F32, kind="ExternalInput")
    h_woa = dt("woa", [D, D], F32, kind="ExternalInput")
    h_wqb = dt("wqb", [D, D], F32, kind="ExternalInput")
    h_wkb = dt("wkb", [D, D], F32, kind="ExternalInput")
    h_wvb = dt("wvb", [D, D], F32, kind="ExternalInput")
    h_wbb = dt("wbb", [D, NH], F32, kind="ExternalInput")
    h_wob = dt("wob", [D, D], F32, kind="ExternalInput")
    h_conv = dt("convr", [1, 3 * 4 * D], F32, kind="ExternalInput")
    h_g = dt("gcol", [128, 1], F32, kind="ExternalInput")
    h_w1 = dt("w1t", [D, HID], F32, kind="ExternalInput")
    h_w3 = dt("w3t", [D, HID], F32, kind="ExternalInput")
    h_w2 = dt("w2t", [HID, D], F32, kind="ExternalInput")
    h_id = dt("ident", [128, 128], BF16, kind="ExternalInput")
    h_h128 = dt("h128", [128, 128], BF16, kind="ExternalInput")
    h_onesc = dt("onesc", [128, 1], BF16, kind="ExternalInput")
    h_onesrf = dt("onesrf", [1, 128], F32, kind="ExternalInput")
    h_onesrb = dt("onesrb", [1, 128], BF16, kind="ExternalInput")
    h_mskTA = dt("mskTA", [128, 1024], BF16, kind="ExternalInput")
    h_mskTB = dt("mskTB", [128, 2048], BF16, kind="ExternalInput")

    h_y = dt("y", [D, TOK], F32, kind="ExternalOutput")
    if debug:
        h_dxn = dt("dxn", [128, NCH * TOK], BF16, kind="ExternalOutput")
        h_dq = dt("dq", [128, NH * TOK], BF16, kind="ExternalOutput")
        h_dk = dt("dk", [128, OB * TSEQ], BF16, kind="ExternalOutput")
        h_dv = dt("dv", [128, OB * TSEQ], BF16, kind="ExternalOutput")
        h_dqo = dt("dqo", [128, NH * TOK], BF16, kind="ExternalOutput")
        h_dvt = dt("dvt", [128, 1024], BF16, kind="ExternalOutput")
        h_dsums = dt("dsums", [128, 16], F32, kind="ExternalOutput")
        h_dss = dt("dss", [128, 2 * 2048], F32, kind="ExternalOutput")
        h_dpb = dt("dpb", [128, 2 * 2048], BF16, kind="ExternalOutput")
        h_dpt = dt("dpt", [128, 2 * 2048], BF16, kind="ExternalOutput")
        h_dnm = dt("dnm", [128, 6], F32, kind="ExternalOutput")
    h_xs1 = dt("xs1", [D, TOK], F32, **dbg)
    h_xs2 = dt("xs2", [D, TOK], F32, **dbg)
    h_xs3 = dt("xs3", [D, TOK], F32, **dbg)

    cnt = {k: 0 for k in SEMS}

    with ExitStack() as ctx:
        def sba(stack, name, shape, d):
            return stack.enter_context(nc.sbuf_tensor(name, shape, d))

        sem = {k: ctx.enter_context(nc.semaphore("s_" + k)) for k in SEMS}

        xn = sba(ctx, "xn", [128, NCH * TOK], BF16)
        ident = sba(ctx, "ident_s", [128, 128], BF16)
        h128 = sba(ctx, "h128_s", [128, 128], BF16)
        onesc = sba(ctx, "onesc_s", [128, 1], BF16)
        onesrf = sba(ctx, "onesrf_s", [1, 128], F32)
        onesrb = sba(ctx, "onesrb_s", [1, 128], BF16)
        gcol = sba(ctx, "gcol_s", [128, 1], F32)
        wbb_sb = sba(ctx, "wbb_s", [128, NCH * NH], BF16)
        epsln = sba(ctx, "epsln", [128, 1], F32)
        eps12 = sba(ctx, "eps12", [128, 1], F32)
        eps12s = sba(ctx, "eps12s", [128, 1], F32)

        with nc.Block() as blk:
            @blk.sync
            def _(sy):
                loads = [(ident, h_id), (h128, h_h128), (onesc, h_onesc),
                         (onesrf, h_onesrf), (onesrb, h_onesrb), (gcol, h_g)]
                for dsb, hh in loads:
                    sy.dma_start(out=dsb[:, :], in_=hh.ap()).then_inc(sem["ld"], 16)
                sy.wait_ge(sem["ld"], cnt["ld"] + 16 * len(loads))

            @blk.vector
            def _(v):
                v.memset(epsln[:, :], EPS_LN)
                v.memset(eps12[:, :], 1e-12)
                v.memset(eps12s[:, :], 128.0 * 1e-12)

            @blk.gpsimd
            def _(g):
                g.dma_start(out=wbb_sb[:, :].rearrange("p (c w) -> p c w", c=NCH),
                            in_=h_wbb.ap().rearrange("(c p) w -> p c w", p=128)
                            ).then_inc(sem["wd"], 16)
                g.wait_ge(sem["wd"], cnt["wd"] + 16)
        cnt["ld"] += 16 * 6
        cnt["wd"] += 16

        def LN(tag, src):
            b = dict(cnt)
            with ExitStack() as lctx:
                xstg = [sba(lctx, f"xstg{tag}{i}", [128, TOK], F32) for i in range(2)]
                sqs = [sba(lctx, f"sqs{tag}{i}", [128, TOK], BF16) for i in range(2)]
                rows = sba(lctx, f"rows{tag}", [1, 6 * TOK], F32)
                ab_b = sba(lctx, f"abb{tag}", [1, 2 * TOK], BF16)
                a_bc = sba(lctx, f"abc{tag}", [128, TOK], F32)
                b_bc = sba(lctx, f"bbc{tag}", [128, TOK], F32)
                tmpf = sba(lctx, f"tmpf{tag}", [128, TOK], F32)
                ps_s0 = lctx.enter_context(nc.psum_tensor(f"pss0{tag}", [1, TOK], F32))
                ps_s1 = lctx.enter_context(nc.psum_tensor(f"pss1{tag}", [1, TOK], F32))
                ps_b0 = lctx.enter_context(nc.psum_tensor(f"psb0{tag}", [128, TOK], F32))
                ps_b1 = lctx.enter_context(nc.psum_tensor(f"psb1{tag}", [128, TOK], F32))

                def row(i):
                    return rows[0:1, i * TOK:(i + 1) * TOK]

                with nc.Block() as blk:
                    @blk.sync
                    def _(sy):
                        for ch in range(NCH):
                            if ch >= 2:
                                sy.wait_ge(sem["cp"], b["cp"] + ch - 1)
                                sy.wait_ge(sem["sq"], b["sq"] + ch - 1)
                            sy.dma_start(out=xstg[ch % 2][:, :],
                                         in_=src.ap()[ch * 128:(ch + 1) * 128, :]
                                         ).then_inc(sem["ld"], 16)
                        sy.wait_ge(sem["cp"], b["cp"] + 16)
                        sy.wait_ge(sem["sq"], b["sq"] + 16)
                        for ch in range(NCH):
                            if ch >= 2:
                                sy.wait_ge(sem["ap"], b["ap"] + ch - 1)
                            sy.dma_start(out=xstg[ch % 2][:, :],
                                         in_=src.ap()[ch * 128:(ch + 1) * 128, :]
                                         ).then_inc(sem["ld"], 16)

                    @blk.vector
                    def _(v):
                        for ch in range(NCH):
                            v.wait_ge(sem["ld"], b["ld"] + 16 * (ch + 1))
                            v.tensor_copy(out=xn[:, ch * TOK:(ch + 1) * TOK],
                                          in_=xstg[ch % 2][:, :]).then_inc(sem["cp"], 1)
                        v.wait_ge(sem["pe"], b["pe"] + 1)
                        v.wait_ge(sem["pemm"], b["pemm"] + 16)
                        v.tensor_scalar_mul(row(0), ps_s0[:, :], 1.0 / D)
                        v.tensor_scalar_mul(row(1), ps_s1[:, :], 1.0 / D)
                        v.tensor_mul(tmpf[0:1, :], row(0), row(0))
                        v.tensor_sub(row(2), row(1), tmpf[0:1, :]).then_inc(sem["dv"], 1)
                        v.wait_ge(sem["ac"], b["ac"] + 1)
                        v.reciprocal(row(4), row(3))
                        v.tensor_copy(out=ab_b[0:1, 0:TOK], in_=row(4))
                        v.tensor_mul(tmpf[0:1, :], row(0), row(4))
                        v.tensor_scalar_mul(ab_b[0:1, TOK:2 * TOK], tmpf[0:1, :],
                                            -1.0).then_inc(sem["dv"], 1)
                        for ch in range(NCH):
                            v.wait_ge(sem["ld"], b["ld"] + 16 * (16 + ch + 1))
                            if ch == 0:
                                v.wait_ge(sem["ac"], b["ac"] + 2)
                            v.tensor_mul(tmpf[:, :], xstg[ch % 2][:, :], a_bc[:, :])
                            v.tensor_add(xn[:, ch * TOK:(ch + 1) * TOK], tmpf[:, :],
                                         b_bc[:, :]).then_inc(sem["ap"], 1)

                    @blk.scalar
                    def _(s):
                        for ch in range(NCH):
                            s.wait_ge(sem["ld"], b["ld"] + 16 * (ch + 1))
                            if ch >= 2:
                                s.wait_ge(sem["pemm"], b["pemm"] + ch - 1)
                            s.activation(out=sqs[ch % 2][:, :], in_=xstg[ch % 2][:, :],
                                         func=AF.Square).then_inc(sem["sq"], 1)
                        s.wait_ge(sem["dv"], b["dv"] + 1)
                        s.activation(out=row(3), in_=row(2), func=AF.Sqrt,
                                     bias=epsln[0:1, 0:1]).then_inc(sem["ac"], 1)
                        s.wait_ge(sem["pe"], b["pe"] + 3)
                        s.activation(out=a_bc[:, :], in_=ps_b0[:, :], func=AF.Copy)
                        s.activation(out=b_bc[:, :], in_=ps_b1[:, :],
                                     func=AF.Copy).then_inc(sem["ac"], 1)

                    @blk.tensor
                    def _(t):
                        for ch in range(NCH):
                            t.wait_ge(sem["cp"], b["cp"] + ch + 1)
                            t.wait_ge(sem["sq"], b["sq"] + ch + 1)
                            ins = None
                            for hh in range(2):
                                ins = t.matmul(ps_s0[0:1, hh * 512:(hh + 1) * 512], onesc[:, :],
                                               xn[:, ch * TOK + hh * 512:ch * TOK + (hh + 1) * 512],
                                               start=(ch == 0), stop=(ch == NCH - 1))
                            if ch == NCH - 1:
                                ins.then_inc(sem["pe"], 1)
                            for hh in range(2):
                                ins = t.matmul(ps_s1[0:1, hh * 512:(hh + 1) * 512], onesc[:, :],
                                               sqs[ch % 2][:, hh * 512:(hh + 1) * 512],
                                               start=(ch == 0), stop=(ch == NCH - 1))
                            ins.then_inc(sem["pemm"], 1)
                        t.wait_ge(sem["dv"], b["dv"] + 2)
                        for hh in range(2):
                            ins = t.matmul(ps_b0[:, hh * 512:(hh + 1) * 512], onesrb[:, :],
                                           ab_b[0:1, hh * 512:(hh + 1) * 512],
                                           start=True, stop=True)
                        ins.then_inc(sem["pe"], 1)
                        for hh in range(2):
                            ins = t.matmul(ps_b1[:, hh * 512:(hh + 1) * 512], onesrb[:, :],
                                           ab_b[0:1, TOK + hh * 512:TOK + (hh + 1) * 512],
                                           start=True, stop=True)
                        ins.then_inc(sem["pe"], 1)

            cnt["ld"] += 16 * 32
            cnt["cp"] += 16
            cnt["sq"] += 16
            cnt["pemm"] += 16
            cnt["pe"] += 3
            cnt["ac"] += 2
            cnt["dv"] += 2
            cnt["ap"] += 16

        def FFN(tag, src, dst):
            with ExitStack() as fctx:
                hid = sba(fctx, f"hid{tag}", [128, HB * TOK], BF16)
                b = dict(cnt)
                with ExitStack() as actx:
                    w1b = [sba(actx, f"w1b{tag}{i}", [128, NCH * 128], BF16) for i in range(3)]
                    w3b = [sba(actx, f"w3b{tag}{i}", [128, NCH * 128], BF16) for i in range(3)]
                    gs = [sba(actx, f"gs{tag}{i}", [128, TOK], BF16) for i in range(2)]
                    tA = sba(actx, f"tA{tag}", [128, TOK], BF16)
                    tB = sba(actx, f"tB{tag}", [128, TOK], BF16)
                    tC = sba(actx, f"tC{tag}", [128, TOK], BF16)
                    tD = sba(actx, f"tD{tag}", [128, TOK], BF16)
                    ub = [sba(actx, f"ub{tag}{i}", [128, TOK], BF16) for i in range(2)]
                    psb = [actx.enter_context(
                        nc.psum_tensor(f"psA{tag}{i}", [128, 512], F32)) for i in range(8)]
                    with nc.Block() as blk:
                        @blk.gpsimd
                        def _(g):
                            for hb in range(HB):
                                if hb >= 3:
                                    g.wait_ge(sem["pemm"], b["pemm"] + 4 * (hb - 2))
                                for hsrc, wb in ((h_w1, w1b), (h_w3, w3b)):
                                    g.dma_start(
                                        out=wb[hb % 3][:, :].rearrange("p (c o) -> p c o", c=NCH),
                                        in_=hsrc.ap()[:, hb * 128:(hb + 1) * 128]
                                        .rearrange("(c p) o -> p c o", p=128)
                                    ).then_inc(sem["wd"], 16)

                        @blk.tensor
                        def _(t):
                            def h128mm(c):
                                t.wait_ge(sem["dv"], b["dv"] + c + 1)
                                t.wait_ge(sem["ac"], b["ac"] + c + 1)
                                bb2 = (c % 2) * 4
                                for tt in range(2):
                                    t.matmul(psb[bb2 + tt][:, :], h128[:, :],
                                             hid[:, c * TOK + tt * 512:c * TOK + (tt + 1) * 512],
                                             start=True, stop=True).then_inc(sem["pe"], 1)

                            for hb in range(HB):
                                t.wait_ge(sem["wd"], b["wd"] + 32 * (hb + 1))
                                if hb >= 2:
                                    t.wait_ge(sem["ac"], b["ac"] + hb - 1)
                                    t.wait_ge(sem["dv"], b["dv"] + hb - 1)
                                    t.wait_ge(sem["ac2"], b["ac2"] + 2 * (hb - 1))
                                bb = (hb % 2) * 4
                                for wsel, boff in ((w1b, 0), (w3b, 2)):
                                    for tt in range(2):
                                        ins = None
                                        for ch in range(NCH):
                                            ins = t.matmul(
                                                psb[bb + boff + tt][:, :],
                                                wsel[hb % 3][:, ch * 128:(ch + 1) * 128],
                                                xn[:, ch * TOK + tt * 512:ch * TOK + (tt + 1) * 512],
                                                start=(ch == 0), stop=(ch == NCH - 1))
                                        ins.then_inc(sem["pemm"], 1)
                                if hb >= 1:
                                    h128mm(hb - 1)
                            h128mm(HB - 1)

                        @blk.scalar
                        def _(s):
                            def h128ev(c):
                                bb2 = (c % 2) * 4
                                for tt in range(2):
                                    s.wait_ge(sem["pe"], b["pe"] + 2 * c + tt + 1)
                                    s.activation(out=hid[:, c * TOK + tt * 512:
                                                         c * TOK + (tt + 1) * 512],
                                                 in_=psb[bb2 + tt][:, :], func=AF.Copy,
                                                 scale=S8192).then_inc(sem["ac2"], 1)

                            for hb in range(HB):
                                bb = (hb % 2) * 4
                                if hb >= 2:
                                    s.wait_ge(sem["dv"], b["dv"] + hb - 1)
                                ins = None
                                for tt in range(2):
                                    s.wait_ge(sem["pemm"], b["pemm"] + 4 * hb + tt + 1)
                                    ins = s.activation(out=gs[hb % 2][:, tt * 512:(tt + 1) * 512],
                                                       in_=psb[bb + tt][:, :], func=AF.Silu)
                                ins.then_inc(sem["ac"], 1)
                                if hb >= 1:
                                    h128ev(hb - 1)
                            h128ev(HB - 1)

                        @blk.gpsimd
                        def _(gp):
                            def radix4p(c0, c1, c2, c3):
                                s0 = hid[:, c0 * TOK:(c0 + 1) * TOK]
                                s1 = hid[:, c1 * TOK:(c1 + 1) * TOK]
                                s2 = hid[:, c2 * TOK:(c2 + 1) * TOK]
                                s3 = hid[:, c3 * TOK:(c3 + 1) * TOK]
                                gp.tensor_tensor(out=tC[:, :], in0=s0, in1=s1, op=ALU.add)
                                gp.tensor_tensor(out=tD[:, :], in0=s0, in1=s1, op=ALU.subtract)
                                gp.tensor_tensor(out=s0, in0=s2, in1=s3, op=ALU.add)
                                gp.tensor_tensor(out=s1, in0=s2, in1=s3, op=ALU.subtract)
                                gp.tensor_tensor(out=s2, in0=tC[:, :], in1=s0, op=ALU.subtract)
                                gp.tensor_tensor(out=s3, in0=tD[:, :], in1=s1, op=ALU.subtract)
                                gp.tensor_tensor(out=s0, in0=tC[:, :], in1=s0, op=ALU.add)
                                gp.tensor_tensor(out=s1, in0=tD[:, :], in1=s1, op=ALU.add)

                            gp.wait_ge(sem["dv"], b["dv"] + HB + 1)
                            ins = None
                            for j in range(13, 16):
                                radix4p(j, j + 16, j + 32, j + 48)
                            gp.engine_nop().then_inc(sem["ac2"], 1)

                        @blk.vector
                        def _(v):
                            def radix4(c0, c1, c2, c3):
                                s0 = hid[:, c0 * TOK:(c0 + 1) * TOK]
                                s1 = hid[:, c1 * TOK:(c1 + 1) * TOK]
                                s2 = hid[:, c2 * TOK:(c2 + 1) * TOK]
                                s3 = hid[:, c3 * TOK:(c3 + 1) * TOK]
                                v.tensor_add(tA[:, :], s0, s1)
                                v.tensor_sub(tB[:, :], s0, s1)
                                v.tensor_add(s0, s2, s3)
                                v.tensor_sub(s1, s2, s3)
                                v.tensor_sub(s2, tA[:, :], s0)
                                v.tensor_sub(s3, tB[:, :], s1)
                                v.tensor_add(s0, tA[:, :], s0)
                                v.tensor_add(s1, tB[:, :], s1)

                            for hb in range(HB):
                                bb = (hb % 2) * 4
                                for tt in range(2):
                                    v.wait_ge(sem["pemm"], b["pemm"] + 4 * hb + 2 + tt + 1)
                                    v.tensor_copy(out=ub[hb % 2][:, tt * 512:(tt + 1) * 512],
                                                  in_=psb[bb + 2 + tt][:, :])
                                v.wait_ge(sem["ac"], b["ac"] + hb + 1)
                                v.tensor_mul(hid[:, hb * TOK:(hb + 1) * TOK], gs[hb % 2][:, :],
                                             ub[hb % 2][:, :]).then_inc(sem["dv"], 1)
                                if hb % 4 == 3:
                                    qd = hb // 4
                                    v.wait_ge(sem["ac2"], b["ac2"] + 2 * (4 * qd + 4))
                                    radix4(4 * qd, 4 * qd + 1, 4 * qd + 2, 4 * qd + 3)
                                if hb % 16 == 15:
                                    g2 = hb // 16
                                    for j in range(4):
                                        radix4(16 * g2 + j, 16 * g2 + j + 4,
                                               16 * g2 + j + 8, 16 * g2 + j + 12)
                            nc.vector.engine_nop().then_inc(sem["dv"], 1)
                            for j in range(13):
                                radix4(j, j + 16, j + 32, j + 48)
                    cnt["wd"] += 16 * 2 * HB
                    cnt["pemm"] += 4 * HB
                    cnt["pe"] += 2 * HB
                    cnt["ac"] += HB
                    cnt["ac2"] += 2 * HB + 1
                    cnt["dv"] += HB + 1

                b = dict(cnt)
                with ExitStack() as dctx:
                    w2b = [sba(dctx, f"w2b{tag}{i}", [128, 32 * 128], BF16) for i in range(3)]
                    xstg = [sba(dctx, f"xsd{tag}{i}", [128, TOK], F32) for i in range(2)]
                    ostg = [sba(dctx, f"osd{tag}{i}", [128, 512], F32) for i in range(6)]
                    pso = [dctx.enter_context(
                        nc.psum_tensor(f"psD{tag}{i}", [128, 512], F32)) for i in range(2)]
                    with nc.Block() as blk:
                        @blk.gpsimd
                        def _(g):
                            for half in range(32):
                                dc, hh = divmod(half, 2)
                                if half >= 3:
                                    g.wait_ge(sem["pemm"],
                                              b["pemm"] + 2 * ((half - 3) // 2) + 2)
                                g.dma_start(
                                    out=w2b[half % 3][:, :].rearrange("p (c o) -> p c o", c=32),
                                    in_=h_w2.ap()[hh * 4096:(hh + 1) * 4096,
                                                  dc * 128:(dc + 1) * 128]
                                    .rearrange("(c p) o -> p c o", p=128)).then_inc(sem["wd"], 16)

                        @blk.sync
                        def _(sy):
                            for dc in range(NCH):
                                if dc >= 2:
                                    sy.wait_ge(sem["dv"], b["dv"] + 2 * (dc - 1))
                                sy.dma_start(out=xstg[dc % 2][:, :],
                                             in_=src.ap()[dc * 128:(dc + 1) * 128, :]
                                             ).then_inc(sem["ld"], 16)

                        @blk.scalar
                        def _(sy):
                            for idx in range(32):
                                dc, tt = divmod(idx, 2)
                                sy.wait_ge(sem["dv"], b["dv"] + idx + 1)
                                sy.dma_start(out=dst.ap()[dc * 128:(dc + 1) * 128,
                                                          tt * 512:(tt + 1) * 512],
                                             in_=ostg[idx % 6][:, :]).then_inc(sem["st"], 16)
                            sy.wait_ge(sem["st"], b["st"] + 16 * 32)

                        @blk.tensor
                        def _(t):
                            for dc in range(NCH):
                                for tt in range(2):
                                    idx = dc * 2 + tt
                                    if idx >= 2:
                                        t.wait_ge(sem["dv"], b["dv"] + idx - 1)
                                    ins = None
                                    for hb in range(HB):
                                        half = dc * 2 + hb // 32
                                        if hb % 32 == 0 and tt == 0:
                                            t.wait_ge(sem["wd"], b["wd"] + 16 * (half + 1))
                                        ins = t.matmul(
                                            pso[idx % 2][:, :],
                                            w2b[half % 3][:, (hb % 32) * 128:(hb % 32 + 1) * 128],
                                            hid[:, hb * TOK + tt * 512:hb * TOK + (tt + 1) * 512],
                                            start=(hb == 0), stop=(hb == HB - 1))
                                    ins.then_inc(sem["pemm"], 1)

                        @blk.vector
                        def _(v):
                            for idx in range(32):
                                dc, tt = divmod(idx, 2)
                                v.wait_ge(sem["pemm"], b["pemm"] + idx + 1)
                                v.wait_ge(sem["ld"], b["ld"] + 16 * (dc + 1))
                                if idx >= 6:
                                    v.wait_ge(sem["st"], b["st"] + 16 * (idx - 5))
                                v.tensor_add(ostg[idx % 6][:, :], pso[idx % 2][:, :],
                                             xstg[dc % 2][:, tt * 512:(tt + 1) * 512]
                                             ).then_inc(sem["dv"], 1)
                    cnt["wd"] += 16 * 32
                    cnt["ld"] += 16 * NCH
                    cnt["pemm"] += 32
                    cnt["dv"] += 32
                    cnt["st"] += 16 * 32

        LN("L1", h_xown)

        with ExitStack() as s1ctx:
            qo = sba(s1ctx, "qo", [128, NH * TOK], BF16)

            b = dict(cnt)
            with ExitStack() as rctx:
                wqp = sba(rctx, "wqp", [128, NCH * D], BF16)
                cab = sba(rctx, "cab", [128, D], F32)
                cbb = sba(rctx, "cbb", [128, D], F32)
                wsA = [sba(rctx, f"wsA{i}", [128, D], F32) for i in range(2)]
                wsB = [sba(rctx, f"wsB{i}", [128, D], F32) for i in range(2)]
                tmpa = sba(rctx, "tmpa", [128, 1024], F32)
                tmpb = sba(rctx, "tmpb", [128, 1024], F32)
                with nc.Block() as blk:
                    @blk.sync
                    def _(sy):
                        sy.dma_start(out=cab[:, :], in_=h_cA.ap()).then_inc(sem["ld"], 16)
                        sy.dma_start(out=cbb[:, :], in_=h_cB.ap()).then_inc(sem["ld"], 16)
                        for ch in range(NCH):
                            if ch >= 2:
                                sy.wait_ge(sem["dv"], b["dv"] + ch - 1)
                            sy.dma_start(out=wsA[ch % 2][:, :],
                                         in_=h_wqA.ap()[ch * 128:(ch + 1) * 128, :]
                                         ).then_inc(sem["ld"], 16)
                            sy.dma_start(out=wsB[ch % 2][:, :],
                                         in_=h_wqB.ap()[ch * 128:(ch + 1) * 128, :]
                                         ).then_inc(sem["ld"], 16)

                    @blk.vector
                    def _(v):
                        for ch in range(NCH):
                            v.wait_ge(sem["ld"], b["ld"] + 32 + 32 * (ch + 1))
                            for hh in range(2):
                                sl = slice(hh * 1024, (hh + 1) * 1024)
                                v.tensor_mul(tmpa[:, :], wsA[ch % 2][:, sl], cab[:, sl])
                                v.tensor_mul(tmpb[:, :], wsB[ch % 2][:, sl], cbb[:, sl])
                                ins = v.tensor_add(wqp[:, ch * D + hh * 1024:
                                                       ch * D + (hh + 1) * 1024],
                                                   tmpa[:, :], tmpb[:, :])
                            ins.then_inc(sem["dv"], 1)
                cnt["ld"] += 16 * 34
                cnt["dv"] += 16

                b = dict(cnt)
                psq = [rctx.enter_context(
                    nc.psum_tensor(f"psQ{i}", [128, 512], F32)) for i in range(2)]
                with nc.Block() as blk:
                    @blk.tensor
                    def _(t):
                        t.wait_ge(sem["dv"], b["dv"])
                        for h in range(NH):
                            for tt in range(2):
                                gi = h * 2 + tt
                                if gi >= 2:
                                    t.wait_ge(sem["ac"], b["ac"] + gi - 1)
                                ins = None
                                for ch in range(NCH):
                                    ins = t.matmul(
                                        psq[gi % 2][:, :],
                                        wqp[:, ch * D + h * 128:ch * D + (h + 1) * 128],
                                        xn[:, ch * TOK + tt * 512:ch * TOK + (tt + 1) * 512],
                                        start=(ch == 0), stop=(ch == NCH - 1))
                                ins.then_inc(sem["pemm"], 1)

                    @blk.scalar
                    def _(s):
                        for gi in range(32):
                            h, tt = divmod(gi, 2)
                            s.wait_ge(sem["pemm"], b["pemm"] + gi + 1)
                            s.activation(out=qo[:, h * TOK + tt * 512:h * TOK + (tt + 1) * 512],
                                         in_=psq[gi % 2][:, :],
                                         func=AF.Copy).then_inc(sem["ac"], 1)
                cnt["pemm"] += 32
                cnt["ac"] += 32

            if debug:
                with nc.Block() as blk:
                    @blk.sync
                    def _(sy):
                        sy.dma_start(out=h_dxn.ap(), in_=xn[:, :]).then_inc(sem["st"], 16)
                        sy.dma_start(out=h_dq.ap(), in_=qo[:, :]).then_inc(sem["st"], 16)
                        sy.wait_ge(sem["st"], cnt["st"] + 32)
                cnt["st"] += 32

            with ExitStack() as actx:
                kfm = sba(actx, "kfm", [128, OB * TSEQ], BF16)
                VW = 132
                vtm = sba(actx, "vtm", [128, 64 * VW], BF16)
                mskTA = sba(actx, "mskTA_s", [128, 1024], BF16)
                mskTB = sba(actx, "mskTB_s", [128, 2048], BF16)
                wkp_sb = sba(actx, "wkp_s", [128, NCH * DV], BF16)
                wvt_sb = sba(actx, "wvt_s", [128, NCH * DV], BF16)
                b20 = sba(actx, "b20", [128, 1], F32)
                pstg = [sba(actx, f"pstg{i}", [128, D], F32) for i in range(2)]
                xnt = [sba(actx, f"xnt{i}", [128, D], BF16) for i in range(2)]
                xfmt = [sba(actx, f"xfmt{i}", [128, NCH * 128], BF16) for i in range(2)]
                sums = sba(actx, "sums", [128, 8], F32)
                dbgs = sba(actx, "dbgs", [128, 16], F32)
                junkb = sba(actx, "junkb", [128, D], BF16)
                ptsb = [sba(actx, f"ptsb{i}", [128, 2048], BF16) for i in range(2)]
                otm = [sba(actx, f"otm{i}", [128, 128], BF16) for i in range(2)]
                rinv = sba(actx, "rinv", [128, 2], F32)
                tpA = actx.enter_context(nc.psum_tensor("psTpA", [128, 128], BF16))
                tpB = actx.enter_context(nc.psum_tensor("psTpB", [128, 128], BF16))
                tp = [tpA, tpB]
                psk = actx.enter_context(nc.psum_tensor("psK", [128, 128], F32))
                psv = actx.enter_context(nc.psum_tensor("psV", [128, 128], F32))
                psS = [actx.enter_context(nc.psum_tensor(f"psS{i}", [128, 512], F32))
                       for i in range(2)]
                pso = [actx.enter_context(nc.psum_tensor(f"psO{i}", [128, 132], F32))
                       for i in range(2)]

                b = dict(cnt)
                with nc.Block() as blk:
                    @blk.sync
                    def _(sy):
                        sy.dma_start(out=mskTA[:, :], in_=h_mskTA.ap()).then_inc(sem["ld"], 16)
                        sy.dma_start(out=mskTB[:, :], in_=h_mskTB.ap()).then_inc(sem["ld"], 16)
                        sy.wait_ge(sem["ld"], b["ld"] + 32)

                    @blk.gpsimd
                    def _(g):
                        for dsb, hh in ((wkp_sb, h_wk), (wvt_sb, h_wv)):
                            g.dma_start(out=dsb[:, :].rearrange("p (c w) -> p c w", c=NCH),
                                        in_=hh.ap().rearrange("(c p) w -> p c w", p=128)
                                        ).then_inc(sem["wd"], 16)
                        g.wait_ge(sem["wd"], b["wd"] + 32)

                    @blk.vector
                    def _(v):
                        v.memset(vtm[:, :], 1.0)
                        v.memset(b20[:, :], -20.0)
                cnt["ld"] += 32
                cnt["wd"] += 32

                b = dict(cnt)
                NT = 64
                with nc.Block() as blk:
                    @blk.sync
                    def _(sy):
                        for i in range(NT):
                            ob, tb = divmod(i, 16)
                            if i >= 2:
                                sy.wait_ge(sem["ap"], b["ap"] + i - 1)
                            sy.dma_start(out=pstg[i % 2][:, :],
                                         in_=h_xpre.ap()[ob * TSEQ + tb * 128:
                                                         ob * TSEQ + tb * 128 + 128, :]
                                         ).then_inc(sem["ld"], 16)

                    @blk.vector
                    def _(v):
                        for i in range(NT):
                            p = i % 2
                            ob, tb = divmod(i, 16)
                            v.wait_ge(sem["ld"], b["ld"] + 16 * (i + 1))
                            v.tensor_scalar(out=junkb[:, :], in0=pstg[p][:, :],
                                            scalar1=1.0, scalar2=0.0, op0=ALU.mult,
                                            op1=ALU.add, accum_out=sums[:, 0:1])
                            v.scalar_tensor_tensor(out=junkb[:, :], in0=pstg[p][:, :],
                                                   scalar=1.0, in1=pstg[p][:, :],
                                                   op0=ALU.mult, op1=ALU.mult,
                                                   accum_out=sums[:, 1:2])
                            v.tensor_scalar_mul(sums[:, 2:3], sums[:, 0:1], 1.0 / D)
                            v.tensor_scalar_mul(sums[:, 4:5], sums[:, 1:2],
                                                1.0 / D).then_inc(sem["fz"], 1)
                            v.wait_ge(sem["fz"], b["fz"] + 2 * i + 1)
                            v.tensor_mul(sums[:, 3:4], sums[:, 2:3],
                                         sums[:, 2:3]).then_inc(sem["fz"], 1)
                            v.wait_ge(sem["fz"], b["fz"] + 2 * i + 2)
                            v.tensor_sub(sums[:, 5:6], sums[:, 4:5],
                                         sums[:, 3:4]).then_inc(sem["dv"], 1)
                            v.wait_ge(sem["ac"], b["ac"] + 17 * i + 1)
                            v.reciprocal(sums[:, 6:7],
                                         sums[:, 7:8]).then_inc(sem["rcp"], 1)
                            if i == 0:
                                v.tensor_copy(out=dbgs[:, 8:16], in_=sums[:, 0:8])
                            v.wait_ge(sem["rcp"], b["rcp"] + i + 1)
                            v.tensor_scalar(out=xnt[p][:, :], in0=pstg[p][:, :],
                                            scalar1=sums[:, 2:3], scalar2=sums[:, 6:7],
                                            op0=ALU.subtract, op1=ALU.mult
                                            ).then_inc(sem["ap"], 1)
                            v.wait_ge(sem["pemm"], b["pemm"] + 2 * i + 1)
                            v.tensor_copy(
                                out=kfm[:, ob * TSEQ + tb * 128:ob * TSEQ + (tb + 1) * 128],
                                in_=psk[:, :]).then_inc(sem["ac2"], 1)
                            v.wait_ge(sem["pemm"], b["pemm"] + 2 * i + 2)
                            v.tensor_copy(
                                out=vtm[:, (ob * 16 + tb) * VW:(ob * 16 + tb) * VW + 128],
                                in_=psv[:, :]).then_inc(sem["ac2"], 1)

                    @blk.scalar
                    def _(s):
                        for i in range(NT):
                            s.wait_ge(sem["dv"], b["dv"] + i + 1)
                            s.activation(out=sums[:, 7:8], in_=sums[:, 5:6],
                                         func=AF.Sqrt,
                                         bias=epsln[:, 0:1]).then_inc(sem["ac"], 1)
                            if i >= 2:
                                s.wait_ge(sem["pemm"], b["pemm"] + 2 * (i - 1))
                            for ch in range(NCH):
                                s.wait_ge(sem["T"], b["T"] + i * NCH + ch + 1)
                                s.activation(out=xfmt[i % 2][:, ch * 128:(ch + 1) * 128],
                                             in_=tp[(i * NCH + ch) % 2][:, :],
                                             func=AF.Copy).then_inc(sem["ac"], 1)

                    @blk.tensor
                    def _(t):
                        for i in range(NT):
                            t.wait_ge(sem["ap"], b["ap"] + i + 1)
                            for ch in range(NCH):
                                gT = i * NCH + ch
                                if gT >= 2:
                                    gp = gT - 2
                                    ip, chp = divmod(gp, NCH)
                                    t.wait_ge(sem["ac"], b["ac"] + 17 * ip + chp + 2)
                                t.transpose(tp[gT % 2][:, :],
                                            xnt[i % 2][:, ch * 128:(ch + 1) * 128],
                                            ident[:, :]).then_inc(sem["T"], 1)
                            t.wait_ge(sem["ac"], b["ac"] + 17 * i + 17)
                            if i >= 1:
                                t.wait_ge(sem["ac2"], b["ac2"] + 2 * (i - 1) + 1)
                            ins = None
                            for ch in range(NCH):
                                ins = t.matmul(psk[:, :], wkp_sb[:, ch * DV:(ch + 1) * DV],
                                               xfmt[i % 2][:, ch * 128:(ch + 1) * 128],
                                               start=(ch == 0), stop=(ch == NCH - 1))
                            ins.then_inc(sem["pemm"], 1)
                            if i >= 1:
                                t.wait_ge(sem["ac2"], b["ac2"] + 2 * (i - 1) + 2)
                            ins = None
                            for ch in range(NCH):
                                ins = t.matmul(psv[:, :],
                                               xfmt[i % 2][:, ch * 128:(ch + 1) * 128],
                                               wvt_sb[:, ch * DV:(ch + 1) * DV],
                                               start=(ch == 0), stop=(ch == NCH - 1))
                            ins.then_inc(sem["pemm"], 1)
                cnt["ld"] += 16 * NT
                cnt["dv"] += NT
                cnt["ap"] += NT
                cnt["ac"] += 17 * NT
                cnt["ac2"] += 2 * NT
                cnt["pemm"] += 2 * NT
                cnt["rcp"] += NT
                cnt["fz"] += 2 * NT
                cnt["T"] += 16 * NT

                b = dict(cnt)
                combos = [(ob, h, bk) for ob in range(4) for h in range(NH)
                          for bk in (0, 1)]
                nkt_l = [8, 16]

                def nk(i):
                    return nkt_l[combos[i][2]]

                def nsl(i):
                    return nk(i) // 4

                cumm = [0]
                cums = [0]
                for i in range(128):
                    cumm.append(cumm[-1] + nk(i))
                    cums.append(cums[-1] + nsl(i))

                with nc.Block() as blk:
                    @blk.tensor
                    def _(t):
                        for i in range(130):
                            if i < 128:
                                ob, h, bk = combos[i]
                                qcol = h * TOK + bk * 512 + ob * 128
                                for kt in range(nk(i)):
                                    sl = cums[i] + kt // 4
                                    if kt % 4 == 0 and sl >= 2:
                                        t.wait_ge(sem["ev"], b["ev"] + sl - 1)
                                    t.matmul(
                                        psS[sl % 2][:, (kt % 4) * 128:(kt % 4 + 1) * 128],
                                        kfm[:, ob * TSEQ + kt * 128:ob * TSEQ + (kt + 1) * 128],
                                        qo[:, qcol:qcol + 128],
                                        start=True, stop=True).then_inc(sem["pe"], 1)
                            if 1 <= i <= 128:
                                j = i - 1
                                ob, h, bk = combos[j]
                                t.wait_ge(sem["mx"], b["mx"] + j + 1)
                                if j >= 2:
                                    t.wait_ge(sem["oev"], b["oev"] + j - 1)
                                ins = None
                                for kt in range(nk(j)):
                                    ins = t.matmul(
                                        pso[j % 2][:, 0:129],
                                        ptsb[j % 2][:, kt * 128:(kt + 1) * 128],
                                        vtm[:, (ob * 16 + kt) * VW:(ob * 16 + kt) * VW + 129],
                                        start=(kt == 0), stop=(kt == nk(j) - 1))
                                ins.then_inc(sem["pv"], 1)
                            if 2 <= i <= 129:
                                j2 = i - 2
                                if j2 >= 2:
                                    t.wait_ge(sem["ptv"], b["ptv"] + j2 - 1)
                                t.wait_ge(sem["oev"], b["oev"] + j2 + 1)
                                t.transpose(tp[j2 % 2][:, :], otm[j2 % 2][:, :],
                                            ident[:, :]).then_inc(sem["T"], 1)

                    @blk.scalar
                    def _(s):
                        for i in range(130):
                            if i < 128:
                                if i >= 2:
                                    s.wait_ge(sem["pv"], b["pv"] + i - 1)
                                for sl in range(nsl(i)):
                                    w = 512
                                    s.wait_ge(sem["pe"], b["pe"] + cumm[i] + (sl + 1) * 4)
                                    s.activation(
                                        out=ptsb[i % 2][:, sl * 512:sl * 512 + w],
                                        in_=psS[(cums[i] + sl) % 2][:, 0:w], func=AF.Exp,
                                        scale=SCALE_A, bias=b20[:, 0:1]).then_inc(sem["ev"], 1)
                            if 1 <= i <= 128:
                                j = i - 1
                                if j >= 2:
                                    s.wait_ge(sem["T"], b["T"] + j - 1)
                                s.wait_ge(sem["rcp"], b["rcp"] + j + 1)
                                s.activation(out=otm[j % 2][:, :],
                                             in_=pso[j % 2][:, 0:128], func=AF.Copy,
                                             scale=rinv[:, j % 2:j % 2 + 1]
                                             ).then_inc(sem["oev"], 1)
                            if 2 <= i <= 129:
                                j2 = i - 2
                                ob, h, bk = combos[j2]
                                qcol = h * TOK + bk * 512 + ob * 128
                                s.wait_ge(sem["T"], b["T"] + j2 + 1)
                                s.activation(out=qo[:, qcol:qcol + 128],
                                             in_=tp[j2 % 2][:, :],
                                             func=AF.Copy).then_inc(sem["ptv"], 1)

                    @blk.vector
                    def _(v):
                        for i in range(129):
                            if i < 128:
                                bk = combos[i][2]
                                msk = mskTA if bk == 0 else mskTB
                                ins = None
                                for sl in range(nsl(i)):
                                    v.wait_ge(sem["ev"], b["ev"] + cums[i] + sl + 1)
                                    ins = v.tensor_mul(ptsb[i % 2][:, sl * 512:(sl + 1) * 512],
                                                       ptsb[i % 2][:, sl * 512:(sl + 1) * 512],
                                                       msk[:, sl * 512:(sl + 1) * 512])
                                ins.then_inc(sem["mx"], 1)
                            if i >= 1:
                                j = i - 1
                                v.wait_ge(sem["pv"], b["pv"] + j + 1)
                                if j >= 2:
                                    v.wait_ge(sem["oev"], b["oev"] + j - 1)
                                v.reciprocal(rinv[:, j % 2:j % 2 + 1],
                                             pso[j % 2][:, 128:129]
                                             ).then_inc(sem["rcp"], 1)
                cnt["pe"] += cumm[128]
                cnt["ev"] += cums[128]
                cnt["mx"] += 128
                cnt["pv"] += 128
                cnt["rcp"] += 128
                cnt["oev"] += 128
                cnt["ptv"] += 128
                cnt["T"] += 128

                if debug:
                    with nc.Block() as blk:
                        @blk.sync
                        def _(sy):
                            sy.dma_start(out=h_dvt.ap(), in_=vtm[:, 0:1024]).then_inc(sem["st"], 16)
                            sy.dma_start(out=h_dsums.ap(), in_=dbgs[:, :]).then_inc(sem["st"], 16)
                            sy.dma_start(out=h_dnm.ap()[:, 0:2], in_=rinv[:, :]).then_inc(sem["st"], 16)
                            sy.dma_start(out=h_dqo.ap(), in_=qo[:, :]).then_inc(sem["st"], 16)
                            sy.dma_start(out=h_dpt.ap()[:, 0:2048],
                                         in_=ptsb[0][:, :]).then_inc(sem["st"], 16)
                            sy.dma_start(out=h_dpt.ap()[:, 2048:4096],
                                         in_=ptsb[1][:, :]).then_inc(sem["st"], 16)
                            sy.wait_ge(sem["st"], cnt["st"] + 96)
                    cnt["st"] += 96

            b = dict(cnt)
            with ExitStack() as wctx:
                woab = [sba(wctx, f"woab{i}", [128, NCH * 128], BF16) for i in range(2)]
                xstg = [sba(wctx, f"xsw{i}", [128, TOK], F32) for i in range(2)]
                ostg = [sba(wctx, f"osw{i}", [128, 512], F32) for i in range(4)]
                psw = [wctx.enter_context(
                    nc.psum_tensor(f"psW{i}", [128, 512], F32)) for i in range(2)]
                with nc.Block() as blk:
                    @blk.gpsimd
                    def _(g):
                        for oc in range(NCH):
                            if oc >= 2:
                                g.wait_ge(sem["pemm"], b["pemm"] + 2 * (oc - 1))
                            g.dma_start(
                                out=woab[oc % 2][:, :].rearrange("p (c o) -> p c o", c=NCH),
                                in_=h_woa.ap()[:, oc * 128:(oc + 1) * 128]
                                .rearrange("(c p) o -> p c o", p=128)).then_inc(sem["wd"], 16)

                    @blk.sync
                    def _(sy):
                        for oc in range(NCH):
                            if oc >= 2:
                                sy.wait_ge(sem["dv"], b["dv"] + 2 * (oc - 1))
                            sy.dma_start(out=xstg[oc % 2][:, :],
                                         in_=h_xown.ap()[oc * 128:(oc + 1) * 128, :]
                                         ).then_inc(sem["ld"], 16)

                    @blk.scalar
                    def _(sy):
                        for idx in range(32):
                            oc, tt = divmod(idx, 2)
                            sy.wait_ge(sem["dv"], b["dv"] + idx + 1)
                            sy.dma_start(out=h_xs1.ap()[oc * 128:(oc + 1) * 128,
                                                        tt * 512:(tt + 1) * 512],
                                         in_=ostg[idx % 4][:, :]).then_inc(sem["st"], 16)
                        sy.wait_ge(sem["st"], b["st"] + 16 * 32)

                    @blk.tensor
                    def _(t):
                        t.wait_ge(sem["ptv"], b["ptv"])
                        for oc in range(NCH):
                            t.wait_ge(sem["wd"], b["wd"] + 16 * (oc + 1))
                            for tt in range(2):
                                idx = oc * 2 + tt
                                if idx >= 2:
                                    t.wait_ge(sem["dv"], b["dv"] + idx - 1)
                                ins = None
                                for ch in range(NCH):
                                    ins = t.matmul(
                                        psw[idx % 2][:, :],
                                        woab[oc % 2][:, ch * 128:(ch + 1) * 128],
                                        qo[:, ch * TOK + tt * 512:ch * TOK + (tt + 1) * 512],
                                        start=(ch == 0), stop=(ch == NCH - 1))
                                ins.then_inc(sem["pemm"], 1)

                    @blk.vector
                    def _(v):
                        for idx in range(32):
                            oc, tt = divmod(idx, 2)
                            v.wait_ge(sem["pemm"], b["pemm"] + idx + 1)
                            v.wait_ge(sem["ld"], b["ld"] + 16 * (oc + 1))
                            if idx >= 4:
                                v.wait_ge(sem["st"], b["st"] + 16 * (idx - 3))
                            v.tensor_add(ostg[idx % 4][:, :], psw[idx % 2][:, :],
                                         xstg[oc % 2][:, tt * 512:(tt + 1) * 512]
                                         ).then_inc(sem["dv"], 1)
                cnt["wd"] += 16 * NCH
                cnt["ld"] += 16 * NCH
                cnt["pemm"] += 32
                cnt["dv"] += 32
                cnt["st"] += 16 * 32

        LN("L2", h_xs1)
        FFN("f1", h_xs1, h_xs2)

        LN("L3", h_xs2)
        _attn_b(nc, sem, cnt, sba, xn, h_conv, wbb_sb, gcol, onesrb, ident,
                h_wqb, h_wkb, h_wvb, h_wob, h_xs2, h_xs3, epsln, eps12, eps12s)

        LN("L4", h_xs3)
        FFN("f2", h_xs3, h_y)

    return nc


def _attn_b(nc, sem, cnt, sba, xn, h_conv, wbb_sb, gcol, onesrb, ident,
            h_wqb, h_wkb, h_wvb, h_wob, h_src, h_dst, epsln, eps12, eps12s):
    from contextlib import ExitStack
    TERMS = {0: [(0, 3)], 1: [(0, 2), (1, 3)], 2: [(0, 1), (1, 2), (2, 3)],
             3: [(0, 0), (1, 1), (2, 2), (3, 3)]}
    CP = [(1, 0), (2, 0), (2, 1), (3, 0), (3, 1), (3, 2)]
    APRS = [(t, i) for t in range(4) for i in range(t + 1)]
    with ExitStack() as bctx:
        ofm = [sba(bctx, f"ofm{b_}", [128, NH * 512], BF16) for b_ in range(2)]
        for bk in range(2):
            with ExitStack() as dctx:
                yq = sba(dctx, f"yq{bk}", [128, 4 * D], BF16)
                yk = sba(dctx, f"yk{bk}", [128, 4 * D], BF16)
                yv = sba(dctx, f"yv{bk}", [128, 4 * D], BF16)

                with ExitStack() as pctx:
                    raw = sba(pctx, f"raw{bk}", [128, 4 * D], BF16)
                    wsl = [sba(pctx, f"wsl{bk}{i}", [128, NCH * 512], BF16)
                           for i in range(2)]
                    wtap = sba(pctx, f"wtap{bk}", [128, 4 * D], BF16)
                    ctmp = sba(pctx, f"ctmp{bk}", [128, D], BF16)
                    crow = [sba(pctx, f"crow{bk}{i}", [1, D], BF16) for i in range(2)]
                    psp = [pctx.enter_context(
                        nc.psum_tensor(f"psP{bk}{i}", [128, 512], F32)) for i in range(4)]
                    psc = pctx.enter_context(nc.psum_tensor(f"psC{bk}", [128, 512], F32))
                    PJ = ((h_wqb, yq), (h_wkb, yk), (h_wvb, yv))
                    b = dict(cnt)
                    with nc.Block() as blk:
                        @blk.gpsimd
                        def _(g):
                            for pi in range(3):
                                hw = PJ[pi][0]
                                for ct in range(4):
                                    g_ = 4 * pi + ct
                                    if g_ >= 2:
                                        g.wait_ge(sem["pemm"], b["pemm"] + 4 * (g_ - 1))
                                    g.dma_start(
                                        out=wsl[g_ % 2][:, :]
                                        .rearrange("p (c o) -> p c o", c=NCH),
                                        in_=hw.ap()[:, ct * 512:(ct + 1) * 512]
                                        .rearrange("(c p) o -> p c o", p=128)
                                        ).then_inc(sem["wd"], 16)
                                for tap in range(4):
                                    gt = 4 * pi + tap
                                    if gt >= 2:
                                        g.wait_ge(sem["pe"], b["pe"] + 4 * (gt - 1))
                                    g.dma_start(
                                        out=crow[gt % 2][:, :],
                                        in_=h_conv.ap()[0:1, (pi * 4 + tap) * D:
                                                        (pi * 4 + tap + 1) * D]
                                        ).then_inc(sem["wd"], 16)

                        @blk.tensor
                        def _(t):
                            for pi in range(3):
                                for ct in range(4):
                                    t.wait_ge(sem["wd"], b["wd"] + 16 * (8 * pi + ct + 1))
                                    for s in range(4):
                                        gg = 16 * pi + 4 * ct + s
                                        if gg >= 4:
                                            gp_, kp = divmod(gg - 4, 16)
                                            t.wait_ge(sem["ac"], b["ac"] + 36 * gp_ + kp + 1)
                                        ins = None
                                        for ch in range(NCH):
                                            ins = t.matmul(
                                                psp[gg % 4][:, :],
                                                xn[:, ch * TOK + bk * 512 + s * 128:
                                                   ch * TOK + bk * 512 + (s + 1) * 128],
                                                wsl[(4 * pi + ct) % 2][:, ch * 512:(ch + 1) * 512],
                                                start=(ch == 0), stop=(ch == NCH - 1))
                                        ins.then_inc(sem["pemm"], 1)
                                for tap in range(4):
                                    t.wait_ge(sem["wd"], b["wd"] + 16 * (8 * pi + 4 + tap + 1))
                                    for sg in range(4):
                                        gm = 16 * pi + tap * 4 + sg
                                        if gm >= 1:
                                            gp_, kp = divmod(gm - 1, 16)
                                            t.wait_ge(sem["ac"], b["ac"] + 36 * gp_ + 16 + kp + 1)
                                        t.matmul(psc[:, :], onesrb[:, :],
                                                 crow[(4 * pi + tap) % 2][0:1, sg * 512:(sg + 1) * 512],
                                                 start=True, stop=True).then_inc(sem["pe"], 1)

                        @blk.scalar
                        def _(s_):
                            for pi in range(3):
                                if pi >= 1:
                                    s_.wait_ge(sem["dv"], b["dv"] + pi)
                                for k in range(16):
                                    ct, s = divmod(k, 4)
                                    s_.wait_ge(sem["pemm"], b["pemm"] + 16 * pi + k + 1)
                                    s_.activation(
                                        out=raw[:, s * D + ct * 512:s * D + (ct + 1) * 512],
                                        in_=psp[(16 * pi + k) % 4][:, :],
                                        func=AF.Copy).then_inc(sem["ac"], 1)
                                for k in range(16):
                                    tap, sg = divmod(k, 4)
                                    s_.wait_ge(sem["pe"], b["pe"] + 16 * pi + k + 1)
                                    s_.activation(
                                        out=wtap[:, tap * D + sg * 512:tap * D + (sg + 1) * 512],
                                        in_=psc[:, :], func=AF.Copy).then_inc(sem["ac"], 1)
                                s_.wait_ge(sem["dv"], b["dv"] + pi + 1)
                                ybuf = PJ[pi][1]
                                for s in range(4):
                                    s_.activation(out=ybuf[:, s * D:(s + 1) * D],
                                                  in_=ybuf[:, s * D:(s + 1) * D],
                                                  func=AF.Silu).then_inc(sem["ac"], 1)

                        @blk.vector
                        def _(v):
                            for pi in range(3):
                                ybuf = PJ[pi][1]
                                v.wait_ge(sem["ac"], b["ac"] + 36 * pi + 32)
                                ins = None
                                for s in range(4):
                                    terms = TERMS[s]
                                    (s0, tp0) = terms[0]
                                    ysl = ybuf[:, s * D:(s + 1) * D]
                                    ins = v.tensor_mul(ysl, raw[:, s0 * D:(s0 + 1) * D],
                                                       wtap[:, tp0 * D:(tp0 + 1) * D])
                                    for (sj, tpj) in terms[1:]:
                                        v.tensor_mul(ctmp[:, :], raw[:, sj * D:(sj + 1) * D],
                                                     wtap[:, tpj * D:(tpj + 1) * D])
                                        ins = v.tensor_add(ysl, ysl, ctmp[:, :])
                                ins.then_inc(sem["dv"], 1)
                    cnt["wd"] += 16 * 24
                    cnt["pemm"] += 48
                    cnt["pe"] += 48
                    cnt["ac"] += 108
                    cnt["dv"] += 3

                b = dict(cnt)
                with ExitStack() as rctx:
                    ubuf = sba(rctx, f"ubuf{bk}", [128, 4 * D], F32)
                    obuf = sba(rctx, f"obuf{bk}", [128, 4 * D], F32)
                    beta = sba(rctx, f"beta{bk}", [128, 4 * NH], F32)
                    dots = sba(rctx, f"dots{bk}", [128, 16 * NH], F32)
                    lsum = sba(rctx, f"lsum{bk}", [128, 8 * NH], F32)
                    scr = sba(rctx, f"scr{bk}", [128, D], BF16)
                    onb = sba(rctx, f"onb{bk}", [128, 2 * 128], BF16)
                    psb_ = rctx.enter_context(nc.psum_tensor(f"psBt{bk}", [128, NH], F32))
                    tpp = [rctx.enter_context(
                        nc.psum_tensor(f"psTo{bk}{i}", [128, 128], BF16)) for i in range(2)]
                    with nc.Block() as blk:
                        @blk.tensor
                        def _(t):
                            for s in range(4):
                                if s >= 1:
                                    t.wait_ge(sem["ac"], b["ac"] + s)
                                ins = None
                                for ch in range(NCH):
                                    ins = t.matmul(
                                        psb_[:, :],
                                        xn[:, ch * TOK + bk * 512 + s * 128:
                                           ch * TOK + bk * 512 + (s + 1) * 128],
                                        wbb_sb[:, ch * NH:(ch + 1) * NH],
                                        start=(ch == 0), stop=(ch == NCH - 1))
                                ins.then_inc(sem["pemm"], 1)
                            for idx in range(64):
                                t.wait_ge(sem["dv"], b["dv"] + 2 + idx)
                                if idx >= 2:
                                    t.wait_ge(sem["ac"], b["ac"] + 6 + idx)
                                t.transpose(tpp[idx % 2][:, :],
                                            onb[:, (idx % 2) * 128:(idx % 2 + 1) * 128],
                                            ident[:, :]).then_inc(sem["T"], 1)

                        @blk.scalar
                        def _(s_):
                            for s in range(4):
                                s_.wait_ge(sem["pemm"], b["pemm"] + s + 1)
                                s_.activation(out=beta[:, s * NH:(s + 1) * NH],
                                              in_=psb_[:, :],
                                              func=AF.Sigmoid).then_inc(sem["ac"], 1)
                            s_.wait_ge(sem["dv"], b["dv"] + 1)
                            s_.activation(out=lsum[:, 0:64], in_=lsum[:, 0:64],
                                          func=AF.Sqrt, scale=128.0,
                                          bias=eps12s[:, 0:1]).then_inc(sem["ac"], 1)
                            s_.activation(out=lsum[:, 64:128], in_=lsum[:, 64:128],
                                          func=AF.Sqrt,
                                          bias=eps12[:, 0:1]).then_inc(sem["ac"], 1)
                            s_.wait_ge(sem["mx"], b["mx"] + 1)
                            s_.activation(out=lsum[:, 0:64], in_=lsum[:, 0:64],
                                          func=AF.Sqrt, scale=1.0 / 128.0,
                                          bias=epsln[:, 0:1]).then_inc(sem["ac"], 1)
                            for idx in range(64):
                                s, h = divmod(idx, 16)
                                s_.wait_ge(sem["T"], b["T"] + idx + 1)
                                s_.activation(
                                    out=ofm[bk][:, h * 512 + s * 128:h * 512 + (s + 1) * 128],
                                    in_=tpp[idx % 2][:, :],
                                    func=AF.Copy).then_inc(sem["ac"], 1)

                        @blk.vector
                        def _(v):
                            ins = None
                            for (buf, col0) in ((yq, 0), (yk, 64)):
                                for s in range(4):
                                    for h in range(NH):
                                        sl = buf[:, s * D + h * 128:s * D + (h + 1) * 128]
                                        ins = v.scalar_tensor_tensor(
                                            out=scr[:, h * 128:(h + 1) * 128], in0=sl,
                                            scalar=1.0, in1=sl, op0=ALU.mult, op1=ALU.mult,
                                            accum_out=lsum[:, col0 + s * 16 + h:
                                                           col0 + s * 16 + h + 1])
                            ins.then_inc(sem["dv"], 1)
                            v.wait_ge(sem["ac"], b["ac"] + 6)
                            v.reciprocal(lsum[:, 0:64], lsum[:, 0:64])
                            v.reciprocal(lsum[:, 64:128],
                                         lsum[:, 64:128]).then_inc(sem["rcp"], 1)
                            v.wait_ge(sem["rcp"], b["rcp"] + 1)
                            for (buf, col0) in ((yq, 0), (yk, 64)):
                                for s in range(4):
                                    for h in range(NH):
                                        sl = buf[:, s * D + h * 128:s * D + (h + 1) * 128]
                                        v.tensor_scalar_mul(
                                            sl, sl,
                                            lsum[:, col0 + s * 16 + h:col0 + s * 16 + h + 1])
                            for p_, (tt, ii) in enumerate(CP):
                                for h in range(NH):
                                    v.scalar_tensor_tensor(
                                        out=scr[:, h * 128:(h + 1) * 128],
                                        in0=yk[:, tt * D + h * 128:tt * D + (h + 1) * 128],
                                        scalar=1.0,
                                        in1=yk[:, ii * D + h * 128:ii * D + (h + 1) * 128],
                                        op0=ALU.mult, op1=ALU.mult,
                                        accum_out=dots[:, p_ * 16 + h:p_ * 16 + h + 1]
                                        ) if not (p_ == len(CP) - 1 and h == NH - 1) else \
                                    v.scalar_tensor_tensor(
                                        out=scr[:, h * 128:(h + 1) * 128],
                                        in0=yk[:, tt * D + h * 128:tt * D + (h + 1) * 128],
                                        scalar=1.0,
                                        in1=yk[:, ii * D + h * 128:ii * D + (h + 1) * 128],
                                        op0=ALU.mult, op1=ALU.mult,
                                        accum_out=dots[:, p_ * 16 + h:p_ * 16 + h + 1]
                                        ).then_inc(sem["rcp"], 1)
                            v.wait_ge(sem["rcp"], b["rcp"] + 2)
                            v.tensor_scalar_mul(dots[:, 0:96], dots[:, 0:96], -1.0)
                            for p_, (tt, ii) in enumerate(APRS):
                                for h in range(NH):
                                    v.scalar_tensor_tensor(
                                        out=scr[:, h * 128:(h + 1) * 128],
                                        in0=yq[:, tt * D + h * 128:tt * D + (h + 1) * 128],
                                        scalar=1.0,
                                        in1=yk[:, ii * D + h * 128:ii * D + (h + 1) * 128],
                                        op0=ALU.mult, op1=ALU.mult,
                                        accum_out=dots[:, 96 + p_ * 16 + h:
                                                       96 + p_ * 16 + h + 1]
                                        ) if not (p_ == len(APRS) - 1 and h == NH - 1) else \
                                    v.scalar_tensor_tensor(
                                        out=scr[:, h * 128:(h + 1) * 128],
                                        in0=yq[:, tt * D + h * 128:tt * D + (h + 1) * 128],
                                        scalar=1.0,
                                        in1=yk[:, ii * D + h * 128:ii * D + (h + 1) * 128],
                                        op0=ALU.mult, op1=ALU.mult,
                                        accum_out=dots[:, 96 + p_ * 16 + h:
                                                       96 + p_ * 16 + h + 1]
                                        ).then_inc(sem["rcp"], 1)
                            v.wait_ge(sem["rcp"], b["rcp"] + 3)
                            v.wait_ge(sem["ac"], b["ac"] + 4)

                            def ccol(t_, i_):
                                return CP.index((t_, i_)) * 16

                            def acol(t_, i_):
                                return 96 + APRS.index((t_, i_)) * 16

                            for s in range(4):
                                if s == 0:
                                    for h in range(NH):
                                        v.tensor_scalar_mul(
                                            ubuf[:, h * 128:(h + 1) * 128],
                                            yv[:, h * 128:(h + 1) * 128],
                                            beta[:, h:h + 1])
                                else:
                                    v.tensor_copy(out=ubuf[:, s * D:(s + 1) * D],
                                                  in_=yv[:, s * D:(s + 1) * D])
                                    for i_ in range(s):
                                        pc = ccol(s, i_)
                                        for h in range(NH):
                                            v.scalar_tensor_tensor(
                                                out=ubuf[:, s * D + h * 128:
                                                         s * D + (h + 1) * 128],
                                                in0=ubuf[:, i_ * D + h * 128:
                                                         i_ * D + (h + 1) * 128],
                                                scalar=dots[:, pc + h:pc + h + 1],
                                                in1=ubuf[:, s * D + h * 128:
                                                         s * D + (h + 1) * 128],
                                                op0=ALU.mult, op1=ALU.add)
                                    for h in range(NH):
                                        v.tensor_scalar_mul(
                                            ubuf[:, s * D + h * 128:s * D + (h + 1) * 128],
                                            ubuf[:, s * D + h * 128:s * D + (h + 1) * 128],
                                            beta[:, s * 16 + h:s * 16 + h + 1])
                                for i_ in range(s + 1):
                                    pa = acol(s, i_)
                                    for h in range(NH):
                                        if i_ == 0:
                                            v.tensor_scalar_mul(
                                                obuf[:, s * D + h * 128:s * D + (h + 1) * 128],
                                                ubuf[:, h * 128:(h + 1) * 128],
                                                dots[:, pa + h:pa + h + 1])
                                        else:
                                            v.scalar_tensor_tensor(
                                                out=obuf[:, s * D + h * 128:
                                                         s * D + (h + 1) * 128],
                                                in0=ubuf[:, i_ * D + h * 128:
                                                         i_ * D + (h + 1) * 128],
                                                scalar=dots[:, pa + h:pa + h + 1],
                                                in1=obuf[:, s * D + h * 128:
                                                         s * D + (h + 1) * 128],
                                                op0=ALU.mult, op1=ALU.add)
                            ins = None
                            for s in range(4):
                                for h in range(NH):
                                    sl = obuf[:, s * D + h * 128:s * D + (h + 1) * 128]
                                    ins = v.scalar_tensor_tensor(
                                        out=scr[:, h * 128:(h + 1) * 128], in0=sl,
                                        scalar=1.0, in1=sl, op0=ALU.mult, op1=ALU.mult,
                                        accum_out=lsum[:, s * 16 + h:s * 16 + h + 1])
                            ins.then_inc(sem["mx"], 1)
                            v.wait_ge(sem["ac"], b["ac"] + 7)
                            v.reciprocal(lsum[:, 0:64],
                                         lsum[:, 0:64]).then_inc(sem["rcp"], 1)
                            v.wait_ge(sem["rcp"], b["rcp"] + 4)
                            for idx in range(64):
                                s, h = divmod(idx, 16)
                                if idx >= 2:
                                    v.wait_ge(sem["T"], b["T"] + idx - 1)
                                v.tensor_scalar_mul(
                                    onb[:, (idx % 2) * 128:(idx % 2 + 1) * 128],
                                    obuf[:, s * D + h * 128:s * D + (h + 1) * 128],
                                    lsum[:, s * 16 + h:s * 16 + h + 1]).then_inc(sem["dv"], 1)
                    cnt["pemm"] += 4
                    cnt["ac"] += 7 + 64
                    cnt["dv"] += 1 + 64
                    cnt["mx"] += 1
                    cnt["rcp"] += 4
                    cnt["T"] += 64

        b = dict(cnt)
        with ExitStack() as wctx:
            wobb = [sba(wctx, f"wobb{i}", [128, NCH * 128], BF16) for i in range(2)]
            xstg = [sba(wctx, f"xsb{i}", [128, TOK], F32) for i in range(2)]
            ostg = [sba(wctx, f"osb{i}", [128, 512], F32) for i in range(4)]
            psw = [wctx.enter_context(
                nc.psum_tensor(f"psWb{i}", [128, 512], F32)) for i in range(2)]
            with nc.Block() as blk:
                @blk.gpsimd
                def _(g):
                    for oc in range(NCH):
                        if oc >= 2:
                            g.wait_ge(sem["pemm"], b["pemm"] + 2 * (oc - 1))
                        g.dma_start(
                            out=wobb[oc % 2][:, :].rearrange("p (c o) -> p c o", c=NCH),
                            in_=h_wob.ap()[:, oc * 128:(oc + 1) * 128]
                            .rearrange("(c p) o -> p c o", p=128)).then_inc(sem["wd"], 16)

                @blk.vector
                def _(v):
                    def adds(idx):
                        oc, bk2 = divmod(idx, 2)
                        v.wait_ge(sem["pemm"], b["pemm"] + idx + 1)
                        v.wait_ge(sem["ld"], b["ld"] + 16 * (oc + 1))
                        if idx >= 4:
                            v.wait_ge(sem["st"], b["st"] + 16 * (idx - 3))
                        v.tensor_add(ostg[idx % 4][:, :], psw[idx % 2][:, :],
                                     xstg[oc % 2][:, bk2 * 512:(bk2 + 1) * 512]
                                     ).then_inc(sem["dv"], 1)

                    for oc in range(NCH):
                        v.wait_ge(sem["wd"], b["wd"] + 16 * (oc + 1))
                        v.tensor_scalar_mul(wobb[oc % 2][:, :], wobb[oc % 2][:, :],
                                            gcol[:, 0:1]).then_inc(sem["nrm"], 1)
                        if oc >= 1:
                            adds(2 * (oc - 1))
                            adds(2 * (oc - 1) + 1)
                    adds(30)
                    adds(31)

                @blk.sync
                def _(sy):
                    for oc in range(NCH):
                        if oc >= 2:
                            sy.wait_ge(sem["dv"], b["dv"] + 2 * (oc - 1))
                        sy.dma_start(out=xstg[oc % 2][:, :],
                                     in_=h_src.ap()[oc * 128:(oc + 1) * 128, :]
                                     ).then_inc(sem["ld"], 16)

                @blk.scalar
                def _(sy):
                    for idx in range(32):
                        oc, bk2 = divmod(idx, 2)
                        sy.wait_ge(sem["dv"], b["dv"] + idx + 1)
                        sy.dma_start(out=h_dst.ap()[oc * 128:(oc + 1) * 128,
                                                    bk2 * 512:(bk2 + 1) * 512],
                                     in_=ostg[idx % 4][:, :]).then_inc(sem["st"], 16)
                    sy.wait_ge(sem["st"], b["st"] + 16 * 32)

                @blk.tensor
                def _(t):
                    for oc in range(NCH):
                        t.wait_ge(sem["nrm"], b["nrm"] + oc + 1)
                        for bk2 in range(2):
                            idx = oc * 2 + bk2
                            if idx >= 2:
                                t.wait_ge(sem["dv"], b["dv"] + idx - 1)
                            ins = None
                            for ch in range(NCH):
                                ins = t.matmul(
                                    psw[idx % 2][:, :],
                                    wobb[oc % 2][:, ch * 128:(ch + 1) * 128],
                                    ofm[bk2][:, ch * 512:(ch + 1) * 512],
                                    start=(ch == 0), stop=(ch == NCH - 1))
                            ins.then_inc(sem["pemm"], 1)
            cnt["wd"] += 16 * NCH
            cnt["ld"] += 16 * NCH
            cnt["nrm"] += NCH
            cnt["pemm"] += 32
            cnt["dv"] += 32
            cnt["st"] += 16 * 32


def make_in_maps(inputs):
    x = np.asarray(inputs["L0_attn_a_hidden_states"], dtype=np.float32)
    srcA, srcB, cA, cB, pK = _rope_tables()
    WqT = np.ascontiguousarray(np.asarray(inputs["L0_attn_a_Wq"], np.float32).T)
    wqA = np.ascontiguousarray(WqT[:, srcA])
    wqB = np.ascontiguousarray(WqT[:, srcB])
    cAb = np.ascontiguousarray(np.broadcast_to(cA[None, :], (128, D)))
    cBb = np.ascontiguousarray(np.broadcast_to(cB[None, :], (128, D)))
    wkp = np.ascontiguousarray(np.asarray(inputs["L0_attn_a_Wk"], np.float32).T[:, pK])
    wvt = np.ascontiguousarray(np.asarray(inputs["L0_attn_a_Wv"], np.float32).T)
    woa = np.ascontiguousarray(np.asarray(inputs["L0_attn_a_Wo"], np.float32).T)
    wqb = np.ascontiguousarray(np.asarray(inputs["L0_attn_b_Wq"], np.float32))
    wkb = np.ascontiguousarray(np.asarray(inputs["L0_attn_b_Wk"], np.float32))
    wvb = np.ascontiguousarray(np.asarray(inputs["L0_attn_b_Wv"], np.float32))
    wbb = np.ascontiguousarray(np.asarray(inputs["L0_attn_b_Wb"], np.float32))
    wob = np.ascontiguousarray(np.asarray(inputs["L0_attn_b_Wo"], np.float32))
    convr = np.ascontiguousarray(np.concatenate(
        [np.asarray(inputs["L0_attn_b_conv_q"], np.float32).T.reshape(-1),
         np.asarray(inputs["L0_attn_b_conv_k"], np.float32).T.reshape(-1),
         np.asarray(inputs["L0_attn_b_conv_v"], np.float32).T.reshape(-1)]
    ).reshape(1, 3 * 4 * D))
    gcol_a = np.ascontiguousarray(
        np.asarray(inputs["L0_attn_b_g_norm"], np.float32).reshape(128, 1))
    w1t = np.ascontiguousarray(np.asarray(inputs["L0_ffn_W1"], np.float32).T)
    w3t = np.ascontiguousarray(np.asarray(inputs["L0_ffn_W3"], np.float32).T)
    w2t = np.ascontiguousarray(np.asarray(inputs["L0_ffn_W2"], np.float32).T)

    identity = np.eye(128, dtype=BF16NP)
    h128 = _hadamard128().astype(BF16NP)
    onesc = np.ones((128, 1), dtype=BF16NP)
    onesrf = np.ones((1, 128), dtype=np.float32)

    x_fm = np.ascontiguousarray(x.transpose(2, 0, 1))
    x_tm = np.ascontiguousarray(x.reshape(OB * TSEQ, D))

    kk = np.arange(1024)
    qq = np.arange(128)
    in_maps = []
    for c in range(NCORE):
        bA, bB = c, 15 - c
        mskTA = (kk[:1024][None, :] * 0).astype(np.float32)
        mskTA = np.zeros((128, 1024), dtype=np.float32)
        for kt in range(8):
            kl = np.arange(128)
            mskTA[:, kt * 128:(kt + 1) * 128] = (
                (kt * 128 + kl[:, None]) <= bA * 128 + qq[None, :]).T * 0
        mskTA = np.concatenate(
            [((kt * 128 + np.arange(128)[:, None]) <= bA * 128 + qq[None, :])
             .astype(np.float32) for kt in range(8)], axis=0)
        mskTA = mskTA.reshape(8, 128, 128).transpose(1, 0, 2).reshape(128, 1024)
        mskTB = np.concatenate(
            [((kt * 128 + np.arange(128)[:, None]) <= bB * 128 + qq[None, :])
             .astype(np.float32) for kt in range(16)], axis=0)
        mskTB = mskTB.reshape(16, 128, 128).transpose(1, 0, 2).reshape(128, 2048)
        xown = np.ascontiguousarray(np.concatenate(
            [x_fm[:, :, bA * 128:(bA + 1) * 128].reshape(D, 512),
             x_fm[:, :, bB * 128:(bB + 1) * 128].reshape(D, 512)], axis=1))
        in_maps.append({
            "xown": xown, "xpre": x_tm, "wqA": wqA, "wqB": wqB, "cAb": cAb,
            "cBb": cBb, "wkp": wkp, "wvt": wvt, "woa": woa, "wqb": wqb,
            "wkb": wkb, "wvb": wvb, "wbb": wbb, "wob": wob, "convr": convr,
            "gcol": gcol_a, "w1t": w1t, "w3t": w3t, "w2t": w2t,
            "ident": identity, "h128": h128, "onesc": onesc, "onesrf": onesrf,
            "onesrb": np.ones((1, 128), dtype=BF16NP),
            "mskTA": mskTA.astype(BF16NP), "mskTB": mskTB.astype(BF16NP),
        })

    return in_maps


def assemble(res_results, key="y"):
    out = np.empty((OB, TSEQ, D), dtype=np.float32)
    for c in range(NCORE):
        y = res_results[c][key]
        y4 = y.reshape(D, 2, OB, 128)
        for bi, bblk in enumerate((c, 15 - c)):
            out[:, bblk * 128:(bblk + 1) * 128, :] = y4[:, bi].transpose(1, 2, 0)
    return out


def kernel(**inputs):
    in_maps = make_in_maps(inputs)
    nc = build_program()
    res = run_bass_kernel_spmd(nc, in_maps, core_ids=list(range(NCORE)))
    return assemble(res.results, "y")


if __name__ == "__main__":
    build_program()
    print("build ok")


# version 10
# speedup vs baseline: 1.4474x; 1.0027x over previous
import numpy as np
import ml_dtypes
from contextlib import ExitStack

import concourse.bass as bass
import concourse.mybir as mybir
from concourse.bass_utils import run_bass_kernel_spmd

F32 = mybir.dt.float32
BF16 = mybir.dt.bfloat16
AF = mybir.ActivationFunctionType
ALU = mybir.AluOpType
AX = mybir.AxisListType
BF16NP = ml_dtypes.bfloat16

D = 2048
OB = 4
TSEQ = 2048
NCH = 16
TOK = 1024
NH, DV = 16, 128
HID, HB = 8192, 64
NCORE = 8
SCALE_A = float(1.0 / np.sqrt(128.0))
S8192 = float(1.0 / np.sqrt(8192.0))
EPS_LN = 1e-5
NEG = -1e9

SEMS = ["ld", "wd", "pe", "pemm", "T", "pv", "ev", "exp", "ptv", "oev",
        "mx", "nrm", "cp", "sq", "dv", "ac", "ac2", "ap", "st", "rcp", "fz", "pl"]


def _rope_tables():
    theta = 10000.0 ** (-np.arange(32) / 32.0)
    srcA = np.zeros(D, dtype=np.int64)
    srcB = np.zeros(D, dtype=np.int64)
    cA = np.zeros(D, dtype=np.float32)
    cB = np.zeros(D, dtype=np.float32)
    for h in range(NH):
        cos, sin = np.cos(h * theta), np.sin(h * theta)
        for j in range(128):
            o = h * 128 + j
            if j < 32:
                m = j; a = 2 * m; ca, cb = cos[m], -sin[m]
            elif j < 64:
                m = j - 32; a = 2 * m; ca, cb = sin[m], cos[m]
            elif j < 96:
                m = j - 64; a = 64 + 2 * m; ca, cb = cos[m], -sin[m]
            else:
                m = j - 96; a = 64 + 2 * m; ca, cb = sin[m], cos[m]
            srcA[o], srcB[o] = h * 128 + a, h * 128 + a + 1
            cA[o], cB[o] = ca, cb
    pK = np.zeros(128, dtype=np.int64)
    for j in range(128):
        if j < 32:
            pK[j] = 2 * j
        elif j < 64:
            pK[j] = 2 * (j - 32) + 1
        elif j < 96:
            pK[j] = 64 + 2 * (j - 64)
        else:
            pK[j] = 64 + 2 * (j - 96) + 1
    return srcA, srcB, cA, cB, pK


def _hadamard128():
    i = np.arange(128)
    pc = np.zeros((128, 128), dtype=np.int64)
    for b in range(7):
        pc += ((i[:, None] >> b) & 1) * ((i[None, :] >> b) & 1)
    return np.where(pc % 2 == 0, 1.0, -1.0).astype(np.float32)


def build_program(debug=False):
    nc = bass.Bass("TRN2", target_bir_lowering=False)
    dt = nc.dram_tensor
    dbg = dict(kind="ExternalOutput") if debug else {}

    h_xown = dt("xown", [D, TOK], F32, kind="ExternalInput")
    h_xpre = dt("xpre", [OB * TSEQ, D], F32, kind="ExternalInput")
    h_wqA = dt("wqA", [D, D], F32, kind="ExternalInput")
    h_wqB = dt("wqB", [D, D], F32, kind="ExternalInput")
    h_cA = dt("cAb", [128, D], F32, kind="ExternalInput")
    h_cB = dt("cBb", [128, D], F32, kind="ExternalInput")
    h_wk = dt("wkp", [D, DV], F32, kind="ExternalInput")
    h_wv = dt("wvt", [D, DV], F32, kind="ExternalInput")
    h_woa = dt("woa", [D, D], F32, kind="ExternalInput")
    h_wqb = dt("wqb", [D, D], F32, kind="ExternalInput")
    h_wkb = dt("wkb", [D, D], F32, kind="ExternalInput")
    h_wvb = dt("wvb", [D, D], F32, kind="ExternalInput")
    h_wbb = dt("wbb", [D, NH], F32, kind="ExternalInput")
    h_wob = dt("wob", [D, D], F32, kind="ExternalInput")
    h_conv = dt("convr", [1, 3 * 4 * D], F32, kind="ExternalInput")
    h_g = dt("gcol", [128, 1], F32, kind="ExternalInput")
    h_w1 = dt("w1t", [D, HID], F32, kind="ExternalInput")
    h_w3 = dt("w3t", [D, HID], F32, kind="ExternalInput")
    h_w2 = dt("w2t", [HID, D], F32, kind="ExternalInput")
    h_id = dt("ident", [128, 128], BF16, kind="ExternalInput")
    h_h128 = dt("h128", [128, 128], BF16, kind="ExternalInput")
    h_onesc = dt("onesc", [128, 1], BF16, kind="ExternalInput")
    h_onesrf = dt("onesrf", [1, 128], F32, kind="ExternalInput")
    h_onesrb = dt("onesrb", [1, 128], BF16, kind="ExternalInput")
    h_mskTA = dt("mskTA", [128, 1024], BF16, kind="ExternalInput")
    h_mskTB = dt("mskTB", [128, 2048], BF16, kind="ExternalInput")

    h_y = dt("y", [D, TOK], F32, kind="ExternalOutput")
    if debug:
        h_dxn = dt("dxn", [128, NCH * TOK], BF16, kind="ExternalOutput")
        h_dq = dt("dq", [128, NH * TOK], BF16, kind="ExternalOutput")
        h_dk = dt("dk", [128, OB * TSEQ], BF16, kind="ExternalOutput")
        h_dv = dt("dv", [128, OB * TSEQ], BF16, kind="ExternalOutput")
        h_dqo = dt("dqo", [128, NH * TOK], BF16, kind="ExternalOutput")
        h_dvt = dt("dvt", [128, 1024], BF16, kind="ExternalOutput")
        h_dsums = dt("dsums", [128, 16], F32, kind="ExternalOutput")
        h_dss = dt("dss", [128, 2 * 2048], F32, kind="ExternalOutput")
        h_dpb = dt("dpb", [128, 2 * 2048], BF16, kind="ExternalOutput")
        h_dpt = dt("dpt", [128, 2 * 2048], BF16, kind="ExternalOutput")
        h_dnm = dt("dnm", [128, 6], F32, kind="ExternalOutput")
    h_xs1 = dt("xs1", [D, TOK], F32, **dbg)
    h_xs2 = dt("xs2", [D, TOK], F32, **dbg)
    h_xs3 = dt("xs3", [D, TOK], F32, **dbg)

    cnt = {k: 0 for k in SEMS}

    with ExitStack() as ctx:
        def sba(stack, name, shape, d):
            return stack.enter_context(nc.sbuf_tensor(name, shape, d))

        sem = {k: ctx.enter_context(nc.semaphore("s_" + k)) for k in SEMS}

        xn = sba(ctx, "xn", [128, NCH * TOK], BF16)
        ident = sba(ctx, "ident_s", [128, 128], BF16)
        h128 = sba(ctx, "h128_s", [128, 128], BF16)
        onesc = sba(ctx, "onesc_s", [128, 1], BF16)
        onesrf = sba(ctx, "onesrf_s", [1, 128], F32)
        onesrb = sba(ctx, "onesrb_s", [1, 128], BF16)
        gcol = sba(ctx, "gcol_s", [128, 1], F32)
        wbb_sb = sba(ctx, "wbb_s", [128, NCH * NH], BF16)
        epsln = sba(ctx, "epsln", [128, 1], F32)
        eps12 = sba(ctx, "eps12", [128, 1], F32)
        eps12s = sba(ctx, "eps12s", [128, 1], F32)

        with nc.Block() as blk:
            @blk.sync
            def _(sy):
                loads = [(ident, h_id), (h128, h_h128), (onesc, h_onesc),
                         (onesrf, h_onesrf), (onesrb, h_onesrb), (gcol, h_g)]
                for dsb, hh in loads:
                    sy.dma_start(out=dsb[:, :], in_=hh.ap()).then_inc(sem["ld"], 16)
                sy.wait_ge(sem["ld"], cnt["ld"] + 16 * len(loads))

            @blk.vector
            def _(v):
                v.memset(epsln[:, :], EPS_LN)
                v.memset(eps12[:, :], 1e-12)
                v.memset(eps12s[:, :], 128.0 * 1e-12)

            @blk.gpsimd
            def _(g):
                g.dma_start(out=wbb_sb[:, :].rearrange("p (c w) -> p c w", c=NCH),
                            in_=h_wbb.ap().rearrange("(c p) w -> p c w", p=128)
                            ).then_inc(sem["wd"], 16)
                g.wait_ge(sem["wd"], cnt["wd"] + 16)
        cnt["ld"] += 16 * 6
        cnt["wd"] += 16

        def LN(tag, src):
            b = dict(cnt)
            with ExitStack() as lctx:
                xstg = [sba(lctx, f"xstg{tag}{i}", [128, TOK], F32) for i in range(2)]
                sqs = [sba(lctx, f"sqs{tag}{i}", [128, TOK], BF16) for i in range(2)]
                rows = sba(lctx, f"rows{tag}", [1, 6 * TOK], F32)
                ab_b = sba(lctx, f"abb{tag}", [1, 2 * TOK], BF16)
                a_bc = sba(lctx, f"abc{tag}", [128, TOK], F32)
                b_bc = sba(lctx, f"bbc{tag}", [128, TOK], F32)
                tmpf = sba(lctx, f"tmpf{tag}", [128, TOK], F32)
                ps_s0 = lctx.enter_context(nc.psum_tensor(f"pss0{tag}", [1, TOK], F32))
                ps_s1 = lctx.enter_context(nc.psum_tensor(f"pss1{tag}", [1, TOK], F32))
                ps_b0 = lctx.enter_context(nc.psum_tensor(f"psb0{tag}", [128, TOK], F32))
                ps_b1 = lctx.enter_context(nc.psum_tensor(f"psb1{tag}", [128, TOK], F32))

                def row(i):
                    return rows[0:1, i * TOK:(i + 1) * TOK]

                with nc.Block() as blk:
                    @blk.sync
                    def _(sy):
                        for ch in range(NCH):
                            if ch >= 2:
                                sy.wait_ge(sem["cp"], b["cp"] + ch - 1)
                                sy.wait_ge(sem["sq"], b["sq"] + ch - 1)
                            sy.dma_start(out=xstg[ch % 2][:, :],
                                         in_=src.ap()[ch * 128:(ch + 1) * 128, :]
                                         ).then_inc(sem["ld"], 16)
                        sy.wait_ge(sem["cp"], b["cp"] + 16)
                        sy.wait_ge(sem["sq"], b["sq"] + 16)
                        for ch in range(NCH):
                            if ch >= 2:
                                sy.wait_ge(sem["ap"], b["ap"] + ch - 1)
                            sy.dma_start(out=xstg[ch % 2][:, :],
                                         in_=src.ap()[ch * 128:(ch + 1) * 128, :]
                                         ).then_inc(sem["ld"], 16)

                    @blk.vector
                    def _(v):
                        for ch in range(NCH):
                            v.wait_ge(sem["ld"], b["ld"] + 16 * (ch + 1))
                            v.tensor_copy(out=xn[:, ch * TOK:(ch + 1) * TOK],
                                          in_=xstg[ch % 2][:, :]).then_inc(sem["cp"], 1)
                        v.wait_ge(sem["pe"], b["pe"] + 1)
                        v.wait_ge(sem["pemm"], b["pemm"] + 16)
                        v.tensor_scalar_mul(row(0), ps_s0[:, :], 1.0 / D)
                        v.tensor_scalar_mul(row(1), ps_s1[:, :], 1.0 / D)
                        v.tensor_mul(tmpf[0:1, :], row(0), row(0))
                        v.tensor_sub(row(2), row(1), tmpf[0:1, :]).then_inc(sem["dv"], 1)
                        v.wait_ge(sem["ac"], b["ac"] + 1)
                        v.reciprocal(row(4), row(3))
                        v.tensor_copy(out=ab_b[0:1, 0:TOK], in_=row(4))
                        v.tensor_mul(tmpf[0:1, :], row(0), row(4))
                        v.tensor_scalar_mul(ab_b[0:1, TOK:2 * TOK], tmpf[0:1, :],
                                            -1.0).then_inc(sem["dv"], 1)
                        for ch in range(NCH):
                            v.wait_ge(sem["ld"], b["ld"] + 16 * (16 + ch + 1))
                            if ch == 0:
                                v.wait_ge(sem["ac"], b["ac"] + 2)
                            v.tensor_mul(tmpf[:, :], xstg[ch % 2][:, :], a_bc[:, :])
                            v.tensor_add(xn[:, ch * TOK:(ch + 1) * TOK], tmpf[:, :],
                                         b_bc[:, :]).then_inc(sem["ap"], 1)

                    @blk.scalar
                    def _(s):
                        for ch in range(NCH):
                            s.wait_ge(sem["ld"], b["ld"] + 16 * (ch + 1))
                            if ch >= 2:
                                s.wait_ge(sem["pemm"], b["pemm"] + ch - 1)
                            s.activation(out=sqs[ch % 2][:, :], in_=xstg[ch % 2][:, :],
                                         func=AF.Square).then_inc(sem["sq"], 1)
                        s.wait_ge(sem["dv"], b["dv"] + 1)
                        s.activation(out=row(3), in_=row(2), func=AF.Sqrt,
                                     bias=epsln[0:1, 0:1]).then_inc(sem["ac"], 1)
                        s.wait_ge(sem["pe"], b["pe"] + 3)
                        s.activation(out=a_bc[:, :], in_=ps_b0[:, :], func=AF.Copy)
                        s.activation(out=b_bc[:, :], in_=ps_b1[:, :],
                                     func=AF.Copy).then_inc(sem["ac"], 1)

                    @blk.tensor
                    def _(t):
                        for ch in range(NCH):
                            t.wait_ge(sem["cp"], b["cp"] + ch + 1)
                            t.wait_ge(sem["sq"], b["sq"] + ch + 1)
                            ins = None
                            for hh in range(2):
                                ins = t.matmul(ps_s0[0:1, hh * 512:(hh + 1) * 512], onesc[:, :],
                                               xn[:, ch * TOK + hh * 512:ch * TOK + (hh + 1) * 512],
                                               start=(ch == 0), stop=(ch == NCH - 1))
                            if ch == NCH - 1:
                                ins.then_inc(sem["pe"], 1)
                            for hh in range(2):
                                ins = t.matmul(ps_s1[0:1, hh * 512:(hh + 1) * 512], onesc[:, :],
                                               sqs[ch % 2][:, hh * 512:(hh + 1) * 512],
                                               start=(ch == 0), stop=(ch == NCH - 1))
                            ins.then_inc(sem["pemm"], 1)
                        t.wait_ge(sem["dv"], b["dv"] + 2)
                        for hh in range(2):
                            ins = t.matmul(ps_b0[:, hh * 512:(hh + 1) * 512], onesrb[:, :],
                                           ab_b[0:1, hh * 512:(hh + 1) * 512],
                                           start=True, stop=True)
                        ins.then_inc(sem["pe"], 1)
                        for hh in range(2):
                            ins = t.matmul(ps_b1[:, hh * 512:(hh + 1) * 512], onesrb[:, :],
                                           ab_b[0:1, TOK + hh * 512:TOK + (hh + 1) * 512],
                                           start=True, stop=True)
                        ins.then_inc(sem["pe"], 1)

            cnt["ld"] += 16 * 32
            cnt["cp"] += 16
            cnt["sq"] += 16
            cnt["pemm"] += 16
            cnt["pe"] += 3
            cnt["ac"] += 2
            cnt["dv"] += 2
            cnt["ap"] += 16

        def FFN(tag, src, dst):
            with ExitStack() as fctx:
                hid = sba(fctx, f"hid{tag}", [128, HB * TOK], BF16)
                b = dict(cnt)
                with ExitStack() as actx:
                    w1b = [sba(actx, f"w1b{tag}{i}", [128, NCH * 128], BF16) for i in range(3)]
                    w3b = [sba(actx, f"w3b{tag}{i}", [128, NCH * 128], BF16) for i in range(3)]
                    gs = [sba(actx, f"gs{tag}{i}", [128, TOK], BF16) for i in range(2)]
                    tA = sba(actx, f"tA{tag}", [128, TOK], BF16)
                    tB = sba(actx, f"tB{tag}", [128, TOK], BF16)
                    tC = sba(actx, f"tC{tag}", [128, TOK], BF16)
                    tD = sba(actx, f"tD{tag}", [128, TOK], BF16)
                    ub = [sba(actx, f"ub{tag}{i}", [128, TOK], BF16) for i in range(2)]
                    psb = [actx.enter_context(
                        nc.psum_tensor(f"psA{tag}{i}", [128, 512], F32)) for i in range(8)]
                    with nc.Block() as blk:
                        @blk.gpsimd
                        def _(g):
                            for hb in range(HB):
                                if hb >= 3:
                                    g.wait_ge(sem["pemm"], b["pemm"] + 4 * (hb - 2))
                                for hsrc, wb in ((h_w1, w1b), (h_w3, w3b)):
                                    g.dma_start(
                                        out=wb[hb % 3][:, :].rearrange("p (c o) -> p c o", c=NCH),
                                        in_=hsrc.ap()[:, hb * 128:(hb + 1) * 128]
                                        .rearrange("(c p) o -> p c o", p=128)
                                    ).then_inc(sem["wd"], 16)

                        @blk.tensor
                        def _(t):
                            def h128mm(c):
                                t.wait_ge(sem["dv"], b["dv"] + c + 1)
                                t.wait_ge(sem["ac"], b["ac"] + c + 1)
                                bb2 = (c % 2) * 4
                                for tt in range(2):
                                    t.matmul(psb[bb2 + tt][:, :], h128[:, :],
                                             hid[:, c * TOK + tt * 512:c * TOK + (tt + 1) * 512],
                                             start=True, stop=True).then_inc(sem["pe"], 1)

                            for hb in range(HB):
                                t.wait_ge(sem["wd"], b["wd"] + 32 * (hb + 1))
                                if hb >= 2:
                                    t.wait_ge(sem["ac"], b["ac"] + hb - 1)
                                    t.wait_ge(sem["dv"], b["dv"] + hb - 1)
                                    t.wait_ge(sem["ac2"], b["ac2"] + 2 * (hb - 1))
                                bb = (hb % 2) * 4
                                for wsel, boff in ((w1b, 0), (w3b, 2)):
                                    for tt in range(2):
                                        ins = None
                                        for ch in range(NCH):
                                            ins = t.matmul(
                                                psb[bb + boff + tt][:, :],
                                                wsel[hb % 3][:, ch * 128:(ch + 1) * 128],
                                                xn[:, ch * TOK + tt * 512:ch * TOK + (tt + 1) * 512],
                                                start=(ch == 0), stop=(ch == NCH - 1))
                                        ins.then_inc(sem["pemm"], 1)
                                if hb >= 1:
                                    h128mm(hb - 1)
                            h128mm(HB - 1)

                        @blk.scalar
                        def _(s):
                            def h128ev(c):
                                bb2 = (c % 2) * 4
                                for tt in range(2):
                                    s.wait_ge(sem["pe"], b["pe"] + 2 * c + tt + 1)
                                    s.activation(out=hid[:, c * TOK + tt * 512:
                                                         c * TOK + (tt + 1) * 512],
                                                 in_=psb[bb2 + tt][:, :], func=AF.Copy,
                                                 scale=S8192).then_inc(sem["ac2"], 1)

                            for hb in range(HB):
                                bb = (hb % 2) * 4
                                if hb >= 2:
                                    s.wait_ge(sem["dv"], b["dv"] + hb - 1)
                                ins = None
                                for tt in range(2):
                                    s.wait_ge(sem["pemm"], b["pemm"] + 4 * hb + tt + 1)
                                    ins = s.activation(out=gs[hb % 2][:, tt * 512:(tt + 1) * 512],
                                                       in_=psb[bb + tt][:, :], func=AF.Silu)
                                ins.then_inc(sem["ac"], 1)
                                if hb >= 1:
                                    h128ev(hb - 1)
                            h128ev(HB - 1)

                        @blk.gpsimd
                        def _(gp):
                            def radix4p(c0, c1, c2, c3):
                                s0 = hid[:, c0 * TOK:(c0 + 1) * TOK]
                                s1 = hid[:, c1 * TOK:(c1 + 1) * TOK]
                                s2 = hid[:, c2 * TOK:(c2 + 1) * TOK]
                                s3 = hid[:, c3 * TOK:(c3 + 1) * TOK]
                                gp.tensor_tensor(out=tC[:, :], in0=s0, in1=s1, op=ALU.add)
                                gp.tensor_tensor(out=tD[:, :], in0=s0, in1=s1, op=ALU.subtract)
                                gp.tensor_tensor(out=s0, in0=s2, in1=s3, op=ALU.add)
                                gp.tensor_tensor(out=s1, in0=s2, in1=s3, op=ALU.subtract)
                                gp.tensor_tensor(out=s2, in0=tC[:, :], in1=s0, op=ALU.subtract)
                                gp.tensor_tensor(out=s3, in0=tD[:, :], in1=s1, op=ALU.subtract)
                                gp.tensor_tensor(out=s0, in0=tC[:, :], in1=s0, op=ALU.add)
                                gp.tensor_tensor(out=s1, in0=tD[:, :], in1=s1, op=ALU.add)

                            gp.wait_ge(sem["dv"], b["dv"] + HB + 1)
                            ins = None
                            for j in range(13, 16):
                                radix4p(j, j + 16, j + 32, j + 48)
                            gp.engine_nop().then_inc(sem["ac2"], 1)

                        @blk.vector
                        def _(v):
                            pend = []

                            def radix4(c0, c1, c2, c3):
                                s0 = hid[:, c0 * TOK:(c0 + 1) * TOK]
                                s1 = hid[:, c1 * TOK:(c1 + 1) * TOK]
                                s2 = hid[:, c2 * TOK:(c2 + 1) * TOK]
                                s3 = hid[:, c3 * TOK:(c3 + 1) * TOK]
                                v.tensor_add(tA[:, :], s0, s1)
                                v.tensor_sub(tB[:, :], s0, s1)
                                v.tensor_add(s0, s2, s3)
                                v.tensor_sub(s1, s2, s3)
                                v.tensor_sub(s2, tA[:, :], s0)
                                v.tensor_sub(s3, tB[:, :], s1)
                                v.tensor_add(s0, tA[:, :], s0)
                                v.tensor_add(s1, tB[:, :], s1)

                            for hb in range(HB):
                                bb = (hb % 2) * 4
                                for tt in range(2):
                                    v.wait_ge(sem["pemm"], b["pemm"] + 4 * hb + 2 + tt + 1)
                                    v.tensor_copy(out=ub[hb % 2][:, tt * 512:(tt + 1) * 512],
                                                  in_=psb[bb + 2 + tt][:, :])
                                v.wait_ge(sem["ac"], b["ac"] + hb + 1)
                                v.tensor_mul(hid[:, hb * TOK:(hb + 1) * TOK], gs[hb % 2][:, :],
                                             ub[hb % 2][:, :]).then_inc(sem["dv"], 1)
                                if hb % 4 == 3:
                                    qd = hb // 4
                                    v.wait_ge(sem["ac2"], b["ac2"] + 2 * (4 * qd + 4))
                                    radix4(4 * qd, 4 * qd + 1, 4 * qd + 2, 4 * qd + 3)
                                if hb % 16 == 15 and hb < 63:
                                    g2 = hb // 16
                                    pend.extend((16 * g2 + j, 16 * g2 + j + 4,
                                                 16 * g2 + j + 8, 16 * g2 + j + 12)
                                                for j in range(4))
                                elif hb == 63:
                                    for j in range(4):
                                        radix4(48 + j, 48 + j + 4, 48 + j + 8, 48 + j + 12)
                                if pend and hb % 4 != 3:
                                    radix4(*pend.pop(0))
                            for args in pend:
                                radix4(*args)
                            nc.vector.engine_nop().then_inc(sem["dv"], 1)
                            for j in range(13):
                                radix4(j, j + 16, j + 32, j + 48)
                    cnt["wd"] += 16 * 2 * HB
                    cnt["pemm"] += 4 * HB
                    cnt["pe"] += 2 * HB
                    cnt["ac"] += HB
                    cnt["ac2"] += 2 * HB + 1
                    cnt["dv"] += HB + 1

                b = dict(cnt)
                with ExitStack() as dctx:
                    w2b = [sba(dctx, f"w2b{tag}{i}", [128, 32 * 128], BF16) for i in range(3)]
                    xstg = [sba(dctx, f"xsd{tag}{i}", [128, TOK], F32) for i in range(2)]
                    ostg = [sba(dctx, f"osd{tag}{i}", [128, 512], F32) for i in range(6)]
                    pso = [dctx.enter_context(
                        nc.psum_tensor(f"psD{tag}{i}", [128, 512], F32)) for i in range(2)]
                    with nc.Block() as blk:
                        @blk.gpsimd
                        def _(g):
                            for half in range(32):
                                dc, hh = divmod(half, 2)
                                if half >= 3:
                                    g.wait_ge(sem["pemm"],
                                              b["pemm"] + 2 * ((half - 3) // 2) + 2)
                                g.dma_start(
                                    out=w2b[half % 3][:, :].rearrange("p (c o) -> p c o", c=32),
                                    in_=h_w2.ap()[hh * 4096:(hh + 1) * 4096,
                                                  dc * 128:(dc + 1) * 128]
                                    .rearrange("(c p) o -> p c o", p=128)).then_inc(sem["wd"], 16)

                        @blk.sync
                        def _(sy):
                            for dc in range(NCH):
                                if dc >= 2:
                                    sy.wait_ge(sem["dv"], b["dv"] + 2 * (dc - 1))
                                sy.dma_start(out=xstg[dc % 2][:, :],
                                             in_=src.ap()[dc * 128:(dc + 1) * 128, :]
                                             ).then_inc(sem["ld"], 16)

                        @blk.scalar
                        def _(sy):
                            for idx in range(32):
                                dc, tt = divmod(idx, 2)
                                sy.wait_ge(sem["dv"], b["dv"] + idx + 1)
                                sy.dma_start(out=dst.ap()[dc * 128:(dc + 1) * 128,
                                                          tt * 512:(tt + 1) * 512],
                                             in_=ostg[idx % 6][:, :]).then_inc(sem["st"], 16)
                            sy.wait_ge(sem["st"], b["st"] + 16 * 32)

                        @blk.tensor
                        def _(t):
                            for dc in range(NCH):
                                for tt in range(2):
                                    idx = dc * 2 + tt
                                    if idx >= 2:
                                        t.wait_ge(sem["dv"], b["dv"] + idx - 1)
                                    ins = None
                                    for hb in range(HB):
                                        half = dc * 2 + hb // 32
                                        if hb % 32 == 0 and tt == 0:
                                            t.wait_ge(sem["wd"], b["wd"] + 16 * (half + 1))
                                        ins = t.matmul(
                                            pso[idx % 2][:, :],
                                            w2b[half % 3][:, (hb % 32) * 128:(hb % 32 + 1) * 128],
                                            hid[:, hb * TOK + tt * 512:hb * TOK + (tt + 1) * 512],
                                            start=(hb == 0), stop=(hb == HB - 1))
                                    ins.then_inc(sem["pemm"], 1)

                        @blk.vector
                        def _(v):
                            for idx in range(32):
                                dc, tt = divmod(idx, 2)
                                v.wait_ge(sem["pemm"], b["pemm"] + idx + 1)
                                v.wait_ge(sem["ld"], b["ld"] + 16 * (dc + 1))
                                if idx >= 6:
                                    v.wait_ge(sem["st"], b["st"] + 16 * (idx - 5))
                                v.tensor_add(ostg[idx % 6][:, :], pso[idx % 2][:, :],
                                             xstg[dc % 2][:, tt * 512:(tt + 1) * 512]
                                             ).then_inc(sem["dv"], 1)
                    cnt["wd"] += 16 * 32
                    cnt["ld"] += 16 * NCH
                    cnt["pemm"] += 32
                    cnt["dv"] += 32
                    cnt["st"] += 16 * 32

        LN("L1", h_xown)

        with ExitStack() as s1ctx:
            qo = sba(s1ctx, "qo", [128, NH * TOK], BF16)

            b = dict(cnt)
            with ExitStack() as rctx:
                wqp = sba(rctx, "wqp", [128, NCH * D], BF16)
                cab = sba(rctx, "cab", [128, D], F32)
                cbb = sba(rctx, "cbb", [128, D], F32)
                wsA = [sba(rctx, f"wsA{i}", [128, D], F32) for i in range(2)]
                wsB = [sba(rctx, f"wsB{i}", [128, D], F32) for i in range(2)]
                tmpa = sba(rctx, "tmpa", [128, 1024], F32)
                tmpb = sba(rctx, "tmpb", [128, 4096], F32)
                with nc.Block() as blk:
                    @blk.sync
                    def _(sy):
                        sy.dma_start(out=cab[:, :], in_=h_cA.ap()).then_inc(sem["ld"], 16)
                        sy.dma_start(out=cbb[:, :], in_=h_cB.ap()).then_inc(sem["ld"], 16)
                        for ch in range(NCH):
                            if ch >= 2:
                                sy.wait_ge(sem["dv"], b["dv"] + ch - 1)
                                sy.wait_ge(sem["pl"], b["pl"] + 2 * (ch - 1))
                            sy.dma_start(out=wsA[ch % 2][:, :],
                                         in_=h_wqA.ap()[ch * 128:(ch + 1) * 128, :]
                                         ).then_inc(sem["ld"], 16)
                            sy.dma_start(out=wsB[ch % 2][:, :],
                                         in_=h_wqB.ap()[ch * 128:(ch + 1) * 128, :]
                                         ).then_inc(sem["ld"], 16)

                    @blk.gpsimd
                    def _(g):
                        for ch in range(NCH):
                            g.wait_ge(sem["ld"], b["ld"] + 32 + 32 * (ch + 1))
                            if ch >= 2:
                                g.wait_ge(sem["dv"], b["dv"] + ch - 1)
                            for hh in range(2):
                                sl = slice(hh * 1024, (hh + 1) * 1024)
                                g.tensor_tensor(out=tmpb[:, (ch % 2) * 2048 + hh * 1024:
                                                         (ch % 2) * 2048 + (hh + 1) * 1024],
                                                in0=wsB[ch % 2][:, sl], in1=cbb[:, sl],
                                                op=ALU.mult).then_inc(sem["pl"], 1)

                    @blk.vector
                    def _(v):
                        for ch in range(NCH):
                            v.wait_ge(sem["ld"], b["ld"] + 32 + 32 * (ch + 1))
                            for hh in range(2):
                                sl = slice(hh * 1024, (hh + 1) * 1024)
                                v.tensor_mul(tmpa[:, :], wsA[ch % 2][:, sl], cab[:, sl])
                                v.wait_ge(sem["pl"], b["pl"] + 2 * ch + hh + 1)
                                ins = v.tensor_add(wqp[:, ch * D + hh * 1024:
                                                       ch * D + (hh + 1) * 1024],
                                                   tmpa[:, :],
                                                   tmpb[:, (ch % 2) * 2048 + hh * 1024:
                                                        (ch % 2) * 2048 + (hh + 1) * 1024])
                            ins.then_inc(sem["dv"], 1)
                cnt["ld"] += 16 * 34
                cnt["dv"] += 16
                cnt["pl"] += 32

                b = dict(cnt)
                psq = [rctx.enter_context(
                    nc.psum_tensor(f"psQ{i}", [128, 512], F32)) for i in range(2)]
                with nc.Block() as blk:
                    @blk.tensor
                    def _(t):
                        t.wait_ge(sem["dv"], b["dv"])
                        for h in range(NH):
                            for tt in range(2):
                                gi = h * 2 + tt
                                if gi >= 2:
                                    t.wait_ge(sem["ac"], b["ac"] + gi - 1)
                                ins = None
                                for ch in range(NCH):
                                    ins = t.matmul(
                                        psq[gi % 2][:, :],
                                        wqp[:, ch * D + h * 128:ch * D + (h + 1) * 128],
                                        xn[:, ch * TOK + tt * 512:ch * TOK + (tt + 1) * 512],
                                        start=(ch == 0), stop=(ch == NCH - 1))
                                ins.then_inc(sem["pemm"], 1)

                    @blk.scalar
                    def _(s):
                        for gi in range(32):
                            h, tt = divmod(gi, 2)
                            s.wait_ge(sem["pemm"], b["pemm"] + gi + 1)
                            s.activation(out=qo[:, h * TOK + tt * 512:h * TOK + (tt + 1) * 512],
                                         in_=psq[gi % 2][:, :],
                                         func=AF.Copy).then_inc(sem["ac"], 1)
                cnt["pemm"] += 32
                cnt["ac"] += 32

            if debug:
                with nc.Block() as blk:
                    @blk.sync
                    def _(sy):
                        sy.dma_start(out=h_dxn.ap(), in_=xn[:, :]).then_inc(sem["st"], 16)
                        sy.dma_start(out=h_dq.ap(), in_=qo[:, :]).then_inc(sem["st"], 16)
                        sy.wait_ge(sem["st"], cnt["st"] + 32)
                cnt["st"] += 32

            with ExitStack() as actx:
                kfm = sba(actx, "kfm", [128, OB * TSEQ], BF16)
                VW = 132
                vtm = sba(actx, "vtm", [128, 64 * VW], BF16)
                mskTA = sba(actx, "mskTA_s", [128, 1024], BF16)
                mskTB = sba(actx, "mskTB_s", [128, 2048], BF16)
                wkp_sb = sba(actx, "wkp_s", [128, NCH * DV], BF16)
                wvt_sb = sba(actx, "wvt_s", [128, NCH * DV], BF16)
                b20 = sba(actx, "b20", [128, 1], F32)
                pstg = [sba(actx, f"pstg{i}", [128, D], F32) for i in range(2)]
                xnt = [sba(actx, f"xnt{i}", [128, D], BF16) for i in range(2)]
                xfmt = [sba(actx, f"xfmt{i}", [128, NCH * 128], BF16) for i in range(2)]
                sums = sba(actx, "sums", [128, 8], F32)
                dbgs = sba(actx, "dbgs", [128, 16], F32)
                junkb = sba(actx, "junkb", [128, D], BF16)
                ptsb = [sba(actx, f"ptsb{i}", [128, 2048], BF16) for i in range(2)]
                otm = [sba(actx, f"otm{i}", [128, 128], BF16) for i in range(2)]
                rinv = sba(actx, "rinv", [128, 2], F32)
                tpA = actx.enter_context(nc.psum_tensor("psTpA", [128, 128], BF16))
                tpB = actx.enter_context(nc.psum_tensor("psTpB", [128, 128], BF16))
                tp = [tpA, tpB]
                psk = actx.enter_context(nc.psum_tensor("psK", [128, 128], F32))
                psv = actx.enter_context(nc.psum_tensor("psV", [128, 128], F32))
                psS = [actx.enter_context(nc.psum_tensor(f"psS{i}", [128, 512], F32))
                       for i in range(2)]
                pso = [actx.enter_context(nc.psum_tensor(f"psO{i}", [128, 132], F32))
                       for i in range(2)]

                b = dict(cnt)
                with nc.Block() as blk:
                    @blk.sync
                    def _(sy):
                        sy.dma_start(out=mskTA[:, :], in_=h_mskTA.ap()).then_inc(sem["ld"], 16)
                        sy.dma_start(out=mskTB[:, :], in_=h_mskTB.ap()).then_inc(sem["ld"], 16)
                        sy.wait_ge(sem["ld"], b["ld"] + 32)

                    @blk.gpsimd
                    def _(g):
                        for dsb, hh in ((wkp_sb, h_wk), (wvt_sb, h_wv)):
                            g.dma_start(out=dsb[:, :].rearrange("p (c w) -> p c w", c=NCH),
                                        in_=hh.ap().rearrange("(c p) w -> p c w", p=128)
                                        ).then_inc(sem["wd"], 16)
                        g.wait_ge(sem["wd"], b["wd"] + 32)

                    @blk.vector
                    def _(v):
                        v.memset(vtm[:, :], 1.0)
                        v.memset(b20[:, :], -20.0)
                cnt["ld"] += 32
                cnt["wd"] += 32

                b = dict(cnt)
                NT = 64
                with nc.Block() as blk:
                    @blk.sync
                    def _(sy):
                        for i in range(NT):
                            ob, tb = divmod(i, 16)
                            if i >= 2:
                                sy.wait_ge(sem["ap"], b["ap"] + i - 1)
                            sy.dma_start(out=pstg[i % 2][:, :],
                                         in_=h_xpre.ap()[ob * TSEQ + tb * 128:
                                                         ob * TSEQ + tb * 128 + 128, :]
                                         ).then_inc(sem["ld"], 16)

                    @blk.vector
                    def _(v):
                        for i in range(NT):
                            p = i % 2
                            ob, tb = divmod(i, 16)
                            v.wait_ge(sem["ld"], b["ld"] + 16 * (i + 1))
                            v.tensor_scalar(out=junkb[:, :], in0=pstg[p][:, :],
                                            scalar1=1.0, scalar2=0.0, op0=ALU.mult,
                                            op1=ALU.add, accum_out=sums[:, 0:1])
                            v.scalar_tensor_tensor(out=junkb[:, :], in0=pstg[p][:, :],
                                                   scalar=1.0, in1=pstg[p][:, :],
                                                   op0=ALU.mult, op1=ALU.mult,
                                                   accum_out=sums[:, 1:2])
                            v.tensor_scalar_mul(sums[:, 2:3], sums[:, 0:1], 1.0 / D)
                            v.tensor_scalar_mul(sums[:, 4:5], sums[:, 1:2],
                                                1.0 / D).then_inc(sem["fz"], 1)
                            v.wait_ge(sem["fz"], b["fz"] + 2 * i + 1)
                            v.tensor_mul(sums[:, 3:4], sums[:, 2:3],
                                         sums[:, 2:3]).then_inc(sem["fz"], 1)
                            v.wait_ge(sem["fz"], b["fz"] + 2 * i + 2)
                            v.tensor_sub(sums[:, 5:6], sums[:, 4:5],
                                         sums[:, 3:4]).then_inc(sem["dv"], 1)
                            v.wait_ge(sem["ac"], b["ac"] + 17 * i + 1)
                            v.reciprocal(sums[:, 6:7],
                                         sums[:, 7:8]).then_inc(sem["rcp"], 1)
                            if i == 0:
                                v.tensor_copy(out=dbgs[:, 8:16], in_=sums[:, 0:8])
                            v.wait_ge(sem["rcp"], b["rcp"] + i + 1)
                            v.tensor_scalar(out=xnt[p][:, :], in0=pstg[p][:, :],
                                            scalar1=sums[:, 2:3], scalar2=sums[:, 6:7],
                                            op0=ALU.subtract, op1=ALU.mult
                                            ).then_inc(sem["ap"], 1)
                            v.wait_ge(sem["pemm"], b["pemm"] + 2 * i + 1)
                            v.tensor_copy(
                                out=kfm[:, ob * TSEQ + tb * 128:ob * TSEQ + (tb + 1) * 128],
                                in_=psk[:, :]).then_inc(sem["ac2"], 1)
                            v.wait_ge(sem["pemm"], b["pemm"] + 2 * i + 2)
                            v.tensor_copy(
                                out=vtm[:, (ob * 16 + tb) * VW:(ob * 16 + tb) * VW + 128],
                                in_=psv[:, :]).then_inc(sem["ac2"], 1)

                    @blk.scalar
                    def _(s):
                        for i in range(NT):
                            s.wait_ge(sem["dv"], b["dv"] + i + 1)
                            s.activation(out=sums[:, 7:8], in_=sums[:, 5:6],
                                         func=AF.Sqrt,
                                         bias=epsln[:, 0:1]).then_inc(sem["ac"], 1)
                            if i >= 2:
                                s.wait_ge(sem["pemm"], b["pemm"] + 2 * (i - 1))
                            for ch in range(NCH):
                                s.wait_ge(sem["T"], b["T"] + i * NCH + ch + 1)
                                s.activation(out=xfmt[i % 2][:, ch * 128:(ch + 1) * 128],
                                             in_=tp[(i * NCH + ch) % 2][:, :],
                                             func=AF.Copy).then_inc(sem["ac"], 1)

                    @blk.tensor
                    def _(t):
                        for i in range(NT):
                            t.wait_ge(sem["ap"], b["ap"] + i + 1)
                            for ch in range(NCH):
                                gT = i * NCH + ch
                                if gT >= 2:
                                    gp = gT - 2
                                    ip, chp = divmod(gp, NCH)
                                    t.wait_ge(sem["ac"], b["ac"] + 17 * ip + chp + 2)
                                t.transpose(tp[gT % 2][:, :],
                                            xnt[i % 2][:, ch * 128:(ch + 1) * 128],
                                            ident[:, :]).then_inc(sem["T"], 1)
                            t.wait_ge(sem["ac"], b["ac"] + 17 * i + 17)
                            if i >= 1:
                                t.wait_ge(sem["ac2"], b["ac2"] + 2 * (i - 1) + 1)
                            ins = None
                            for ch in range(NCH):
                                ins = t.matmul(psk[:, :], wkp_sb[:, ch * DV:(ch + 1) * DV],
                                               xfmt[i % 2][:, ch * 128:(ch + 1) * 128],
                                               start=(ch == 0), stop=(ch == NCH - 1))
                            ins.then_inc(sem["pemm"], 1)
                            if i >= 1:
                                t.wait_ge(sem["ac2"], b["ac2"] + 2 * (i - 1) + 2)
                            ins = None
                            for ch in range(NCH):
                                ins = t.matmul(psv[:, :],
                                               xfmt[i % 2][:, ch * 128:(ch + 1) * 128],
                                               wvt_sb[:, ch * DV:(ch + 1) * DV],
                                               start=(ch == 0), stop=(ch == NCH - 1))
                            ins.then_inc(sem["pemm"], 1)
                cnt["ld"] += 16 * NT
                cnt["dv"] += NT
                cnt["ap"] += NT
                cnt["ac"] += 17 * NT
                cnt["ac2"] += 2 * NT
                cnt["pemm"] += 2 * NT
                cnt["rcp"] += NT
                cnt["fz"] += 2 * NT
                cnt["T"] += 16 * NT

                b = dict(cnt)
                combos = [(ob, h, bk) for ob in range(4) for h in range(NH)
                          for bk in (0, 1)]
                nkt_l = [8, 16]

                def nk(i):
                    return nkt_l[combos[i][2]]

                def nsl(i):
                    return nk(i) // 4

                cumm = [0]
                cums = [0]
                for i in range(128):
                    cumm.append(cumm[-1] + nk(i))
                    cums.append(cums[-1] + nsl(i))

                with nc.Block() as blk:
                    @blk.tensor
                    def _(t):
                        for i in range(130):
                            if i < 128:
                                ob, h, bk = combos[i]
                                qcol = h * TOK + bk * 512 + ob * 128
                                for kt in range(nk(i)):
                                    sl = cums[i] + kt // 4
                                    if kt % 4 == 0 and sl >= 2:
                                        t.wait_ge(sem["ev"], b["ev"] + sl - 1)
                                    t.matmul(
                                        psS[sl % 2][:, (kt % 4) * 128:(kt % 4 + 1) * 128],
                                        kfm[:, ob * TSEQ + kt * 128:ob * TSEQ + (kt + 1) * 128],
                                        qo[:, qcol:qcol + 128],
                                        start=True, stop=True).then_inc(sem["pe"], 1)
                            if 1 <= i <= 128:
                                j = i - 1
                                ob, h, bk = combos[j]
                                t.wait_ge(sem["mx"], b["mx"] + j + 1)
                                if j >= 2:
                                    t.wait_ge(sem["oev"], b["oev"] + j - 1)
                                ins = None
                                for kt in range(nk(j)):
                                    ins = t.matmul(
                                        pso[j % 2][:, 0:129],
                                        ptsb[j % 2][:, kt * 128:(kt + 1) * 128],
                                        vtm[:, (ob * 16 + kt) * VW:(ob * 16 + kt) * VW + 129],
                                        start=(kt == 0), stop=(kt == nk(j) - 1))
                                ins.then_inc(sem["pv"], 1)
                            if 2 <= i <= 129:
                                j2 = i - 2
                                if j2 >= 2:
                                    t.wait_ge(sem["ptv"], b["ptv"] + j2 - 1)
                                t.wait_ge(sem["oev"], b["oev"] + j2 + 1)
                                t.transpose(tp[j2 % 2][:, :], otm[j2 % 2][:, :],
                                            ident[:, :]).then_inc(sem["T"], 1)

                    @blk.scalar
                    def _(s):
                        for i in range(130):
                            if i < 128:
                                if i >= 2:
                                    s.wait_ge(sem["pv"], b["pv"] + i - 1)
                                for sl in range(nsl(i)):
                                    w = 512
                                    s.wait_ge(sem["pe"], b["pe"] + cumm[i] + (sl + 1) * 4)
                                    s.activation(
                                        out=ptsb[i % 2][:, sl * 512:sl * 512 + w],
                                        in_=psS[(cums[i] + sl) % 2][:, 0:w], func=AF.Exp,
                                        scale=SCALE_A, bias=b20[:, 0:1]).then_inc(sem["ev"], 1)
                            if 1 <= i <= 128:
                                j = i - 1
                                if j >= 2:
                                    s.wait_ge(sem["T"], b["T"] + j - 1)
                                s.wait_ge(sem["rcp"], b["rcp"] + j + 1)
                                s.activation(out=otm[j % 2][:, :],
                                             in_=pso[j % 2][:, 0:128], func=AF.Copy,
                                             scale=rinv[:, j % 2:j % 2 + 1]
                                             ).then_inc(sem["oev"], 1)
                            if 2 <= i <= 129:
                                j2 = i - 2
                                ob, h, bk = combos[j2]
                                qcol = h * TOK + bk * 512 + ob * 128
                                s.wait_ge(sem["T"], b["T"] + j2 + 1)
                                s.activation(out=qo[:, qcol:qcol + 128],
                                             in_=tp[j2 % 2][:, :],
                                             func=AF.Copy).then_inc(sem["ptv"], 1)

                    @blk.vector
                    def _(v):
                        for i in range(129):
                            if i < 128:
                                bk = combos[i][2]
                                msk = mskTA if bk == 0 else mskTB
                                ins = None
                                for sl in range(nsl(i)):
                                    v.wait_ge(sem["ev"], b["ev"] + cums[i] + sl + 1)
                                    ins = v.tensor_mul(ptsb[i % 2][:, sl * 512:(sl + 1) * 512],
                                                       ptsb[i % 2][:, sl * 512:(sl + 1) * 512],
                                                       msk[:, sl * 512:(sl + 1) * 512])
                                ins.then_inc(sem["mx"], 1)
                            if i >= 1:
                                j = i - 1
                                v.wait_ge(sem["pv"], b["pv"] + j + 1)
                                if j >= 2:
                                    v.wait_ge(sem["oev"], b["oev"] + j - 1)
                                v.reciprocal(rinv[:, j % 2:j % 2 + 1],
                                             pso[j % 2][:, 128:129]
                                             ).then_inc(sem["rcp"], 1)
                cnt["pe"] += cumm[128]
                cnt["ev"] += cums[128]
                cnt["mx"] += 128
                cnt["pv"] += 128
                cnt["rcp"] += 128
                cnt["oev"] += 128
                cnt["ptv"] += 128
                cnt["T"] += 128

                if debug:
                    with nc.Block() as blk:
                        @blk.sync
                        def _(sy):
                            sy.dma_start(out=h_dvt.ap(), in_=vtm[:, 0:1024]).then_inc(sem["st"], 16)
                            sy.dma_start(out=h_dsums.ap(), in_=dbgs[:, :]).then_inc(sem["st"], 16)
                            sy.dma_start(out=h_dnm.ap()[:, 0:2], in_=rinv[:, :]).then_inc(sem["st"], 16)
                            sy.dma_start(out=h_dqo.ap(), in_=qo[:, :]).then_inc(sem["st"], 16)
                            sy.dma_start(out=h_dpt.ap()[:, 0:2048],
                                         in_=ptsb[0][:, :]).then_inc(sem["st"], 16)
                            sy.dma_start(out=h_dpt.ap()[:, 2048:4096],
                                         in_=ptsb[1][:, :]).then_inc(sem["st"], 16)
                            sy.wait_ge(sem["st"], cnt["st"] + 96)
                    cnt["st"] += 96

            b = dict(cnt)
            with ExitStack() as wctx:
                woab = [sba(wctx, f"woab{i}", [128, NCH * 128], BF16) for i in range(2)]
                xstg = [sba(wctx, f"xsw{i}", [128, TOK], F32) for i in range(2)]
                ostg = [sba(wctx, f"osw{i}", [128, 512], F32) for i in range(4)]
                psw = [wctx.enter_context(
                    nc.psum_tensor(f"psW{i}", [128, 512], F32)) for i in range(2)]
                with nc.Block() as blk:
                    @blk.gpsimd
                    def _(g):
                        for oc in range(NCH):
                            if oc >= 2:
                                g.wait_ge(sem["pemm"], b["pemm"] + 2 * (oc - 1))
                            g.dma_start(
                                out=woab[oc % 2][:, :].rearrange("p (c o) -> p c o", c=NCH),
                                in_=h_woa.ap()[:, oc * 128:(oc + 1) * 128]
                                .rearrange("(c p) o -> p c o", p=128)).then_inc(sem["wd"], 16)

                    @blk.sync
                    def _(sy):
                        for oc in range(NCH):
                            if oc >= 2:
                                sy.wait_ge(sem["dv"], b["dv"] + 2 * (oc - 1))
                            sy.dma_start(out=xstg[oc % 2][:, :],
                                         in_=h_xown.ap()[oc * 128:(oc + 1) * 128, :]
                                         ).then_inc(sem["ld"], 16)

                    @blk.scalar
                    def _(sy):
                        for idx in range(32):
                            oc, tt = divmod(idx, 2)
                            sy.wait_ge(sem["dv"], b["dv"] + idx + 1)
                            sy.dma_start(out=h_xs1.ap()[oc * 128:(oc + 1) * 128,
                                                        tt * 512:(tt + 1) * 512],
                                         in_=ostg[idx % 4][:, :]).then_inc(sem["st"], 16)
                        sy.wait_ge(sem["st"], b["st"] + 16 * 32)

                    @blk.tensor
                    def _(t):
                        t.wait_ge(sem["ptv"], b["ptv"])
                        for oc in range(NCH):
                            t.wait_ge(sem["wd"], b["wd"] + 16 * (oc + 1))
                            for tt in range(2):
                                idx = oc * 2 + tt
                                if idx >= 2:
                                    t.wait_ge(sem["dv"], b["dv"] + idx - 1)
                                ins = None
                                for ch in range(NCH):
                                    ins = t.matmul(
                                        psw[idx % 2][:, :],
                                        woab[oc % 2][:, ch * 128:(ch + 1) * 128],
                                        qo[:, ch * TOK + tt * 512:ch * TOK + (tt + 1) * 512],
                                        start=(ch == 0), stop=(ch == NCH - 1))
                                ins.then_inc(sem["pemm"], 1)

                    @blk.vector
                    def _(v):
                        for idx in range(32):
                            oc, tt = divmod(idx, 2)
                            v.wait_ge(sem["pemm"], b["pemm"] + idx + 1)
                            v.wait_ge(sem["ld"], b["ld"] + 16 * (oc + 1))
                            if idx >= 4:
                                v.wait_ge(sem["st"], b["st"] + 16 * (idx - 3))
                            v.tensor_add(ostg[idx % 4][:, :], psw[idx % 2][:, :],
                                         xstg[oc % 2][:, tt * 512:(tt + 1) * 512]
                                         ).then_inc(sem["dv"], 1)
                cnt["wd"] += 16 * NCH
                cnt["ld"] += 16 * NCH
                cnt["pemm"] += 32
                cnt["dv"] += 32
                cnt["st"] += 16 * 32

        LN("L2", h_xs1)
        FFN("f1", h_xs1, h_xs2)

        LN("L3", h_xs2)
        _attn_b(nc, sem, cnt, sba, xn, h_conv, wbb_sb, gcol, onesrb, ident,
                h_wqb, h_wkb, h_wvb, h_wob, h_xs2, h_xs3, epsln, eps12, eps12s)

        LN("L4", h_xs3)
        FFN("f2", h_xs3, h_y)

    return nc


def _attn_b(nc, sem, cnt, sba, xn, h_conv, wbb_sb, gcol, onesrb, ident,
            h_wqb, h_wkb, h_wvb, h_wob, h_src, h_dst, epsln, eps12, eps12s):
    from contextlib import ExitStack
    TERMS = {0: [(0, 3)], 1: [(0, 2), (1, 3)], 2: [(0, 1), (1, 2), (2, 3)],
             3: [(0, 0), (1, 1), (2, 2), (3, 3)]}
    CP = [(1, 0), (2, 0), (2, 1), (3, 0), (3, 1), (3, 2)]
    APRS = [(t, i) for t in range(4) for i in range(t + 1)]
    with ExitStack() as bctx:
        ofm = [sba(bctx, f"ofm{b_}", [128, NH * 512], BF16) for b_ in range(2)]
        for bk in range(2):
            with ExitStack() as dctx:
                yq = sba(dctx, f"yq{bk}", [128, 4 * D], BF16)
                yk = sba(dctx, f"yk{bk}", [128, 4 * D], BF16)
                yv = sba(dctx, f"yv{bk}", [128, 4 * D], BF16)

                with ExitStack() as pctx:
                    raw = sba(pctx, f"raw{bk}", [128, 4 * D], BF16)
                    wsl = [sba(pctx, f"wsl{bk}{i}", [128, NCH * 512], BF16)
                           for i in range(2)]
                    wtap = sba(pctx, f"wtap{bk}", [128, 4 * D], BF16)
                    ctmp = sba(pctx, f"ctmp{bk}", [128, D], BF16)
                    crow = [sba(pctx, f"crow{bk}{i}", [1, D], BF16) for i in range(2)]
                    psp = [pctx.enter_context(
                        nc.psum_tensor(f"psP{bk}{i}", [128, 512], F32)) for i in range(4)]
                    psc = pctx.enter_context(nc.psum_tensor(f"psC{bk}", [128, 512], F32))
                    PJ = ((h_wqb, yq), (h_wkb, yk), (h_wvb, yv))
                    b = dict(cnt)
                    with nc.Block() as blk:
                        @blk.gpsimd
                        def _(g):
                            for pi in range(3):
                                hw = PJ[pi][0]
                                for ct in range(4):
                                    g_ = 4 * pi + ct
                                    if g_ >= 2:
                                        g.wait_ge(sem["pemm"], b["pemm"] + 4 * (g_ - 1))
                                    g.dma_start(
                                        out=wsl[g_ % 2][:, :]
                                        .rearrange("p (c o) -> p c o", c=NCH),
                                        in_=hw.ap()[:, ct * 512:(ct + 1) * 512]
                                        .rearrange("(c p) o -> p c o", p=128)
                                        ).then_inc(sem["wd"], 16)
                                for tap in range(4):
                                    gt = 4 * pi + tap
                                    if gt >= 2:
                                        g.wait_ge(sem["pe"], b["pe"] + 4 * (gt - 1))
                                    g.dma_start(
                                        out=crow[gt % 2][:, :],
                                        in_=h_conv.ap()[0:1, (pi * 4 + tap) * D:
                                                        (pi * 4 + tap + 1) * D]
                                        ).then_inc(sem["wd"], 16)

                        @blk.tensor
                        def _(t):
                            for pi in range(3):
                                for ct in range(4):
                                    t.wait_ge(sem["wd"], b["wd"] + 16 * (8 * pi + ct + 1))
                                    for s in range(4):
                                        gg = 16 * pi + 4 * ct + s
                                        if gg >= 4:
                                            gp_, kp = divmod(gg - 4, 16)
                                            t.wait_ge(sem["ac"], b["ac"] + 36 * gp_ + kp + 1)
                                        ins = None
                                        for ch in range(NCH):
                                            ins = t.matmul(
                                                psp[gg % 4][:, :],
                                                xn[:, ch * TOK + bk * 512 + s * 128:
                                                   ch * TOK + bk * 512 + (s + 1) * 128],
                                                wsl[(4 * pi + ct) % 2][:, ch * 512:(ch + 1) * 512],
                                                start=(ch == 0), stop=(ch == NCH - 1))
                                        ins.then_inc(sem["pemm"], 1)
                                for tap in range(4):
                                    t.wait_ge(sem["wd"], b["wd"] + 16 * (8 * pi + 4 + tap + 1))
                                    for sg in range(4):
                                        gm = 16 * pi + tap * 4 + sg
                                        if gm >= 1:
                                            gp_, kp = divmod(gm - 1, 16)
                                            t.wait_ge(sem["ac"], b["ac"] + 36 * gp_ + 16 + kp + 1)
                                        t.matmul(psc[:, :], onesrb[:, :],
                                                 crow[(4 * pi + tap) % 2][0:1, sg * 512:(sg + 1) * 512],
                                                 start=True, stop=True).then_inc(sem["pe"], 1)

                        @blk.scalar
                        def _(s_):
                            for pi in range(3):
                                if pi >= 1:
                                    s_.wait_ge(sem["dv"], b["dv"] + pi)
                                for k in range(16):
                                    ct, s = divmod(k, 4)
                                    s_.wait_ge(sem["pemm"], b["pemm"] + 16 * pi + k + 1)
                                    s_.activation(
                                        out=raw[:, s * D + ct * 512:s * D + (ct + 1) * 512],
                                        in_=psp[(16 * pi + k) % 4][:, :],
                                        func=AF.Copy).then_inc(sem["ac"], 1)
                                for k in range(16):
                                    tap, sg = divmod(k, 4)
                                    s_.wait_ge(sem["pe"], b["pe"] + 16 * pi + k + 1)
                                    s_.activation(
                                        out=wtap[:, tap * D + sg * 512:tap * D + (sg + 1) * 512],
                                        in_=psc[:, :], func=AF.Copy).then_inc(sem["ac"], 1)
                                s_.wait_ge(sem["dv"], b["dv"] + pi + 1)
                                ybuf = PJ[pi][1]
                                for s in range(4):
                                    s_.activation(out=ybuf[:, s * D:(s + 1) * D],
                                                  in_=ybuf[:, s * D:(s + 1) * D],
                                                  func=AF.Silu).then_inc(sem["ac"], 1)

                        @blk.vector
                        def _(v):
                            for pi in range(3):
                                ybuf = PJ[pi][1]
                                v.wait_ge(sem["ac"], b["ac"] + 36 * pi + 32)
                                ins = None
                                for s in range(4):
                                    terms = TERMS[s]
                                    (s0, tp0) = terms[0]
                                    ysl = ybuf[:, s * D:(s + 1) * D]
                                    ins = v.tensor_mul(ysl, raw[:, s0 * D:(s0 + 1) * D],
                                                       wtap[:, tp0 * D:(tp0 + 1) * D])
                                    for (sj, tpj) in terms[1:]:
                                        v.tensor_mul(ctmp[:, :], raw[:, sj * D:(sj + 1) * D],
                                                     wtap[:, tpj * D:(tpj + 1) * D])
                                        ins = v.tensor_add(ysl, ysl, ctmp[:, :])
                                ins.then_inc(sem["dv"], 1)
                    cnt["wd"] += 16 * 24
                    cnt["pemm"] += 48
                    cnt["pe"] += 48
                    cnt["ac"] += 108
                    cnt["dv"] += 3

                b = dict(cnt)
                with ExitStack() as rctx:
                    ubuf = sba(rctx, f"ubuf{bk}", [128, 4 * D], F32)
                    obuf = sba(rctx, f"obuf{bk}", [128, 4 * D], F32)
                    beta = sba(rctx, f"beta{bk}", [128, 4 * NH], F32)
                    dots = sba(rctx, f"dots{bk}", [128, 16 * NH], F32)
                    lsum = sba(rctx, f"lsum{bk}", [128, 8 * NH], F32)
                    scr = sba(rctx, f"scr{bk}", [128, D], BF16)
                    onb = sba(rctx, f"onb{bk}", [128, 2 * 128], BF16)
                    psb_ = rctx.enter_context(nc.psum_tensor(f"psBt{bk}", [128, NH], F32))
                    tpp = [rctx.enter_context(
                        nc.psum_tensor(f"psTo{bk}{i}", [128, 128], BF16)) for i in range(2)]
                    with nc.Block() as blk:
                        @blk.tensor
                        def _(t):
                            for s in range(4):
                                if s >= 1:
                                    t.wait_ge(sem["ac"], b["ac"] + s)
                                ins = None
                                for ch in range(NCH):
                                    ins = t.matmul(
                                        psb_[:, :],
                                        xn[:, ch * TOK + bk * 512 + s * 128:
                                           ch * TOK + bk * 512 + (s + 1) * 128],
                                        wbb_sb[:, ch * NH:(ch + 1) * NH],
                                        start=(ch == 0), stop=(ch == NCH - 1))
                                ins.then_inc(sem["pemm"], 1)
                            for idx in range(64):
                                t.wait_ge(sem["dv"], b["dv"] + 2 + idx)
                                if idx >= 2:
                                    t.wait_ge(sem["ac"], b["ac"] + 6 + idx)
                                t.transpose(tpp[idx % 2][:, :],
                                            onb[:, (idx % 2) * 128:(idx % 2 + 1) * 128],
                                            ident[:, :]).then_inc(sem["T"], 1)

                        @blk.scalar
                        def _(s_):
                            for s in range(4):
                                s_.wait_ge(sem["pemm"], b["pemm"] + s + 1)
                                s_.activation(out=beta[:, s * NH:(s + 1) * NH],
                                              in_=psb_[:, :],
                                              func=AF.Sigmoid).then_inc(sem["ac"], 1)
                            s_.wait_ge(sem["dv"], b["dv"] + 1)
                            s_.activation(out=lsum[:, 0:64], in_=lsum[:, 0:64],
                                          func=AF.Sqrt, scale=128.0,
                                          bias=eps12s[:, 0:1]).then_inc(sem["ac"], 1)
                            s_.activation(out=lsum[:, 64:128], in_=lsum[:, 64:128],
                                          func=AF.Sqrt,
                                          bias=eps12[:, 0:1]).then_inc(sem["ac"], 1)
                            s_.wait_ge(sem["mx"], b["mx"] + 1)
                            s_.activation(out=lsum[:, 0:64], in_=lsum[:, 0:64],
                                          func=AF.Sqrt, scale=1.0 / 128.0,
                                          bias=epsln[:, 0:1]).then_inc(sem["ac"], 1)
                            for idx in range(64):
                                s, h = divmod(idx, 16)
                                s_.wait_ge(sem["T"], b["T"] + idx + 1)
                                s_.activation(
                                    out=ofm[bk][:, h * 512 + s * 128:h * 512 + (s + 1) * 128],
                                    in_=tpp[idx % 2][:, :],
                                    func=AF.Copy).then_inc(sem["ac"], 1)

                        @blk.vector
                        def _(v):
                            ins = None
                            for (buf, col0) in ((yq, 0), (yk, 64)):
                                for s in range(4):
                                    for h in range(NH):
                                        sl = buf[:, s * D + h * 128:s * D + (h + 1) * 128]
                                        ins = v.scalar_tensor_tensor(
                                            out=scr[:, h * 128:(h + 1) * 128], in0=sl,
                                            scalar=1.0, in1=sl, op0=ALU.mult, op1=ALU.mult,
                                            accum_out=lsum[:, col0 + s * 16 + h:
                                                           col0 + s * 16 + h + 1])
                            ins.then_inc(sem["dv"], 1)
                            v.wait_ge(sem["ac"], b["ac"] + 6)
                            v.reciprocal(lsum[:, 0:64], lsum[:, 0:64])
                            v.reciprocal(lsum[:, 64:128],
                                         lsum[:, 64:128]).then_inc(sem["rcp"], 1)
                            v.wait_ge(sem["rcp"], b["rcp"] + 1)
                            for (buf, col0) in ((yq, 0), (yk, 64)):
                                for s in range(4):
                                    for h in range(NH):
                                        sl = buf[:, s * D + h * 128:s * D + (h + 1) * 128]
                                        v.tensor_scalar_mul(
                                            sl, sl,
                                            lsum[:, col0 + s * 16 + h:col0 + s * 16 + h + 1])
                            for p_, (tt, ii) in enumerate(CP):
                                for h in range(NH):
                                    v.scalar_tensor_tensor(
                                        out=scr[:, h * 128:(h + 1) * 128],
                                        in0=yk[:, tt * D + h * 128:tt * D + (h + 1) * 128],
                                        scalar=1.0,
                                        in1=yk[:, ii * D + h * 128:ii * D + (h + 1) * 128],
                                        op0=ALU.mult, op1=ALU.mult,
                                        accum_out=dots[:, p_ * 16 + h:p_ * 16 + h + 1]
                                        ) if not (p_ == len(CP) - 1 and h == NH - 1) else \
                                    v.scalar_tensor_tensor(
                                        out=scr[:, h * 128:(h + 1) * 128],
                                        in0=yk[:, tt * D + h * 128:tt * D + (h + 1) * 128],
                                        scalar=1.0,
                                        in1=yk[:, ii * D + h * 128:ii * D + (h + 1) * 128],
                                        op0=ALU.mult, op1=ALU.mult,
                                        accum_out=dots[:, p_ * 16 + h:p_ * 16 + h + 1]
                                        ).then_inc(sem["rcp"], 1)
                            v.wait_ge(sem["rcp"], b["rcp"] + 2)
                            v.tensor_scalar_mul(dots[:, 0:96], dots[:, 0:96], -1.0)
                            for p_, (tt, ii) in enumerate(APRS):
                                for h in range(NH):
                                    v.scalar_tensor_tensor(
                                        out=scr[:, h * 128:(h + 1) * 128],
                                        in0=yq[:, tt * D + h * 128:tt * D + (h + 1) * 128],
                                        scalar=1.0,
                                        in1=yk[:, ii * D + h * 128:ii * D + (h + 1) * 128],
                                        op0=ALU.mult, op1=ALU.mult,
                                        accum_out=dots[:, 96 + p_ * 16 + h:
                                                       96 + p_ * 16 + h + 1]
                                        ) if not (p_ == len(APRS) - 1 and h == NH - 1) else \
                                    v.scalar_tensor_tensor(
                                        out=scr[:, h * 128:(h + 1) * 128],
                                        in0=yq[:, tt * D + h * 128:tt * D + (h + 1) * 128],
                                        scalar=1.0,
                                        in1=yk[:, ii * D + h * 128:ii * D + (h + 1) * 128],
                                        op0=ALU.mult, op1=ALU.mult,
                                        accum_out=dots[:, 96 + p_ * 16 + h:
                                                       96 + p_ * 16 + h + 1]
                                        ).then_inc(sem["rcp"], 1)
                            v.wait_ge(sem["rcp"], b["rcp"] + 3)
                            v.wait_ge(sem["ac"], b["ac"] + 4)

                            def ccol(t_, i_):
                                return CP.index((t_, i_)) * 16

                            def acol(t_, i_):
                                return 96 + APRS.index((t_, i_)) * 16

                            for s in range(4):
                                if s == 0:
                                    for h in range(NH):
                                        v.tensor_scalar_mul(
                                            ubuf[:, h * 128:(h + 1) * 128],
                                            yv[:, h * 128:(h + 1) * 128],
                                            beta[:, h:h + 1])
                                else:
                                    v.tensor_copy(out=ubuf[:, s * D:(s + 1) * D],
                                                  in_=yv[:, s * D:(s + 1) * D])
                                    for i_ in range(s):
                                        pc = ccol(s, i_)
                                        for h in range(NH):
                                            v.scalar_tensor_tensor(
                                                out=ubuf[:, s * D + h * 128:
                                                         s * D + (h + 1) * 128],
                                                in0=ubuf[:, i_ * D + h * 128:
                                                         i_ * D + (h + 1) * 128],
                                                scalar=dots[:, pc + h:pc + h + 1],
                                                in1=ubuf[:, s * D + h * 128:
                                                         s * D + (h + 1) * 128],
                                                op0=ALU.mult, op1=ALU.add)
                                    for h in range(NH):
                                        v.tensor_scalar_mul(
                                            ubuf[:, s * D + h * 128:s * D + (h + 1) * 128],
                                            ubuf[:, s * D + h * 128:s * D + (h + 1) * 128],
                                            beta[:, s * 16 + h:s * 16 + h + 1])
                                for i_ in range(s + 1):
                                    pa = acol(s, i_)
                                    for h in range(NH):
                                        if i_ == 0:
                                            v.tensor_scalar_mul(
                                                obuf[:, s * D + h * 128:s * D + (h + 1) * 128],
                                                ubuf[:, h * 128:(h + 1) * 128],
                                                dots[:, pa + h:pa + h + 1])
                                        else:
                                            v.scalar_tensor_tensor(
                                                out=obuf[:, s * D + h * 128:
                                                         s * D + (h + 1) * 128],
                                                in0=ubuf[:, i_ * D + h * 128:
                                                         i_ * D + (h + 1) * 128],
                                                scalar=dots[:, pa + h:pa + h + 1],
                                                in1=obuf[:, s * D + h * 128:
                                                         s * D + (h + 1) * 128],
                                                op0=ALU.mult, op1=ALU.add)
                            ins = None
                            for s in range(4):
                                for h in range(NH):
                                    sl = obuf[:, s * D + h * 128:s * D + (h + 1) * 128]
                                    ins = v.scalar_tensor_tensor(
                                        out=scr[:, h * 128:(h + 1) * 128], in0=sl,
                                        scalar=1.0, in1=sl, op0=ALU.mult, op1=ALU.mult,
                                        accum_out=lsum[:, s * 16 + h:s * 16 + h + 1])
                            ins.then_inc(sem["mx"], 1)
                            v.wait_ge(sem["ac"], b["ac"] + 7)
                            v.reciprocal(lsum[:, 0:64],
                                         lsum[:, 0:64]).then_inc(sem["rcp"], 1)
                            v.wait_ge(sem["rcp"], b["rcp"] + 4)
                            for idx in range(64):
                                s, h = divmod(idx, 16)
                                if idx >= 2:
                                    v.wait_ge(sem["T"], b["T"] + idx - 1)
                                v.tensor_scalar_mul(
                                    onb[:, (idx % 2) * 128:(idx % 2 + 1) * 128],
                                    obuf[:, s * D + h * 128:s * D + (h + 1) * 128],
                                    lsum[:, s * 16 + h:s * 16 + h + 1]).then_inc(sem["dv"], 1)
                    cnt["pemm"] += 4
                    cnt["ac"] += 7 + 64
                    cnt["dv"] += 1 + 64
                    cnt["mx"] += 1
                    cnt["rcp"] += 4
                    cnt["T"] += 64

        b = dict(cnt)
        with ExitStack() as wctx:
            wobb = [sba(wctx, f"wobb{i}", [128, NCH * 128], BF16) for i in range(2)]
            xstg = [sba(wctx, f"xsb{i}", [128, TOK], F32) for i in range(2)]
            ostg = [sba(wctx, f"osb{i}", [128, 512], F32) for i in range(4)]
            psw = [wctx.enter_context(
                nc.psum_tensor(f"psWb{i}", [128, 512], F32)) for i in range(2)]
            with nc.Block() as blk:
                @blk.gpsimd
                def _(g):
                    for oc in range(NCH):
                        if oc >= 2:
                            g.wait_ge(sem["pemm"], b["pemm"] + 2 * (oc - 1))
                        g.dma_start(
                            out=wobb[oc % 2][:, :].rearrange("p (c o) -> p c o", c=NCH),
                            in_=h_wob.ap()[:, oc * 128:(oc + 1) * 128]
                            .rearrange("(c p) o -> p c o", p=128)).then_inc(sem["wd"], 16)

                @blk.vector
                def _(v):
                    def adds(idx):
                        oc, bk2 = divmod(idx, 2)
                        v.wait_ge(sem["pemm"], b["pemm"] + idx + 1)
                        v.wait_ge(sem["ld"], b["ld"] + 16 * (oc + 1))
                        if idx >= 4:
                            v.wait_ge(sem["st"], b["st"] + 16 * (idx - 3))
                        v.tensor_add(ostg[idx % 4][:, :], psw[idx % 2][:, :],
                                     xstg[oc % 2][:, bk2 * 512:(bk2 + 1) * 512]
                                     ).then_inc(sem["dv"], 1)

                    for oc in range(NCH):
                        v.wait_ge(sem["wd"], b["wd"] + 16 * (oc + 1))
                        v.tensor_scalar_mul(wobb[oc % 2][:, :], wobb[oc % 2][:, :],
                                            gcol[:, 0:1]).then_inc(sem["nrm"], 1)
                        if oc >= 1:
                            adds(2 * (oc - 1))
                            adds(2 * (oc - 1) + 1)
                    adds(30)
                    adds(31)

                @blk.sync
                def _(sy):
                    for oc in range(NCH):
                        if oc >= 2:
                            sy.wait_ge(sem["dv"], b["dv"] + 2 * (oc - 1))
                        sy.dma_start(out=xstg[oc % 2][:, :],
                                     in_=h_src.ap()[oc * 128:(oc + 1) * 128, :]
                                     ).then_inc(sem["ld"], 16)

                @blk.scalar
                def _(sy):
                    for idx in range(32):
                        oc, bk2 = divmod(idx, 2)
                        sy.wait_ge(sem["dv"], b["dv"] + idx + 1)
                        sy.dma_start(out=h_dst.ap()[oc * 128:(oc + 1) * 128,
                                                    bk2 * 512:(bk2 + 1) * 512],
                                     in_=ostg[idx % 4][:, :]).then_inc(sem["st"], 16)
                    sy.wait_ge(sem["st"], b["st"] + 16 * 32)

                @blk.tensor
                def _(t):
                    for oc in range(NCH):
                        t.wait_ge(sem["nrm"], b["nrm"] + oc + 1)
                        for bk2 in range(2):
                            idx = oc * 2 + bk2
                            if idx >= 2:
                                t.wait_ge(sem["dv"], b["dv"] + idx - 1)
                            ins = None
                            for ch in range(NCH):
                                ins = t.matmul(
                                    psw[idx % 2][:, :],
                                    wobb[oc % 2][:, ch * 128:(ch + 1) * 128],
                                    ofm[bk2][:, ch * 512:(ch + 1) * 512],
                                    start=(ch == 0), stop=(ch == NCH - 1))
                            ins.then_inc(sem["pemm"], 1)
            cnt["wd"] += 16 * NCH
            cnt["ld"] += 16 * NCH
            cnt["nrm"] += NCH
            cnt["pemm"] += 32
            cnt["dv"] += 32
            cnt["st"] += 16 * 32


def make_in_maps(inputs):
    x = np.asarray(inputs["L0_attn_a_hidden_states"], dtype=np.float32)
    srcA, srcB, cA, cB, pK = _rope_tables()
    WqT = np.ascontiguousarray(np.asarray(inputs["L0_attn_a_Wq"], np.float32).T)
    wqA = np.ascontiguousarray(WqT[:, srcA])
    wqB = np.ascontiguousarray(WqT[:, srcB])
    cAb = np.ascontiguousarray(np.broadcast_to(cA[None, :], (128, D)))
    cBb = np.ascontiguousarray(np.broadcast_to(cB[None, :], (128, D)))
    wkp = np.ascontiguousarray(np.asarray(inputs["L0_attn_a_Wk"], np.float32).T[:, pK])
    wvt = np.ascontiguousarray(np.asarray(inputs["L0_attn_a_Wv"], np.float32).T)
    woa = np.ascontiguousarray(np.asarray(inputs["L0_attn_a_Wo"], np.float32).T)
    wqb = np.ascontiguousarray(np.asarray(inputs["L0_attn_b_Wq"], np.float32))
    wkb = np.ascontiguousarray(np.asarray(inputs["L0_attn_b_Wk"], np.float32))
    wvb = np.ascontiguousarray(np.asarray(inputs["L0_attn_b_Wv"], np.float32))
    wbb = np.ascontiguousarray(np.asarray(inputs["L0_attn_b_Wb"], np.float32))
    wob = np.ascontiguousarray(np.asarray(inputs["L0_attn_b_Wo"], np.float32))
    convr = np.ascontiguousarray(np.concatenate(
        [np.asarray(inputs["L0_attn_b_conv_q"], np.float32).T.reshape(-1),
         np.asarray(inputs["L0_attn_b_conv_k"], np.float32).T.reshape(-1),
         np.asarray(inputs["L0_attn_b_conv_v"], np.float32).T.reshape(-1)]
    ).reshape(1, 3 * 4 * D))
    gcol_a = np.ascontiguousarray(
        np.asarray(inputs["L0_attn_b_g_norm"], np.float32).reshape(128, 1))
    w1t = np.ascontiguousarray(np.asarray(inputs["L0_ffn_W1"], np.float32).T)
    w3t = np.ascontiguousarray(np.asarray(inputs["L0_ffn_W3"], np.float32).T)
    w2t = np.ascontiguousarray(np.asarray(inputs["L0_ffn_W2"], np.float32).T)

    identity = np.eye(128, dtype=BF16NP)
    h128 = _hadamard128().astype(BF16NP)
    onesc = np.ones((128, 1), dtype=BF16NP)
    onesrf = np.ones((1, 128), dtype=np.float32)

    x_fm = np.ascontiguousarray(x.transpose(2, 0, 1))
    x_tm = np.ascontiguousarray(x.reshape(OB * TSEQ, D))

    kk = np.arange(1024)
    qq = np.arange(128)
    in_maps = []
    for c in range(NCORE):
        bA, bB = c, 15 - c
        mskTA = (kk[:1024][None, :] * 0).astype(np.float32)
        mskTA = np.zeros((128, 1024), dtype=np.float32)
        for kt in range(8):
            kl = np.arange(128)
            mskTA[:, kt * 128:(kt + 1) * 128] = (
                (kt * 128 + kl[:, None]) <= bA * 128 + qq[None, :]).T * 0
        mskTA = np.concatenate(
            [((kt * 128 + np.arange(128)[:, None]) <= bA * 128 + qq[None, :])
             .astype(np.float32) for kt in range(8)], axis=0)
        mskTA = mskTA.reshape(8, 128, 128).transpose(1, 0, 2).reshape(128, 1024)
        mskTB = np.concatenate(
            [((kt * 128 + np.arange(128)[:, None]) <= bB * 128 + qq[None, :])
             .astype(np.float32) for kt in range(16)], axis=0)
        mskTB = mskTB.reshape(16, 128, 128).transpose(1, 0, 2).reshape(128, 2048)
        xown = np.ascontiguousarray(np.concatenate(
            [x_fm[:, :, bA * 128:(bA + 1) * 128].reshape(D, 512),
             x_fm[:, :, bB * 128:(bB + 1) * 128].reshape(D, 512)], axis=1))
        in_maps.append({
            "xown": xown, "xpre": x_tm, "wqA": wqA, "wqB": wqB, "cAb": cAb,
            "cBb": cBb, "wkp": wkp, "wvt": wvt, "woa": woa, "wqb": wqb,
            "wkb": wkb, "wvb": wvb, "wbb": wbb, "wob": wob, "convr": convr,
            "gcol": gcol_a, "w1t": w1t, "w3t": w3t, "w2t": w2t,
            "ident": identity, "h128": h128, "onesc": onesc, "onesrf": onesrf,
            "onesrb": np.ones((1, 128), dtype=BF16NP),
            "mskTA": mskTA.astype(BF16NP), "mskTB": mskTB.astype(BF16NP),
        })

    return in_maps


def assemble(res_results, key="y"):
    out = np.empty((OB, TSEQ, D), dtype=np.float32)
    for c in range(NCORE):
        y = res_results[c][key]
        y4 = y.reshape(D, 2, OB, 128)
        for bi, bblk in enumerate((c, 15 - c)):
            out[:, bblk * 128:(bblk + 1) * 128, :] = y4[:, bi].transpose(1, 2, 0)
    return out


def kernel(**inputs):
    in_maps = make_in_maps(inputs)
    nc = build_program()
    res = run_bass_kernel_spmd(nc, in_maps, core_ids=list(range(NCORE)))
    return assemble(res.results, "y")


if __name__ == "__main__":
    build_program()
    print("build ok")
